# version 01
# baseline (speedup 1.0000x reference)
import numpy as np
import ml_dtypes
import concourse.bass as bass
import concourse.mybir as mybir
from concourse.bass_utils import run_bass_kernel_spmd

F32 = mybir.dt.float32
BF16 = mybir.dt.bfloat16
AF = mybir.ActivationFunctionType
ALU = mybir.AluOpType
SP = mybir.EngineType.SP
AX = mybir.AxisListType.X

H = 4096; DIN = 8192; NST = 16; KCV = 4; R = 256
B = 2; L = 4096; T = 1024; DSL = 2048; E2 = 2 * DIN
NE = 8; INT = 8192; LAT = 2048
EPS_LN = 1e-5; EPS_RMS = 1e-6
NTC = 8; LTC = 32; NKH = 32; NKD = 16
RG = [[0, 1, 2, 3], [4, 5, 6, 7]]
ts = bass.ts

EV_INP, EV_GATE, EV_FC1, EV_EXP, EV_UP, EV_END = 0, 256, 264, 296, 328, 456


def build_nc(debug_phase=99, race_detect=True):
    nc = bass.Bass("TRN2", target_bir_lowering=False, detect_race_conditions=race_detect)

    def I(name, shape):
        return nc.dram_tensor(name, shape, F32, kind="ExternalInput").ap()
    x_tok = I("x_tok", [T, H]); w_in = I("w_in", [H, E2])
    convw = I("convw", [DSL, KCV]); convb = I("convb", [128, NKD])
    xprojw = I("xprojw", [R + 2 * NST, DSL]); dtprojw = I("dtprojw", [DSL, R])
    dtbias = I("dtbias", [128, NKD]); alog = I("alog", [DSL, NST]); dpar = I("dpar", [128, NKD])
    outw = I("outw", [H, DIN])
    dtlnw = I("dtlnw", [1, R]); blnw = I("blnw", [1, NST]); clnw = I("clnw", [1, NST])
    gatew = I("gatew", [H, NE]); upw = I("upw", [H, INT]); downw = I("downw", [INT, H])
    fc1w = I("fc1w", [H, LAT]); fc2w = I("fc2w", [LAT, H]); expw = I("expw", [LAT, LAT])
    lngamma = I("lngamma", [1, H]); lnbeta = I("lnbeta", [1, H])
    identf = I("identf", [128, 128])

    out_tok = nc.dram_tensor("out_tok", [T, H], F32, kind="ExternalOutput").ap()
    dbg_out = None
    if debug_phase < 99:
        shp = {1: ([DIN, T], BF16), 2: ([DSL, L], BF16), 3: ([4 * DSL, T], BF16),
               4: ([T, H], F32), 10: ([128, NKH * T], BF16), 11: ([DIN, T], BF16),
               21: ([DSL, L], BF16), 22: ([L, R + 2 * NST], F32)}[debug_phase]
        dbg_out = nc.dram_tensor("dbg_out", shp[0], shp[1], kind="ExternalOutput").ap()

    D = nc.dram_tensor
    ccA_in = D("ccA_in", [DIN, T], BF16, kind="Internal").ap()
    ccA_out = D("ccA_out", [4 * DIN, T], BF16, kind="Internal").ap()
    ccB_in = D("ccB_in", [4 * DSL, T], BF16, kind="Internal").ap()
    ccB_out = D("ccB_out", [16 * DSL, T], BF16, kind="Internal").ap()
    ccC_in = D("ccC_in", [L, R + 2 * NST], F32, kind="Internal").ap()
    ccC_out = D("ccC_out", [L, R + 2 * NST], F32, kind="Internal").ap()
    zbufd = D("zbufd", [DIN, T], BF16, kind="Internal").ap()
    hactd = D("hactd", [INT, T], BF16, kind="Internal").ap()
    moed = D("moed", [LAT, T], BF16, kind="Internal").ap()
    routd = D("routd", [H, T], BF16, kind="Internal").ap()
    dtspd = D("dtspd", [DSL, L], BF16, kind="Internal").ap()
    xcspd = D("xcspd", [DSL, L], BF16, kind="Internal").ap()
    ffnTd = D("ffnTd", [T, H], BF16, kind="Internal").ap()
    xplusd = D("xplusd", [T, H], F32, kind="Internal").ap()
    browd = D("browd", [1, NST * L], BF16, kind="Internal").ap()
    crowd = D("crowd", [1, NST * L], BF16, kind="Internal").ap()
    srowd = D("srowd", [1, T], BF16, kind="Internal").ap()

    ps = nc.alloc_psum_tensor([128, 8, 512], F32)

    AL = nc.alloc_sbuf_tensor
    s_identf = AL([128, 128], F32)
    s_A = AL([128, NKD, NST], F32)
    s_convw = AL([128, NKD, KCV], F32)
    s_convb = AL([128, NKD], F32)
    s_dtbias = AL([128, NKD], F32)
    s_dpar = AL([128, NKD], F32)
    s_hstate = AL([128, NKD * NST], F32)
    s_gamr = AL([128, R + 2 * NST], F32)
    s_srow = AL([1, T], BF16)
    gsb = AL([128, NKH, NE], BF16)
    base0 = nc.sbuf_base

    xnf = AL([128, NKH, T], BF16)
    wslab = AL([128, 2, NKH, 512], BF16)
    ovl0 = nc.sbuf_base
    xt = AL([128, 2, H], F32)
    xnstage = AL([128, 2, H], F32)
    lncols = AL([128, 16], F32)
    p0_end = nc.sbuf_base
    nc.sbuf_base = ovl0
    evst = AL([128, 2, 4, 1024], BF16)
    xlat = AL([128, 16, T], BF16)
    rlst = AL([128, 2, 512], BF16)
    s_rep = AL([128, T], BF16)
    logit = AL([128, NE], F32)
    lgm = AL([128, NE], F32)
    l_m = AL([128, 4], F32)
    scols = AL([128, NTC], F32)
    p1_end = nc.sbuf_base

    nc.sbuf_base = base0
    xdb = AL([128, NKD, L], BF16)
    cacc = AL([128, L], F32)
    cvst = AL([128, L], BF16)
    wxa = AL([128, DSL], F32)
    wxb = AL([128, DSL], F32)
    wxc = AL([32, DSL], F32)
    wxT = AL([128, NKD, R + 2 * NST], BF16)
    ssmst = AL([128, 8, R + 2 * NST], F32)
    p2a_end = nc.sbuf_base

    nc.sbuf_base = base0
    ssm2 = AL([128, LTC, R + 2 * NST], F32)
    sqst = AL([128, R + 2 * NST], F32)
    rmcols = AL([128, 8], F32)
    dtr32 = AL([128, 4, R], F32)
    bnt = AL([128, 4, NST], F32)
    cnt = AL([128, 4, NST], F32)
    btst = AL([16, LTC, 128], BF16)
    ctst = AL([16, LTC, 128], BF16)
    dtrT = AL([128, 2, L], BF16)
    wdtnat = AL([128, NKD, R], F32)
    wdtT = AL([128, 2, DSL], BF16)
    moesb = AL([128, 16, T], BF16)
    fcsl = AL([128, 2, 16, 128], BF16)
    fc2st = AL([128, 2, 512], BF16)
    spb = AL([128, 2, DSL], F32)
    spa = AL([128, 2, DSL], F32)
    dtst = AL([128, 2, DSL], BF16)
    p2b_end = nc.sbuf_base

    nc.sbuf_base = base0
    brep = AL([128, NST, T], BF16)
    crep = AL([128, NST, T], BF16)
    dtq = AL([128, 2, T], BF16)
    xcq = AL([128, 2, T], BF16)
    dtxq = AL([128, T], BF16)
    abuf = AL([128, 2, T], BF16)
    bbuf = AL([128, T], BF16)
    hbuf = AL([128, T], BF16)
    hcb = AL([128, 2, T], BF16)
    yac = AL([128, T], F32)
    y2t = AL([128, T], F32)
    hacth = AL([128, 64, 512], BF16)
    dwsl = AL([128, 2, 64, 128], BF16)
    routt = AL([128, 4, 512], BF16)
    f4st = AL([128, 2, 512], F32)
    ftst = AL([128, 2, 4, 128], BF16)
    p34_end = nc.sbuf_base

    nc.sbuf_base = base0
    gbr = AL([128, 2, H], F32)
    fft = AL([128, 2, H], BF16)
    xrt = AL([128, 2, H], F32)
    xpt = AL([128, 2, H], F32)
    sqxt = AL([128, H], F32)
    xpcols = AL([128, 8], F32)
    p4b_end = nc.sbuf_base

    nc.sbuf_base = base0
    ybuf = AL([128, 64, T], BF16)
    wosl = AL([128, 2, 2048], F32)
    wline = AL([128, 64, 128], BF16)
    zkt = AL([128, 2, T], BF16)
    zst = AL([128, 2, T], BF16)
    xpcs = AL([128, 2, 8, 128], F32)
    ost = AL([128, 2, 8, 128], F32)
    a5st = AL([128, 1024], F32)
    p5_end = nc.sbuf_base

    LIMIT = 229344
    for nm, e in [("p0", p0_end), ("p1", p1_end), ("p2a", p2a_end), ("p2b", p2b_end),
                  ("p34", p34_end), ("p4b", p4b_end), ("p5", p5_end)]:
        assert e <= LIMIT, f"SBUF overflow {nm}: {e} > {LIMIT}"
    nc.sbuf_base = max(p0_end, p1_end, p2a_end, p2b_end, p34_end, p4b_end, p5_end)

    names = """q_misc q_x c_sq0 c_lnr c_ln c_tr0 c_xnf q_w0 q_w1 q_gw c_mm1 c_ev1 q_exp
    c_scol c_str c_srow q_srep c_relu q_cca q_xd c_tap c_silu c_cvb
    q_wx c_trwx c_wxc c_trwd c_wdtc c_mm2 c_ev2 q_ssm q_ccc q_ssml c_sq2 c_rms
    c_bct c_bcc c_dtt c_dtrc q_brow q_moe2 q_fcs c_mmf c_evf q_rout c_mm3 c_sp0 c_spa c_sp
    q_dtsp q_xcsp q_bc q_sc c_dtx c_a c_b c_h c_hc c_y c_fin q_yexp q_ccb
    q_hact q_dwn q_routt c_mm4 c_ev4 c_tr4 c_ev4b q_ffnT
    q_xp_in c_sqx c_xp q_xp_out q_y q_z c_zs c_gate q_wo0 q_wo1 c_tr5 c_ev5 c_mm5
    c_ev5b c_tr5b q_xpc c_asm q_out
    c_veps c_rstd c_red3 c_vep3 c_rst3 c_vepx c_rstx ch_v ch_s ch_p""".split()
    S = {n: nc.alloc_semaphore(n) for n in names}
    g = lambda n: S[n]

    LD = {}

    class DQ:
        def __init__(self, eng, semname):
            self.eng, self.sem, self.n = eng, S[semname], 0

        def dma(self, name, out, in_, **kw):
            if self.n >= 1:
                self.eng.wait_ge(self.sem, 16 * self.n)
            self.n += 1
            r = self.eng.dma_start(out=out, in_=in_, **kw).then_inc(self.sem, 16)
            if name is not None:
                LD[name] = (self.sem, 16 * self.n)
            return r

        def flush(self):
            if self.n >= 1:
                self.eng.wait_ge(self.sem, 16 * self.n)

    def wld(eng, name):
        sem, cnt = LD[name]
        eng.wait_ge(sem, cnt)

    class Chain:
        def __init__(self, semname):
            self.sem, self.n, self.last = S[semname], 0, None

        def bump(self, ins):
            ins.then_inc(self.sem, 1)
            self.n += 1
            self.last = (self.sem, self.n)
            return ins

        def mark(self, semname, val):
            self.last = (S[semname], val)

        def join(self, eng):
            if self.last is not None:
                eng.wait_ge(*self.last)

    chv, chs, chp = Chain("ch_v"), Chain("ch_s"), Chain("ch_p")

    dwn_sched = [(th, hc) for th in range(2) for hc in range(32)]

    with nc.Block() as blk:

        @blk.sync
        def _(sy):
            pid = nc.partition_id([SP])
            myg = pid % 4
            dq_misc = DQ(sy, "q_misc"); dq_x = DQ(sy, "q_x"); dq_exp = DQ(sy, "q_exp")
            dq_srep = DQ(sy, "q_srep"); dq_xd = DQ(sy, "q_xd"); dq_ssm = DQ(sy, "q_ssm")
            dq_xcsp = DQ(sy, "q_xcsp"); dq_ssml = DQ(sy, "q_ssml"); dq_moe2 = DQ(sy, "q_moe2")
            dq_brow = DQ(sy, "q_brow"); dq_rout = DQ(sy, "q_rout"); dq_dtsp = DQ(sy, "q_dtsp")
            dq_bc = DQ(sy, "q_bc"); dq_sc = DQ(sy, "q_sc"); dq_hact = DQ(sy, "q_hact")
            dq_routt = DQ(sy, "q_routt"); dq_ffnT = DQ(sy, "q_ffnT"); dq_xpin = DQ(sy, "q_xp_in")
            dq_xpout = DQ(sy, "q_xp_out"); dq_y = DQ(sy, "q_y"); dq_z = DQ(sy, "q_z")
            dq_wo = [DQ(sy, "q_wo0"), DQ(sy, "q_wo1")]
            dq_xpc = DQ(sy, "q_xpc"); dq_out = DQ(sy, "q_out")

            dq_misc.dma(None, out=s_identf[:], in_=identf)
            dq_misc.dma(None, out=s_convw[:, :, :], in_=convw.rearrange("(k p) j -> p k j", p=128))
            dq_misc.dma(None, out=s_convb[:, :], in_=convb)
            dq_misc.dma(None, out=s_dtbias[:, :], in_=dtbias)
            dq_misc.dma(None, out=s_dpar[:, :], in_=dpar)
            dq_misc.dma(None, out=s_A[:, :, :], in_=alog.rearrange("(k p) n -> p k n", p=128))
            dq_misc.dma(None, out=s_gamr[:, 0:R], in_=dtlnw.partition_broadcast(128))
            dq_misc.dma(None, out=s_gamr[:, R:R + NST], in_=blnw.partition_broadcast(128))
            dq_misc.dma("misc", out=s_gamr[:, R + NST:], in_=clnw.partition_broadcast(128))
            for tc in range(NTC):
                if tc >= 2:
                    sy.wait_ge(g("c_ln"), tc - 1)
                dq_x.dma(f"x{tc}", out=xt[:, tc % 2, :], in_=x_tok[ts(tc, 128), :])
            if debug_phase == 10:
                sy.wait_ge(g("c_xnf"), 256)
                dq_out.dma(None, out=dbg_out, in_=xnf[:, :, :].rearrange("p a b -> p (a b)"))
                dq_out.flush()
                return
            for og in range(32):
                sy.wait_ge(g("c_ev1"), EV_INP + 8 * (og + 1))
                dst, ro = (ccA_in, og * 512) if og < 16 else (zbufd, (og - 16) * 512)
                dq_exp.dma(f"exp{og}", out=dst[ro:ro + 512, :].rearrange("(j p) t -> p j t", p=128),
                           in_=evst[:, og % 2, :, :])
            if debug_phase == 11:
                dq_exp.flush()
                dq_out.dma(None, out=dbg_out, in_=zbufd)
                dq_out.flush()
                return
            sy.wait_ge(g("c_srow"), NTC)
            dq_srep.dma(None, out=srowd, in_=s_srow[:, :])
            dq_srep.dma("srep", out=s_rep[:, :], in_=srowd.partition_broadcast(128))
            for og in range(4):
                sy.wait_ge(g("c_ev1"), EV_EXP + 8 * (og + 1))
                dq_exp.dma(f"expm{og}", out=moed[og * 512:(og + 1) * 512, :].rearrange("(j p) t -> p j t", p=128),
                           in_=evst[:, og % 2, :, :])
            for og in range(16):
                sy.wait_ge(g("c_ev1"), EV_UP + 8 * (og + 1))
                dq_exp.dma(f"expu{og}", out=hactd[og * 512:(og + 1) * 512, :].rearrange("(j p) t -> p j t", p=128),
                           in_=evst[:, og % 2, :, :])
            if debug_phase == 1:
                dq_exp.flush()
                sy.wait_ge(g("q_cca"), 1)
                dq_out.dma(None, out=dbg_out, in_=zbufd)
                dq_out.flush()
                return

            sy.wait_ge(g("q_cca"), 1)
            sy.wait_ge(g("c_mm1"), EV_END)
            dq_exp.flush()
            for rr in range(4):
                srcx = ccA_out[rr * DIN:(rr + 1) * DIN, :].rearrange("(gg kd p) t -> gg p kd t", gg=4, p=128)[ts(myg, 1), :, :, :]
                dq_xd.dma("xd" if rr == 3 else None,
                          out=xdb[:, :, rr * 1024:(rr + 1) * 1024], in_=srcx)
            for gse in range(4):
                sy.wait_ge(g("c_ev2"), 8 * (gse + 1))
                dq_ssm.dma(f"ssm{gse}", out=ccC_in[gse * 1024:(gse + 1) * 1024, :].rearrange("(tc p) e -> p tc e", p=128),
                           in_=ssmst[:, :, :])
            sy.wait_ge(g("c_mm2"), LTC)
            dq_xcsp.dma("xcsp", out=xcspd.rearrange("(k p) t -> p k t", p=128), in_=xdb[:, :, :])
            if debug_phase == 21:
                sy.wait_ge(g("q_xcsp"), 16)
                dq_out.dma(None, out=dbg_out, in_=xcspd)
                dq_out.flush()
                return
            sy.wait_ge(g("q_xcsp"), 16)
            dq_moe2.dma("moe2", out=moesb[:, :, :], in_=moed.rearrange("(k p) t -> p k t", p=128))
            sy.wait_ge(g("q_ccc"), 1)
            if debug_phase == 22:
                dq_out.dma(None, out=dbg_out, in_=ccC_out)
                dq_out.flush()
                return
            dq_ssml.dma("ssml", out=ssm2[:, :, :], in_=ccC_out.rearrange("(tc p) e -> p tc e", p=128))
            for i in range(64):
                hc, th = divmod(i, 2)
                sy.wait_ge(g("c_evf"), i + 1)
                dq_rout.dma(f"rout{i}", out=routd[ts(hc, 128), ts(th, 512)], in_=fc2st[:, i % 2, :])
            sy.wait_ge(g("c_bcc"), 2 * LTC)
            dq_brow.dma(None, out=browd.rearrange("one (n tc j) -> one n tc j", n=NST, tc=LTC),
                        in_=btst[:, :, :])
            dq_brow.dma("brow", out=crowd.rearrange("one (n tc j) -> one n tc j", n=NST, tc=LTC),
                        in_=ctst[:, :, :])
            for i in range(2 * NKD):
                kd, hf = divmod(i, 2)
                sy.wait_ge(g("c_sp"), i + 1)
                dq_dtsp.dma(f"dtsp{i}", out=dtspd[ts(kd, 128), ts(hf, 2048)], in_=dtst[:, i % 2, :])
            if debug_phase == 2:
                dq_dtsp.flush()
                dq_out.dma(None, out=dbg_out, in_=dtspd)
                dq_out.flush()
                return

            dq_dtsp.flush()
            dq_brow.flush()
            dq_rout.flush()
            dq_hact.dma("hact0", out=hacth[:, :, :],
                        in_=hactd.rearrange("(k p) t -> p k t", p=128)[:, :, 0:512])

            def p4_interleave(i):
                if i < 64:
                    j = i
                    th4, hc4 = dwn_sched[j]
                    if j >= 4:
                        sy.wait_ge(g("c_ev4"), j - 3)
                    dq_routt.dma(f"routt{j}", out=routt[:, j % 4, :], in_=routd[ts(hc4, 128), ts(th4, 512)])
                je = i - 4
                if 0 <= je < 64:
                    th4, hc4 = dwn_sched[je]
                    sy.wait_ge(g("c_ev4b"), 4 * (je + 1))
                    dq_ffnT.dma(f"ffnT{je}",
                                out=ffnTd[th4 * 512:(th4 + 1) * 512, ts(hc4, 128)].rearrange("(sub p) c -> p sub c", p=128),
                                in_=ftst[:, je % 2, :, :])
                if i == 34:
                    sy.wait_ge(g("c_mm4"), 32)
                    dq_hact.dma("hact1", out=hacth[:, :, :],
                                in_=hactd.rearrange("(k p) t -> p k t", p=128)[:, :, 512:1024])

            for q in range(4):
                for n in range(NST):
                    if q > 0:
                        sy.wait_ge(g("c_b"), q * 256)
                        sy.wait_ge(g("c_hc"), q * 240)
                    dq_bc.dma(None, out=brep[:, n, :],
                              in_=browd[0:1, n * L + q * T:n * L + (q + 1) * T].partition_broadcast(128))
                    dq_bc.dma(f"bc{q}_{n}", out=crep[:, n, :],
                              in_=crowd[0:1, n * L + q * T:n * L + (q + 1) * T].partition_broadcast(128))
                for k in range(NKD):
                    i = q * NKD + k
                    if i >= 2:
                        sy.wait_ge(g("c_fin"), i - 1)
                    dq_sc.dma(None, out=dtq[:, i % 2, :], in_=dtspd[ts(k, 128), ts(q, T)])
                    dq_sc.dma(f"sc{i}", out=xcq[:, i % 2, :], in_=xcspd[ts(k, 128), ts(q, T)])
                    p4_interleave(i)
            for je in range(60, 64):
                th4, hc4 = dwn_sched[je]
                sy.wait_ge(g("c_ev4b"), 4 * (je + 1))
                dq_ffnT.dma(f"ffnT{je}",
                            out=ffnTd[th4 * 512:(th4 + 1) * 512, ts(hc4, 128)].rearrange("(sub p) c -> p sub c", p=128),
                            in_=ftst[:, je % 2, :, :])
            if debug_phase == 3:
                sy.wait_ge(g("q_yexp"), 16 * 64)
                dq_out.dma(None, out=dbg_out, in_=ccB_in)
                dq_out.flush()
                return

            dq_ffnT.flush()
            sy.wait_ge(g("c_fin"), 64)
            sy.wait_ge(g("q_yexp"), 16 * 64)
            dq_xpin.dma(None, out=gbr[:, 0, :], in_=lngamma.partition_broadcast(128))
            dq_xpin.dma("xpg", out=gbr[:, 1, :], in_=lnbeta.partition_broadcast(128))
            for tc in range(NTC):
                if tc >= 2:
                    sy.wait_ge(g("c_xp"), tc - 1)
                dq_xpin.dma(f"fft{tc}", out=fft[:, tc % 2, :], in_=ffnTd[ts(tc, 128), :])
                dq_xpin.dma(f"xrt{tc}", out=xrt[:, tc % 2, :], in_=x_tok[ts(tc, 128), :])
                if tc >= 2:
                    sy.wait_ge(g("c_xp"), tc - 1)
                    dq_xpout.dma(f"xpo{tc - 2}", out=xplusd[ts(tc - 2, 128), :], in_=xpt[:, tc % 2, :])
            for tc in range(NTC - 2, NTC):
                sy.wait_ge(g("c_xp"), tc + 1)
                dq_xpout.dma(f"xpo{tc}", out=xplusd[ts(tc, 128), :], in_=xpt[:, tc % 2, :])
            if debug_phase == 4:
                dq_xpout.flush()
                dq_out.dma(None, out=dbg_out, in_=xplusd)
                dq_out.flush()
                return

            sy.wait_ge(g("q_ccb"), 1)
            dq_xpout.flush()
            sy.wait_ge(g("c_mm4"), 64)
            for rr in range(4):
                srcy = ccB_out[rr * 4 * DSL:(rr + 1) * 4 * DSL, :].rearrange("(j kl p) t -> j p kl t", j=4, p=128)[ts(myg, 1), :, :, :]
                dq_y.dma("y" if rr == 3 else None,
                         out=ybuf[:, rr * 16:(rr + 1) * 16, :], in_=srcy)
            for i in range(64):
                if i >= 2:
                    sy.wait_ge(g("c_zs"), i - 1)
                dq_z.dma(f"z{i}", out=zkt[:, i % 2, :], in_=zbufd[ts(i, 128), :])
            def wo_load(i):
                hc4, qf = divmod(i, 4)
                if i >= 2:
                    sy.wait_ge(g("c_tr5"), (i - 1) * 16)
                dq_wo[i % 2].dma(f"wo{i}", out=wosl[:, i % 2, :], in_=outw[ts(hc4, 128), ts(qf, 2048)])
            wo_load(0)
            wo_load(1)
            for hc in range(32):
                if hc >= 2:
                    sy.wait_ge(g("c_asm"), (hc - 1) * 8)
                dq_xpc.dma(f"xpc{hc}", out=xpcs[:, hc % 2, :, :],
                           in_=xplusd.rearrange("(tc p) h -> p tc h", p=128)[:, :, ts(hc, 128)])
                for qq in range(4 * hc + 2, min(4 * hc + 6, 128)):
                    wo_load(qq)
                sy.wait_ge(g("c_asm"), (hc + 1) * 8)
                dq_out.dma(f"out{hc}", out=out_tok.rearrange("(tc p) h -> p tc h", p=128)[:, :, ts(hc, 128)],
                           in_=ost[:, hc % 2, :, :])
            dq_out.flush()

        @blk.gpsimd
        def _(gp):
            dq_gw = DQ(gp, "q_gw"); dq_w = [DQ(gp, "q_w0"), DQ(gp, "q_w1")]
            dq_wx = DQ(gp, "q_wx"); dq_fcs = DQ(gp, "q_fcs"); dq_dwn = DQ(gp, "q_dwn")
            dq_yexp = DQ(gp, "q_yexp")

            def slab_src(w, og, width=512):
                return w.rearrange("(k p) c -> p k c", p=128)[:, :, og * width:(og + 1) * width]
            if debug_phase == 10:
                return
            dq_gw.dma("gatew", out=gsb[:, :, :], in_=gatew.rearrange("(k p) e -> p k e", p=128))
            for og in range(32):
                if og >= 2:
                    gp.wait_ge(g("c_mm1"), EV_INP + 8 * (og - 1))
                dq_w[og % 2].dma(f"win{og}", out=wslab[:, og % 2, :, :], in_=slab_src(w_in, og))
            if debug_phase == 11:
                return
            for og in range(4):
                gp.wait_ge(g("c_mm1"), (EV_INP + 256) if og < 2 else (EV_FC1 + 8 * (og - 1)))
                dq_w[og % 2].dma(f"fc1w{og}", out=wslab[:, og % 2, :, :], in_=slab_src(fc1w, og))
            for og in range(4):
                gp.wait_ge(g("c_mm1"), (EV_FC1 + 32) if og < 2 else (EV_EXP + 8 * (og - 1)))
                dq_w[og % 2].dma(f"expw{og}", out=wslab[:, og % 2, 0:16, :], in_=slab_src(expw, og))
            for og in range(16):
                gp.wait_ge(g("c_mm1"), (EV_EXP + 32) if og < 2 else (EV_UP + 8 * (og - 1)))
                dq_w[og % 2].dma(f"upw{og}", out=wslab[:, og % 2, :, :], in_=slab_src(upw, og))
            wld(gp, "exp15")
            gp.collective_compute("AllGather", ALU.bypass, replica_groups=RG,
                                  ins=[ccA_in.opt()], outs=[ccA_out.opt()]).then_inc(g("q_cca"))
            if debug_phase == 1:
                return
            gp.wait_ge(g("c_mm1"), EV_END)
            wld(gp, "expu15")
            dq_wx.dma(None, out=wxa[:, :], in_=xprojw[0:128, :])
            dq_wx.dma(None, out=wxb[:, :], in_=xprojw[128:256, :])
            dq_wx.dma("wxc3", out=wxc[:, :], in_=xprojw[256:288, :])
            gp.wait_ge(g("q_xcsp"), 16)
            dq_wx.dma("wdtn", out=wdtnat[:, :, :], in_=dtprojw.rearrange("(k p) r -> p k r", p=128))
            if debug_phase == 21:
                return
            wld(gp, "ssm3")
            gp.collective_compute("AllReduce", ALU.add, replica_groups=RG,
                                  ins=[ccC_in.opt()], outs=[ccC_out.opt()]).then_inc(g("q_ccc"))
            if debug_phase == 22:
                return
            for hc in range(32):
                if hc >= 2:
                    gp.wait_ge(g("c_mmf"), 2 * (hc - 1))
                dq_fcs.dma(f"fcs{hc}", out=fcsl[:, hc % 2, :, :],
                           in_=fc2w.rearrange("(k p) c -> p k c", p=128)[:, :, ts(hc, 128)])
            if debug_phase in (2, 21, 22):
                return
            wld(gp, "dtsp31")
            wld(gp, "rout63")
            dwn_issued = 0
            for q in range(4):
                for k in range(NKD):
                    i = q * NKD + k
                    while dwn_issued < 64 and dwn_issued * 64 <= i * 64 + 32:
                        jj = dwn_issued
                        th4, hc4 = dwn_sched[jj]
                        if jj >= 2:
                            gp.wait_ge(g("c_mm4"), jj - 1)
                        dq_dwn.dma(f"dwn{jj}", out=dwsl[:, jj % 2, :, :],
                                   in_=downw.rearrange("(k p) c -> p k c", p=128)[:, :, ts(hc4, 128)])
                        dwn_issued += 1
                    for n in range(1, NST):
                        gp.wait_ge(g("c_hc"), i * 15 + n)
                        if n >= 2:
                            chp.join(gp)
                        gp.tensor_tensor(yac[:, :], yac[:, :], hcb[:, n % 2, :], ALU.add).then_inc(g("c_y"), 1)
                        chp.mark("c_y", i * 15 + n)
                    gp.wait_ge(g("c_fin"), i + 1)
                    dq_yexp.dma(f"yexp{i}", out=ccB_in[q * DSL + k * 128:q * DSL + (k + 1) * 128, :],
                                in_=y2t[:, :])
            while dwn_issued < 64:
                jj = dwn_issued
                th4, hc4 = dwn_sched[jj]
                if jj >= 2:
                    gp.wait_ge(g("c_mm4"), jj - 1)
                dq_dwn.dma(f"dwn{jj}", out=dwsl[:, jj % 2, :, :],
                           in_=downw.rearrange("(k p) c -> p k c", p=128)[:, :, ts(hc4, 128)])
                dwn_issued += 1
            dq_yexp.flush()
            gp.collective_compute("AllGather", ALU.bypass, replica_groups=RG,
                                  ins=[ccB_in.opt()], outs=[ccB_out.opt()]).then_inc(g("q_ccb"))

        @blk.tensor
        def _(t):
            wld(t, "misc")
            for tc in range(NTC):
                t.wait_ge(g("c_ln"), tc + 1)
                for hc in range(NKH):
                    i = tc * NKH + hc
                    if i >= 4:
                        t.wait_ge(g("c_xnf"), i - 3)
                    t.transpose(ps[:, i % 4, 0:128], xnstage[:, tc % 2, ts(hc, 128)], s_identf[:]).then_inc(g("c_tr0"), 1)
            def mm_slabs(n_og, nk, ev_base, wname, rhs_sb):
                for og in range(n_og):
                    wld(t, wname.format(og))
                    t.wait_ge(g("c_ev1"), ev_base + 8 * og)
                    for j in range(4):
                        for th in range(2):
                            last = None
                            for k in range(nk):
                                last = t.matmul(ps[:, j * 2 + th, :],
                                                wslab[:, og % 2, k, ts(j, 128)],
                                                rhs_sb[:, k, ts(th, 512)],
                                                start=(k == 0), stop=(k == nk - 1))
                            last.then_inc(g("c_mm1"), 1)
            if debug_phase == 10:
                return
            t.wait_ge(g("c_xnf"), 256)
            mm_slabs(32, NKH, EV_INP, "win{}", xnf)
            if debug_phase == 11:
                return
            wld(t, "gatew")
            for tc in range(NTC):
                t.wait_ge(g("c_ev1"), EV_GATE + tc)
                last = None
                for k in range(NKH):
                    last = t.matmul(ps[:, 6, 0:NE], xnf[:, k, ts(tc, 128)], gsb[:, k, :],
                                    start=(k == 0), stop=(k == NKH - 1))
                last.then_inc(g("c_mm1"), 1)
            for tc in range(NTC):
                t.wait_ge(g("c_scol"), tc + 1)
                if tc >= 1:
                    t.wait_ge(g("c_srow"), tc)
                t.transpose(ps[0:1, 7, 0:128], scols[:, tc:tc + 1], s_identf[:]).then_inc(g("c_str"), 1)
            t.wait_ge(g("c_srow"), NTC)
            mm_slabs(4, NKH, EV_FC1, "fc1w{}", xnf)
            t.wait_ge(g("c_ev1"), EV_FC1 + 32)
            mm_slabs(4, 16, EV_EXP, "expw{}", xlat)
            mm_slabs(16, NKH, EV_UP, "upw{}", xnf)
            if debug_phase == 1:
                return
            wld(t, "wxc3")
            for kd in range(NKD):
                if kd >= 1:
                    t.wait_ge(g("c_wxc"), 3 * kd)
                t.transpose(ps[:, 4, 0:128], wxa[:, ts(kd, 128)], s_identf[:])
                t.transpose(ps[:, 5, 0:128], wxb[:, ts(kd, 128)], s_identf[:])
                t.transpose(ps[:, 6, 0:32], wxc[0:32, ts(kd, 128)], s_identf[0:32, 0:32]).then_inc(g("c_trwx"), 3)
            wld(t, "xd")
            t.wait_ge(g("c_cvb"), NKD)
            t.wait_ge(g("c_wxc"), 3 * NKD)
            for tc in range(LTC):
                if tc >= 4:
                    t.wait_ge(g("c_ev2"), tc - 3)
                last = None
                for kd in range(NKD):
                    last = t.matmul(ps[:, tc % 4, 0:R + 2 * NST], xdb[:, kd, ts(tc, 128)], wxT[:, kd, :],
                                    start=(kd == 0), stop=(kd == NKD - 1))
                last.then_inc(g("c_mm2"), 1)
            if debug_phase in (21, 22):
                return
            t.wait_ge(g("c_ev2"), LTC)
            wld(t, "moe2")
            for i in range(64):
                hc, th = divmod(i, 2)
                wld(t, f"fcs{hc}")
                if i >= 4:
                    t.wait_ge(g("c_evf"), i - 3)
                last = None
                for k in range(16):
                    last = t.matmul(ps[:, i % 4, :], fcsl[:, hc % 2, k, :], moesb[:, k, ts(th, 512)],
                                    start=(k == 0), stop=(k == 15))
                last.then_inc(g("c_mmf"), 1)
            wld(t, "wdtn")
            for kd in range(NKD):
                for rc in range(2):
                    i = kd * 2 + rc
                    if i >= 1:
                        t.wait_ge(g("c_wdtc"), i)
                    t.transpose(ps[:, 7, 0:128], wdtnat[:, kd, ts(rc, 128)], s_identf[:]).then_inc(g("c_trwd"), 1)
            for tc in range(LTC):
                t.wait_ge(g("c_rms"), 3 * (tc + 1))
                if tc >= 1:
                    t.wait_ge(g("c_bcc"), 2 * tc)
                t.transpose(ps[0:16, 4, 0:128], bnt[:, tc % 4, :], s_identf[:])
                t.transpose(ps[0:16, 5, 0:128], cnt[:, tc % 4, :], s_identf[:]).then_inc(g("c_bct"), 2)
                for rc in range(2):
                    i = tc * 2 + rc
                    if i >= 1:
                        t.wait_ge(g("c_dtrc"), i)
                    t.transpose(ps[:, 6, 0:128], dtr32[:, tc % 4, ts(rc, 128)], s_identf[:]).then_inc(g("c_dtt"), 1)
            t.wait_ge(g("c_evf"), 64)
            t.wait_ge(g("c_dtrc"), 64)
            t.wait_ge(g("c_wdtc"), 32)
            for i in range(2 * NKD):
                kd, hf = divmod(i, 2)
                if i >= 1:
                    t.wait_ge(g("c_sp0"), i)
                last = None
                for f in range(4):
                    for rc in range(2):
                        last = t.matmul(ps[:, f, :], wdtT[:, rc, ts(kd, 128)],
                                        dtrT[:, rc, hf * 2048 + f * 512:hf * 2048 + (f + 1) * 512],
                                        start=(rc == 0), stop=(rc == 1))
                last.then_inc(g("c_mm3"), 1)
            if debug_phase in (2, 21, 22):
                return
            def p4trs(jt):
                t.wait_ge(g("c_ev4"), jt + 1)
                for sub in range(4):
                    si = jt * 4 + sub
                    if si >= 2:
                        t.wait_ge(g("c_ev4b"), si - 1)
                    t.transpose(ps[:, 6 + si % 2, 0:128], f4st[:, jt % 2, :][:, ts(sub, 128)],
                                s_identf[:]).then_inc(g("c_tr4"), 1)
            for j in range(64):
                if j % 32 == 0:
                    wld(t, f"hact{j // 32}")
                wld(t, f"dwn{j}")
                if j >= 2:
                    t.wait_ge(g("c_ev4"), j - 1)
                pb = (j % 2) * 2
                last = None
                for k in range(64):
                    last = t.matmul(ps[:, pb, :], dwsl[:, j % 2, k, :],
                                    hacth[:, k, :], start=(k == 0), stop=(k == 63))
                last.then_inc(g("c_mm4"), 1)
                if j >= 1:
                    p4trs(j - 1)
            p4trs(63)
            if debug_phase <= 4:
                return
            wld(t, "y")
            t.wait_ge(g("c_gate"), 64)
            for hc in range(32):
                for qf in range(4):
                    ii = hc * 4 + qf
                    wld(t, f"wo{ii}")
                    for k in range(16):
                        i = ii * 16 + k
                        if i >= 2:
                            t.wait_ge(g("c_ev5"), i - 1)
                        t.transpose(ps[:, 4 + i % 2, 0:128], wosl[:, ii % 2, ts(k, 128)],
                                    s_identf[:]).then_inc(g("c_tr5"), 1)
                t.wait_ge(g("c_ev5"), (hc + 1) * 64)
                if hc >= 2:
                    t.wait_ge(g("c_ev5b"), hc - 1)
                pb = (hc % 2) * 2
                for th in range(2):
                    last = None
                    for k in range(64):
                        last = t.matmul(ps[:, pb + th, :], wline[:, k, :], ybuf[:, k, ts(th, 512)],
                                        start=(k == 0), stop=(k == 63))
                    last.then_inc(g("c_mm5"), 1)
                t.wait_ge(g("c_ev5b"), hc + 1)
                for tcc in range(8):
                    i = hc * 8 + tcc
                    if i >= 2:
                        t.wait_ge(g("c_asm"), i - 1)
                    t.transpose(ps[:, 6 + i % 2, 0:128], a5st[:, ts(tcc, 128)],
                                s_identf[:]).then_inc(g("c_tr5b"), 1)

        @blk.scalar
        def _(s):
            def xnf_copies(tcb):
                for hc in range(NKH):
                    i = tcb * NKH + hc
                    s.wait_ge(g("c_tr0"), i + 1)
                    s.copy(xnf[:, hc, ts(tcb, 128)], ps[:, i % 4, 0:128]).then_inc(g("c_xnf"), 1)
            for tc in range(NTC):
                if tc >= 2:
                    xnf_copies(tc - 2)
                wld(s, f"x{tc}")
                if tc >= 1:
                    s.wait_ge(g("c_lnr"), tc)
                if tc >= 2:
                    s.wait_ge(g("c_tr0"), 32 * (tc - 1))
                s.activation(xnstage[:, tc % 2, :], xt[:, tc % 2, :], AF.Square).then_inc(g("c_sq0"), 1)
                s.wait_ge(g("c_veps"), tc + 1)
                chs.bump(s.activation(lncols[:, 7:8], lncols[:, 6:7], AF.Ln))
                chs.join(s)
                s.activation(lncols[:, 7:8], lncols[:, 7:8], AF.Exp, scale=-0.5).then_inc(g("c_rstd"), 1)
            xnf_copies(NTC - 2)
            xnf_copies(NTC - 1)
            if debug_phase == 10 or debug_phase == 11:
                return
            for og in range(16):
                for jt in range(8):
                    idx = og * 8 + jt
                    j, th = divmod(jt, 2)
                    s.wait_ge(g("c_relu"), idx + 1)
                    s.activation(evst[:, og % 2, j, th * 512:(th + 1) * 512], rlst[:, idx % 2, :],
                                 AF.Square).then_inc(g("c_ev1"), 1)
            if debug_phase == 1:
                return
            for k in range(NKD):
                s.wait_ge(g("c_tap"), 4 * (k + 1))
                if k >= 1:
                    s.wait_ge(g("c_cvb"), k)
                s.activation(cvst[:, :], cacc[:, :], AF.Silu).then_inc(g("c_silu"), 1)
            if debug_phase in (21, 22):
                return
            wld(s, "ssml")
            for tc in range(LTC):
                if tc >= 1:
                    s.wait_ge(g("c_red3"), tc)
                s.activation(sqst[:, :], ssm2[:, tc, :], AF.Square).then_inc(g("c_sq2"), 1)
                s.wait_ge(g("c_vep3"), tc + 1)
                chs.bump(s.activation(rmcols[:, 3:4], rmcols[:, 3:4], AF.Ln))
                chs.bump(s.activation(rmcols[:, 4:5], rmcols[:, 4:5], AF.Ln))
                chs.bump(s.activation(rmcols[:, 5:6], rmcols[:, 5:6], AF.Ln))
                chs.join(s)
                s.activation(rmcols[:, 3:4], rmcols[:, 3:4], AF.Exp, scale=-0.5)
                s.activation(rmcols[:, 4:5], rmcols[:, 4:5], AF.Exp, scale=-0.5)
                s.activation(rmcols[:, 5:6], rmcols[:, 5:6], AF.Exp, scale=-0.5).then_inc(g("c_rst3"), 1)
            for i in range(2 * NKD):
                s.wait_ge(g("c_sp0"), i + 1)
                chs.bump(s.activation(spa[:, i % 2, :], spb[:, i % 2, :], AF.Abs))
                chs.join(s)
                chs.bump(s.activation(spa[:, i % 2, :], spa[:, i % 2, :], AF.Exp, scale=-1.0))
                chs.join(s)
                s.activation(spa[:, i % 2, :], spa[:, i % 2, :], AF.Ln, bias=1.0).then_inc(g("c_spa"), 1)
            if debug_phase in (2, 21, 22):
                return
            wld(s, "misc")
            chs.bump(s.activation(s_A[:, :, :], s_A[:, :, :], AF.Exp))
            chs.join(s)
            chs.bump(s.mul(s_A[:, :, :], s_A[:, :, :], -1.0))
            chs.join(s)
            for q in range(4):
                for k in range(NKD):
                    i = q * NKD + k
                    wld(s, f"sc{i}")
                    for n in range(NST):
                        idx = i * NST + n
                        if idx >= 2:
                            s.wait_ge(g("c_h"), idx - 1)
                        s.activation(abuf[:, idx % 2, :], dtq[:, i % 2, :], AF.Exp,
                                     scale=s_A[:, k, n:n + 1]).then_inc(g("c_a"), 1)
                    j = i - 3
                    if 0 <= j < 64:
                        if j >= 2:
                            wld(s, f"ffnT{j - 2}")
                        for sub in range(4):
                            si = j * 4 + sub
                            s.wait_ge(g("c_tr4"), si + 1)
                            s.copy(ftst[:, j % 2, sub, :], ps[:, 6 + si % 2, 0:128]).then_inc(g("c_ev4b"), 1)
            for j in (61, 62, 63):
                wld(s, f"ffnT{j - 2}")
                for sub in range(4):
                    si = j * 4 + sub
                    s.wait_ge(g("c_tr4"), si + 1)
                    s.copy(ftst[:, j % 2, sub, :], ps[:, 6 + si % 2, 0:128]).then_inc(g("c_ev4b"), 1)
            if debug_phase == 3:
                return
            for tc in range(NTC):
                wld(s, f"fft{tc}")
                if tc >= 1:
                    s.wait_ge(g("c_xp"), tc)
                s.activation(sqxt[:, :], fft[:, tc % 2, :], AF.Square).then_inc(g("c_sqx"), 1)
                s.wait_ge(g("c_vepx"), tc + 1)
                chs.bump(s.activation(xpcols[:, 7:8], xpcols[:, 6:7], AF.Ln))
                chs.join(s)
                s.activation(xpcols[:, 7:8], xpcols[:, 7:8], AF.Exp, scale=-0.5).then_inc(g("c_rstx"), 1)
            if debug_phase <= 4:
                return
            for i in range(64):
                wld(s, f"z{i}")
                if i >= 2:
                    s.wait_ge(g("c_gate"), i - 1)
                s.activation(zst[:, i % 2, :], zkt[:, i % 2, :], AF.Silu).then_inc(g("c_zs"), 1)
            for i in range(2048):
                s.wait_ge(g("c_tr5"), i + 1)
                s.copy(wline[:, i % 64, :], ps[:, 4 + i % 2, 0:128]).then_inc(g("c_ev5"), 1)

        @blk.vector
        def _(v):
            for tc in range(NTC):
                wld(v, f"x{tc}")
                chv.bump(v.reduce_sum(lncols[:, 0:1], xt[:, tc % 2, :], axis=AX))
                chv.join(v)
                chv.bump(v.tensor_scalar(lncols[:, 1:2], lncols[:, 0:1], 1.0 / H, None, ALU.mult))
                v.wait_ge(g("c_sq0"), tc + 1)
                v.reduce_sum(lncols[:, 2:3], xnstage[:, tc % 2, :], axis=AX).then_inc(g("c_lnr"), 1)
                chv.mark("c_lnr", tc + 1)
                chv.join(v)
                chv.bump(v.tensor_scalar(lncols[:, 3:4], lncols[:, 2:3], 1.0 / H, None, ALU.mult))
                chv.bump(v.tensor_tensor(lncols[:, 4:5], lncols[:, 1:2], lncols[:, 1:2], ALU.mult))
                chv.join(v)
                chv.bump(v.tensor_tensor(lncols[:, 5:6], lncols[:, 3:4], lncols[:, 4:5], ALU.subtract))
                chv.join(v)
                v.tensor_scalar(lncols[:, 6:7], lncols[:, 5:6], 1.0, EPS_LN, ALU.mult, ALU.add).then_inc(g("c_veps"), 1)
                chv.mark("c_veps", tc + 1)
                if tc >= 2:
                    v.wait_ge(g("c_tr0"), 32 * (tc - 1))
                v.wait_ge(g("c_rstd"), tc + 1)
                v.tensor_scalar(xnstage[:, tc % 2, :], xt[:, tc % 2, :], lncols[:, 1:2], lncols[:, 7:8],
                                ALU.subtract, ALU.mult).then_inc(g("c_ln"), 1)
            if debug_phase == 10:
                return
            for og in range(32):
                if og >= 2:
                    wld(v, f"exp{og - 2}")
                for jt in range(8):
                    j, th = divmod(jt, 2)
                    v.wait_ge(g("c_mm1"), EV_INP + og * 8 + jt + 1)
                    v.tensor_copy(evst[:, og % 2, j, th * 512:(th + 1) * 512], ps[:, jt, :]).then_inc(g("c_ev1"), 1)
            if debug_phase == 11:
                return
            for tc in range(NTC):
                v.wait_ge(g("c_mm1"), EV_GATE + tc + 1)
                v.tensor_copy(logit[:, :], ps[:, 6, 0:NE]).then_inc(g("c_ev1"), 1)
                chv.mark("c_ev1", EV_GATE + tc + 1)
                chv.join(v)
                chv.bump(v.reduce_max(l_m[:, 0:1], logit[:, :], axis=AX))
                chv.join(v)
                chv.bump(v.tensor_scalar(lgm[:, :], logit[:, :], l_m[:, 0:1], -1e30, ALU.is_ge, ALU.mult))
                chv.join(v)
                chv.bump(v.tensor_tensor(lgm[:, :], lgm[:, :], logit[:, :], ALU.add))
                chv.join(v)
                chv.bump(v.reduce_max(l_m[:, 1:2], lgm[:, :], axis=AX))
                chv.join(v)
                v.tensor_tensor(scols[:, tc:tc + 1], l_m[:, 0:1], l_m[:, 1:2], ALU.add).then_inc(g("c_scol"), 1)
            for tc in range(NTC):
                v.wait_ge(g("c_str"), tc + 1)
                v.tensor_copy(s_srow[0:1, ts(tc, 128)], ps[0:1, 7, 0:128]).then_inc(g("c_srow"), 1)
            for og in range(4):
                for jt in range(8):
                    j, th = divmod(jt, 2)
                    v.wait_ge(g("c_mm1"), EV_FC1 + og * 8 + jt + 1)
                    v.tensor_copy(xlat[:, og * 4 + j, th * 512:(th + 1) * 512], ps[:, jt, :]).then_inc(g("c_ev1"), 1)
            wld(v, "srep")
            for og in range(4):
                wld(v, f"exp{30 + og}" if og < 2 else f"expm{og - 2}")
                for jt in range(8):
                    j, th = divmod(jt, 2)
                    v.wait_ge(g("c_mm1"), EV_EXP + og * 8 + jt + 1)
                    v.tensor_tensor(evst[:, og % 2, j, th * 512:(th + 1) * 512], ps[:, jt, :],
                                    s_rep[:, th * 512:(th + 1) * 512], ALU.mult).then_inc(g("c_ev1"), 1)
            for og in range(16):
                wld(v, f"expm{2 + og}" if og < 2 else f"expu{og - 2}")
                for jt in range(8):
                    idx = og * 8 + jt
                    v.wait_ge(g("c_mm1"), EV_UP + idx + 1)
                    if idx >= 2:
                        v.wait_ge(g("c_ev1"), EV_UP + idx - 1)
                    v.tensor_scalar(rlst[:, idx % 2, :], ps[:, jt, :], 0.0, None, ALU.max).then_inc(g("c_relu"), 1)
            if debug_phase == 1:
                return
            for kd in range(NKD):
                v.wait_ge(g("c_trwx"), 3 * (kd + 1))
                v.tensor_copy(wxT[:, kd, 0:128], ps[:, 4, 0:128])
                v.tensor_copy(wxT[:, kd, 128:256], ps[:, 5, 0:128])
                v.tensor_copy(wxT[:, kd, 256:288], ps[:, 6, 0:32]).then_inc(g("c_wxc"), 3)
            wld(v, "xd")
            wld(v, "misc")
            for k in range(NKD):
                v.tensor_scalar(cacc[:, :], xdb[:, k, :], s_convw[:, k, 3:4], s_convb[:, k:k + 1],
                                ALU.mult, ALU.add).then_inc(g("c_tap"), 1)
                chv.mark("c_tap", 4 * k + 1)
                for kk in range(3):
                    o = 3 - kk
                    chv.join(v)
                    v.scalar_tensor_tensor(cacc[:, o:L], xdb[:, k, 0:L - o], s_convw[:, k, kk:kk + 1],
                                           cacc[:, o:L], ALU.mult, ALU.add).then_inc(g("c_tap"), 1)
                    chv.mark("c_tap", 4 * k + 1 + kk + 1)
                v.wait_ge(g("c_silu"), k + 1)
                v.tensor_copy(xdb[:, k, :], cvst[:, :]).then_inc(g("c_cvb"), 1)
            for tc in range(LTC):
                v.wait_ge(g("c_mm2"), tc + 1)
                if tc >= 8:
                    wld(v, f"ssm{tc // 8 - 1}")
                v.tensor_copy(ssmst[:, tc % 8, :], ps[:, tc % 4, 0:R + 2 * NST]).then_inc(g("c_ev2"), 1)
            if debug_phase in (21, 22):
                return
            for i in range(64):
                hc, th = divmod(i, 2)
                v.wait_ge(g("c_mmf"), i + 1)
                if i >= 2:
                    wld(v, f"rout{i - 2}")
                v.tensor_copy(fc2st[:, i % 2, :], ps[:, i % 4, :]).then_inc(g("c_evf"), 1)
            for kd in range(NKD):
                for rc in range(2):
                    i = kd * 2 + rc
                    v.wait_ge(g("c_trwd"), i + 1)
                    v.tensor_copy(wdtT[:, rc, ts(kd, 128)], ps[:, 7, 0:128]).then_inc(g("c_wdtc"), 1)
            def bc_dtr_copies(tcb):
                v.wait_ge(g("c_bct"), 2 * (tcb + 1))
                v.tensor_copy(btst[:, tcb, :], ps[0:16, 4, 0:128])
                v.tensor_copy(ctst[:, tcb, :], ps[0:16, 5, 0:128]).then_inc(g("c_bcc"), 2)
                for rc in range(2):
                    i2 = tcb * 2 + rc
                    v.wait_ge(g("c_dtt"), i2 + 1)
                    v.tensor_copy(dtrT[:, rc, ts(tcb, 128)], ps[:, 6, 0:128]).then_inc(g("c_dtrc"), 1)
            wld(v, "ssml")
            for tc in range(LTC):
                v.wait_ge(g("c_sq2"), tc + 1)
                if tc >= 4:
                    v.wait_ge(g("c_dtt"), 2 * (tc - 3))
                    v.wait_ge(g("c_bct"), 2 * (tc - 3))
                v.reduce_sum(rmcols[:, 0:1], sqst[:, 0:R], axis=AX)
                v.reduce_sum(rmcols[:, 1:2], sqst[:, R:R + NST], axis=AX)
                v.reduce_sum(rmcols[:, 2:3], sqst[:, R + NST:], axis=AX).then_inc(g("c_red3"), 1)
                chv.mark("c_red3", tc + 1)
                chv.join(v)
                v.tensor_scalar(rmcols[:, 3:4], rmcols[:, 0:1], 1.0 / R, EPS_RMS, ALU.mult, ALU.add)
                v.tensor_scalar(rmcols[:, 4:5], rmcols[:, 1:2], 1.0 / NST, EPS_RMS, ALU.mult, ALU.add)
                v.tensor_scalar(rmcols[:, 5:6], rmcols[:, 2:3], 1.0 / NST, EPS_RMS, ALU.mult, ALU.add).then_inc(g("c_vep3"), 1)
                v.wait_ge(g("c_rst3"), tc + 1)
                v.scalar_tensor_tensor(dtr32[:, tc % 4, :], ssm2[:, tc, 0:R], rmcols[:, 3:4],
                                       s_gamr[:, 0:R], ALU.mult, ALU.mult).then_inc(g("c_rms"), 1)
                v.scalar_tensor_tensor(bnt[:, tc % 4, :], ssm2[:, tc, R:R + NST], rmcols[:, 4:5],
                                       s_gamr[:, R:R + NST], ALU.mult, ALU.mult).then_inc(g("c_rms"), 1)
                v.scalar_tensor_tensor(cnt[:, tc % 4, :], ssm2[:, tc, R + NST:], rmcols[:, 5:6],
                                       s_gamr[:, R + NST:], ALU.mult, ALU.mult).then_inc(g("c_rms"), 1)
                if tc >= 1:
                    bc_dtr_copies(tc - 1)
            bc_dtr_copies(LTC - 1)
            for i in range(2 * NKD):
                kd, hf = divmod(i, 2)
                v.wait_ge(g("c_mm3"), i + 1)
                if i >= 2:
                    wld(v, f"dtsp{i - 2}")
                v.tensor_scalar(spb[:, i % 2, :], ps[:, 0:4, :].rearrange("p b f -> p (b f)"),
                                s_dtbias[:, kd:kd + 1], None, ALU.add).then_inc(g("c_sp0"), 1)
                chv.mark("c_sp0", i + 1)
                v.wait_ge(g("c_spa"), i + 1)
                chv.join(v)
                chv.bump(v.tensor_scalar(spb[:, i % 2, :], spb[:, i % 2, :], 0.0, None, ALU.max))
                chv.join(v)
                v.tensor_tensor(dtst[:, i % 2, :], spb[:, i % 2, :], spa[:, i % 2, :], ALU.add).then_inc(g("c_sp"), 1)
            if debug_phase in (2, 21, 22):
                return
            v.memset(s_hstate[:, :], 0.0)
            for q in range(4):
                for k in range(NKD):
                    i = q * NKD + k
                    wld(v, f"sc{i}")
                    v.tensor_tensor(dtxq[:, :], dtq[:, i % 2, :], xcq[:, i % 2, :], ALU.mult).then_inc(g("c_dtx"), 1)
                    chv.mark("c_dtx", i + 1)
                    chv.join(v)
                    for n in range(NST):
                        idx = i * NST + n
                        kn = k * NST + n
                        v.wait_ge(g("c_a"), idx + 1)
                        wld(v, f"bc{q}_{n}")
                        v.tensor_tensor(bbuf[:, :], dtxq[:, :], brep[:, n, :], ALU.mult).then_inc(g("c_b"), 1)
                        chv.mark("c_b", i * NST + n + 1)
                        chv.join(v)
                        v.tensor_tensor_scan(hbuf[:, :], abuf[:, idx % 2, :], bbuf[:, :],
                                             s_hstate[:, kn:kn + 1], ALU.mult, ALU.add).then_inc(g("c_h"), 1)
                        chv.mark("c_h", idx + 1)
                        chv.join(v)
                        v.tensor_copy(s_hstate[:, kn:kn + 1], hbuf[:, T - 1:T])
                        if n == 0:
                            v.tensor_tensor(yac[:, :], hbuf[:, :], crep[:, 0, :], ALU.mult)
                        else:
                            if n >= 3:
                                v.wait_ge(g("c_y"), i * 15 + n - 2)
                            v.tensor_tensor(hcb[:, n % 2, :], hbuf[:, :], crep[:, n, :], ALU.mult).then_inc(g("c_hc"), 1)
                    v.wait_ge(g("c_y"), (i + 1) * 15)
                    if i >= 2:
                        wld(v, f"yexp{i - 2}")
                    v.scalar_tensor_tensor(y2t[:, :], xcq[:, i % 2, :], s_dpar[:, k:k + 1], yac[:, :],
                                           ALU.mult, ALU.add).then_inc(g("c_fin"), 1)
                    j = i - 2
                    if 0 <= j < 64:
                        v.wait_ge(g("c_mm4"), j + 1)
                        wld(v, f"routt{j}")
                        if j >= 2:
                            v.wait_ge(g("c_ev4b"), 4 * (j - 1))
                        pb4 = (j % 2) * 2
                        v.scalar_tensor_tensor(f4st[:, j % 2, :], ps[:, pb4, :], 1.0, routt[:, j % 4, :],
                                               ALU.mult, ALU.add).then_inc(g("c_ev4"), 1)
            for j in (62, 63):
                v.wait_ge(g("c_mm4"), j + 1)
                wld(v, f"routt{j}")
                v.wait_ge(g("c_ev4b"), 4 * (j - 1))
                pb4 = (j % 2) * 2
                v.scalar_tensor_tensor(f4st[:, j % 2, :], ps[:, pb4, :], 1.0, routt[:, j % 4, :],
                                       ALU.mult, ALU.add).then_inc(g("c_ev4"), 1)
            if debug_phase == 3:
                return
            wld(v, "xpg")
            for tc in range(NTC):
                wld(v, f"xrt{tc}")
                v.wait_ge(g("c_sqx"), tc + 1)
                if tc >= 2:
                    wld(v, f"xpo{tc - 2}")
                chv.bump(v.reduce_sum(xpcols[:, 0:1], fft[:, tc % 2, :], axis=AX))
                chv.join(v)
                chv.bump(v.tensor_scalar(xpcols[:, 1:2], xpcols[:, 0:1], 1.0 / H, None, ALU.mult))
                chv.bump(v.reduce_sum(xpcols[:, 2:3], sqxt[:, :], axis=AX))
                chv.join(v)
                chv.bump(v.tensor_scalar(xpcols[:, 3:4], xpcols[:, 2:3], 1.0 / H, None, ALU.mult))
                chv.bump(v.tensor_tensor(xpcols[:, 4:5], xpcols[:, 1:2], xpcols[:, 1:2], ALU.mult))
                chv.join(v)
                chv.bump(v.tensor_tensor(xpcols[:, 5:6], xpcols[:, 3:4], xpcols[:, 4:5], ALU.subtract))
                chv.join(v)
                v.tensor_scalar(xpcols[:, 6:7], xpcols[:, 5:6], 1.0, EPS_LN, ALU.mult, ALU.add).then_inc(g("c_vepx"), 1)
                chv.mark("c_vepx", tc + 1)
                v.wait_ge(g("c_rstx"), tc + 1)
                chv.bump(v.tensor_scalar(xpt[:, tc % 2, :], fft[:, tc % 2, :], xpcols[:, 1:2], xpcols[:, 7:8],
                                         ALU.subtract, ALU.mult))
                chv.join(v)
                chv.bump(v.tensor_tensor(xpt[:, tc % 2, :], xpt[:, tc % 2, :], gbr[:, 0, :], ALU.mult))
                chv.join(v)
                chv.bump(v.tensor_tensor(xpt[:, tc % 2, :], xpt[:, tc % 2, :], gbr[:, 1, :], ALU.add))
                chv.join(v)
                v.tensor_tensor(xpt[:, tc % 2, :], xpt[:, tc % 2, :], xrt[:, tc % 2, :], ALU.add).then_inc(g("c_xp"), 1)
            if debug_phase <= 4:
                return
            wld(v, "y")
            for i in range(64):
                v.wait_ge(g("c_zs"), i + 1)
                v.tensor_tensor(ybuf[:, i, :], ybuf[:, i, :], zst[:, i % 2, :], ALU.mult).then_inc(g("c_gate"), 1)
            for hc in range(32):
                v.wait_ge(g("c_mm5"), 2 * (hc + 1))
                if hc >= 1:
                    v.wait_ge(g("c_tr5b"), 8 * hc)
                pb = (hc % 2) * 2
                v.tensor_copy(a5st[:, :], ps[:, pb:pb + 2, :].rearrange("p b f -> p (b f)")).then_inc(g("c_ev5b"), 1)
                wld(v, f"xpc{hc}")
                if hc >= 2:
                    wld(v, f"out{hc - 2}")
                for tcc in range(8):
                    i = hc * 8 + tcc
                    v.wait_ge(g("c_tr5b"), i + 1)
                    v.tensor_tensor(ost[:, hc % 2, tcc, :], ps[:, 6 + i % 2, 0:128], xpcs[:, hc % 2, tcc, :],
                                    ALU.add).then_inc(g("c_asm"), 1)

    return nc


_CACHE = {}


def kernel(**inputs):
    nc = _CACHE.get("nc99")
    if nc is None:
        nc = build_nc()
        _CACHE["nc99"] = nc
    in_maps = make_in_maps(inputs)
    res = run_bass_kernel_spmd(nc, in_maps, core_ids=list(range(8)))
    out = np.empty((B, L, H), dtype=np.float32)
    for c in range(8):
        b, r = divmod(c, 4)
        out[b, r * T:(r + 1) * T, :] = res.results[c]["out_tok"]
    return out


def make_in_maps(inputs):
    f32 = np.float32
    X = np.ascontiguousarray(np.asarray(inputs["L0_attn_hidden_states"], dtype=f32))
    asf = lambda k: np.ascontiguousarray(np.asarray(inputs[k], dtype=f32))
    shared = dict(
        w_in=asf("L0_attn_in_proj_w"),
        outw=asf("L0_attn_out_proj_w"),
        dtlnw=asf("L0_attn_dt_ln_w").reshape(1, R),
        blnw=asf("L0_attn_B_ln_w").reshape(1, NST),
        clnw=asf("L0_attn_C_ln_w").reshape(1, NST),
        gatew=asf("L0_ffn_gate_w"),
        upw=asf("L0_ffn_up_w"),
        downw=asf("L0_ffn_down_w"),
        fc1w=asf("L0_ffn_fc1_w"),
        fc2w=asf("L0_ffn_fc2_w"),
        expw=asf("L0_ffn_expert_w"),
        lngamma=asf("L0_ffn_ln_gamma").reshape(1, H),
        lnbeta=asf("L0_ffn_ln_beta").reshape(1, H),
        identf=np.eye(128, dtype=f32),
    )
    cw = asf("L0_attn_conv_w"); cb = asf("L0_attn_conv_b")
    xpw = asf("L0_attn_x_proj_w"); dpw = asf("L0_attn_dt_proj_w")
    dtb = asf("L0_attn_dt_bias"); al = asf("L0_attn_A_log"); dp = asf("L0_attn_D_param")
    in_maps = []
    for c in range(8):
        b, r = divmod(c, 4)
        sl = slice(r * DSL, (r + 1) * DSL)
        m = dict(shared)
        m["x_tok"] = np.ascontiguousarray(X[b, r * T:(r + 1) * T, :])
        m["convw"] = np.ascontiguousarray(cw[sl])
        m["convb"] = np.ascontiguousarray(cb[sl].reshape(NKD, 128).T)
        m["xprojw"] = np.ascontiguousarray(xpw[:, sl])
        m["dtprojw"] = np.ascontiguousarray(dpw[sl])
        m["dtbias"] = np.ascontiguousarray(dtb[sl].reshape(NKD, 128).T)
        m["alog"] = np.ascontiguousarray(al[sl])
        m["dpar"] = np.ascontiguousarray(dp[sl].reshape(NKD, 128).T)
        in_maps.append(m)
    return in_maps


# version 02
# speedup vs baseline: 1.0085x; 1.0085x over previous
import numpy as np
import ml_dtypes
import concourse.bass as bass
import concourse.mybir as mybir
from concourse.bass_utils import run_bass_kernel_spmd

F32 = mybir.dt.float32
BF16 = mybir.dt.bfloat16
AF = mybir.ActivationFunctionType
ALU = mybir.AluOpType
SP = mybir.EngineType.SP
AX = mybir.AxisListType.X

H = 4096; DIN = 8192; NST = 16; KCV = 4; R = 256
B = 2; L = 4096; T = 1024; DSL = 2048; E2 = 2 * DIN
NE = 8; INT = 8192; LAT = 2048
EPS_LN = 1e-5; EPS_RMS = 1e-6
NTC = 8; LTC = 32; NKH = 32; NKD = 16
RG = [[0, 1, 2, 3], [4, 5, 6, 7]]
ts = bass.ts

EV_INP, EV_GATE, EV_FC1, EV_EXP, EV_UP, EV_END = 0, 256, 264, 296, 328, 456


def build_nc(debug_phase=99, race_detect=True):
    nc = bass.Bass("TRN2", target_bir_lowering=False, detect_race_conditions=race_detect)

    def I(name, shape):
        return nc.dram_tensor(name, shape, F32, kind="ExternalInput").ap()
    x_tok = I("x_tok", [T, H]); w_in = I("w_in", [32, 128, E2])
    convw = I("convw", [DSL, KCV]); convb = I("convb", [128, NKD])
    xprojw = I("xprojw", [R + 2 * NST, DSL]); dtprojw = I("dtprojw", [DSL, R])
    dtbias = I("dtbias", [128, NKD]); alog = I("alog", [DSL, NST]); dpar = I("dpar", [128, NKD])
    outw = I("outw", [H, DIN])
    dtlnw = I("dtlnw", [1, R]); blnw = I("blnw", [1, NST]); clnw = I("clnw", [1, NST])
    gatew = I("gatew", [128, NKH * NE]); upw = I("upw", [16, 128, 2 * INT]); downw = I("downw", [32, 128, INT // 128 * 128])
    fc1w = I("fc1w", [4, 128, NKH * 512]); fc2w = I("fc2w", [32, 128, LAT]); expw = I("expw", [4, 128, 16 * 512])
    lngamma = I("lngamma", [1, H]); lnbeta = I("lnbeta", [1, H])
    identf = I("identf", [128, 128])

    out_tok = nc.dram_tensor("out_tok", [T, H], F32, kind="ExternalOutput").ap()
    dbg_out = None
    if debug_phase < 99:
        shp = {1: ([DIN, T], BF16), 2: ([DSL, L], BF16), 3: ([4 * DSL, T], BF16),
               4: ([T, H], F32), 10: ([128, NKH * T], BF16), 11: ([DIN, T], BF16),
               21: ([DSL, L], BF16), 22: ([L, R + 2 * NST], F32)}[debug_phase]
        dbg_out = nc.dram_tensor("dbg_out", shp[0], shp[1], kind="ExternalOutput").ap()

    D = nc.dram_tensor
    ccA_in = D("ccA_in", [DIN, T], BF16, kind="Internal").ap()
    ccA_out = D("ccA_out", [4 * DIN, T], BF16, kind="Internal").ap()
    ccB_in = D("ccB_in", [4 * DSL, T], BF16, kind="Internal").ap()
    ccB_out = D("ccB_out", [16 * DSL, T], BF16, kind="Internal").ap()
    ccC_in = D("ccC_in", [L, R + 2 * NST], F32, kind="Internal").ap()
    ccC_out = D("ccC_out", [L, R + 2 * NST], F32, kind="Internal").ap()
    zbufd = D("zbufd", [DIN, T], BF16, kind="Internal").ap()
    hactd = D("hactd", [INT, T], BF16, kind="Internal").ap()
    moed = D("moed", [LAT, T], BF16, kind="Internal").ap()
    routd = D("routd", [H, T], BF16, kind="Internal").ap()
    dtspd = D("dtspd", [DSL, L], BF16, kind="Internal").ap()
    xcspd = D("xcspd", [DSL, L], BF16, kind="Internal").ap()
    ffnTd = D("ffnTd", [T, H], BF16, kind="Internal").ap()
    xplusd = D("xplusd", [T, H], F32, kind="Internal").ap()
    browd = D("browd", [1, NST * L], BF16, kind="Internal").ap()
    crowd = D("crowd", [1, NST * L], BF16, kind="Internal").ap()
    srowd = D("srowd", [1, T], BF16, kind="Internal").ap()

    ps = nc.alloc_psum_tensor([128, 8, 512], F32)

    AL = nc.alloc_sbuf_tensor
    s_identf = AL([128, 128], F32)
    s_A = AL([128, NKD, NST], F32)
    s_convw = AL([128, NKD, KCV], F32)
    s_convb = AL([128, NKD], F32)
    s_dtbias = AL([128, NKD], F32)
    s_dpar = AL([128, NKD], F32)
    s_hstate = AL([128, NKD * NST], F32)
    s_gamr = AL([128, R + 2 * NST], F32)
    s_srow = AL([1, T], BF16)
    gsb = AL([128, NKH, NE], BF16)
    base0 = nc.sbuf_base

    xnf = AL([128, NKH, T], BF16)
    wslab = AL([128, 2, NKH, 512], BF16)
    ovl0 = nc.sbuf_base
    xt = AL([128, 2, H], F32)
    xnstage = AL([128, 2, H], F32)
    lncols = AL([128, 16], F32)
    p0_end = nc.sbuf_base
    nc.sbuf_base = ovl0
    evst = AL([128, 2, 4, 1024], BF16)
    xlat = AL([128, 16, T], BF16)
    rlst = AL([128, 2, 512], BF16)
    s_rep = AL([128, T], BF16)
    logit = AL([128, NE], F32)
    lgm = AL([128, NE], F32)
    l_m = AL([128, 4], F32)
    scols = AL([128, NTC], F32)
    p1_end = nc.sbuf_base

    nc.sbuf_base = base0
    xdb = AL([128, NKD, L], BF16)
    cacc = AL([128, L], F32)
    cvst = AL([128, L], BF16)
    wxa = AL([128, DSL], F32)
    wxb = AL([128, DSL], F32)
    wxc = AL([32, DSL], F32)
    wxT = AL([128, NKD, R + 2 * NST], BF16)
    ssmst = AL([128, 8, R + 2 * NST], F32)
    p2a_end = nc.sbuf_base

    nc.sbuf_base = base0
    ssm2 = AL([128, LTC, R + 2 * NST], F32)
    sqst = AL([128, R + 2 * NST], F32)
    rmcols = AL([128, 8], F32)
    dtr32 = AL([128, 4, R], F32)
    bnt = AL([128, 4, NST], F32)
    cnt = AL([128, 4, NST], F32)
    btst = AL([16, LTC, 128], BF16)
    ctst = AL([16, LTC, 128], BF16)
    dtrT = AL([128, 2, L], BF16)
    wdtnat = AL([128, NKD, R], F32)
    wdtT = AL([128, 2, DSL], BF16)
    moesb = AL([128, 16, T], BF16)
    fcsl = AL([128, 2, 16, 128], BF16)
    fc2st = AL([128, 2, 512], BF16)
    spb = AL([128, 2, DSL], F32)
    spa = AL([128, 2, DSL], F32)
    dtst = AL([128, 2, DSL], BF16)
    p2b_end = nc.sbuf_base

    nc.sbuf_base = base0
    brep = AL([128, NST, T], BF16)
    crep = AL([128, NST, T], BF16)
    dtq = AL([128, 2, T], BF16)
    xcq = AL([128, 2, T], BF16)
    dtxq = AL([128, T], BF16)
    abuf = AL([128, 3, T], BF16)
    bbuf = AL([128, T], BF16)
    hbuf = AL([128, T], BF16)
    hcb = AL([128, 2, T], BF16)
    yac = AL([128, T], F32)
    y2t = AL([128, T], F32)
    hacth = AL([128, 64, 512], BF16)
    dwsl = AL([128, 2, 64, 128], BF16)
    routt = AL([128, 3, 512], BF16)
    f4st = AL([128, 2, 512], F32)
    ftst = AL([128, 2, 4, 128], BF16)
    p34_end = nc.sbuf_base

    nc.sbuf_base = base0
    gbr = AL([128, 2, H], F32)
    fft = AL([128, 2, H], BF16)
    xrt = AL([128, 2, H], F32)
    xpt = AL([128, 2, H], F32)
    sqxt = AL([128, H], F32)
    xpcols = AL([128, 8], F32)
    p4b_end = nc.sbuf_base

    nc.sbuf_base = base0
    ybuf = AL([128, 64, T], BF16)
    wosl = AL([128, 2, 2048], F32)
    wline = AL([128, 64, 128], BF16)
    zkt = AL([128, 2, T], BF16)
    zst = AL([128, 2, T], BF16)
    xpcs = AL([128, 2, 8, 128], F32)
    ost = AL([128, 2, 8, 128], F32)
    a5st = AL([128, 1024], F32)
    p5_end = nc.sbuf_base

    LIMIT = 229344
    for nm, e in [("p0", p0_end), ("p1", p1_end), ("p2a", p2a_end), ("p2b", p2b_end),
                  ("p34", p34_end), ("p4b", p4b_end), ("p5", p5_end)]:
        assert e <= LIMIT, f"SBUF overflow {nm}: {e} > {LIMIT}"
    nc.sbuf_base = max(p0_end, p1_end, p2a_end, p2b_end, p34_end, p4b_end, p5_end)

    names = """q_misc q_x c_sq0 c_lnr c_ln c_tr0 c_xnf q_w0 q_w1 q_gw c_mm1 c_ev1 q_exp
    c_scol c_str c_srow q_srep c_relu q_cca q_xd c_tap c_silu c_cvb
    q_wx c_trwx c_wxc c_trwd c_wdtc c_mm2 c_ev2 q_ssm q_ccc q_ssml c_sq2 c_rms
    c_bct c_bcc c_dtt c_dtrc q_brow q_moe2 q_fcs c_mmf c_evf q_rout c_mm3 c_sp0 c_spa c_sp
    q_dtsp q_xcsp q_bc q_sc c_dtx c_a c_b c_h c_hc c_y c_fin q_yexp q_ccb
    q_hact q_dwn q_routt c_mm4 c_ev4 c_tr4 c_ev4b q_ffnT
    q_xp_in c_sqx c_xp q_xp_out q_y q_z c_zs c_gate q_wo0 q_wo1 c_tr5 c_ev5 c_mm5
    c_ev5b c_tr5b q_xpc c_asm q_out
    c_veps c_rstd c_red3 c_vep3 c_rst3 c_vepx c_rstx ch_v ch_s ch_p""".split()
    S = {n: nc.alloc_semaphore(n) for n in names}
    g = lambda n: S[n]

    LD = {}

    class DQ:
        def __init__(self, eng, semname):
            self.eng, self.sem, self.n = eng, S[semname], 0

        def dma(self, name, out, in_, **kw):
            if self.n >= 1:
                self.eng.wait_ge(self.sem, 16 * self.n)
            self.n += 1
            r = self.eng.dma_start(out=out, in_=in_, **kw).then_inc(self.sem, 16)
            if name is not None:
                LD[name] = (self.sem, 16 * self.n)
            return r

        def flush(self):
            if self.n >= 1:
                self.eng.wait_ge(self.sem, 16 * self.n)

    def wld(eng, name):
        sem, cnt = LD[name]
        eng.wait_ge(sem, cnt)

    class Chain:
        def __init__(self, semname):
            self.sem, self.n, self.last = S[semname], 0, None

        def bump(self, ins):
            ins.then_inc(self.sem, 1)
            self.n += 1
            self.last = (self.sem, self.n)
            return ins

        def mark(self, semname, val):
            self.last = (S[semname], val)

        def join(self, eng):
            if self.last is not None:
                eng.wait_ge(*self.last)

    chv, chs, chp = Chain("ch_v"), Chain("ch_s"), Chain("ch_p")

    dwn_sched = [(th, hc) for th in range(2) for hc in range(32)]

    with nc.Block() as blk:

        @blk.sync
        def _(sy):
            pid = nc.partition_id([SP])
            myg = pid % 4
            dq_misc = DQ(sy, "q_misc"); dq_x = DQ(sy, "q_x"); dq_exp = DQ(sy, "q_exp")
            dq_srep = DQ(sy, "q_srep"); dq_xd = DQ(sy, "q_xd"); dq_ssm = DQ(sy, "q_ssm")
            dq_xcsp = DQ(sy, "q_xcsp"); dq_ssml = DQ(sy, "q_ssml"); dq_moe2 = DQ(sy, "q_moe2")
            dq_brow = DQ(sy, "q_brow"); dq_rout = DQ(sy, "q_rout"); dq_dtsp = DQ(sy, "q_dtsp")
            dq_bc = DQ(sy, "q_bc"); dq_sc = DQ(sy, "q_sc"); dq_hact = DQ(sy, "q_hact")
            dq_routt = DQ(sy, "q_routt"); dq_ffnT = DQ(sy, "q_ffnT"); dq_xpin = DQ(sy, "q_xp_in")
            dq_xpout = DQ(sy, "q_xp_out"); dq_y = DQ(sy, "q_y"); dq_z = DQ(sy, "q_z")
            dq_wo = [DQ(sy, "q_wo0"), DQ(sy, "q_wo1")]
            dq_xpc = DQ(sy, "q_xpc"); dq_out = DQ(sy, "q_out")

            dq_misc.dma(None, out=s_identf[:], in_=identf)
            dq_misc.dma(None, out=s_convw[:, :, :], in_=convw.rearrange("(k p) j -> p k j", p=128))
            dq_misc.dma(None, out=s_convb[:, :], in_=convb)
            dq_misc.dma(None, out=s_dtbias[:, :], in_=dtbias)
            dq_misc.dma(None, out=s_dpar[:, :], in_=dpar)
            dq_misc.dma(None, out=s_A[:, :, :], in_=alog.rearrange("(k p) n -> p k n", p=128))
            dq_misc.dma(None, out=s_gamr[:, 0:R], in_=dtlnw.partition_broadcast(128))
            dq_misc.dma(None, out=s_gamr[:, R:R + NST], in_=blnw.partition_broadcast(128))
            dq_misc.dma("misc", out=s_gamr[:, R + NST:], in_=clnw.partition_broadcast(128))
            for tc in range(NTC):
                if tc >= 2:
                    sy.wait_ge(g("c_ln"), tc - 1)
                dq_x.dma(f"x{tc}", out=xt[:, tc % 2, :], in_=x_tok[ts(tc, 128), :])
            if debug_phase == 10:
                sy.wait_ge(g("c_xnf"), 256)
                dq_out.dma(None, out=dbg_out, in_=xnf[:, :, :].rearrange("p a b -> p (a b)"))
                dq_out.flush()
                return
            for og in range(32):
                sy.wait_ge(g("c_ev1"), EV_INP + 8 * (og + 1))
                dst, ro = (ccA_in, og * 512) if og < 16 else (zbufd, (og - 16) * 512)
                dq_exp.dma(f"exp{og}", out=dst[ro:ro + 512, :].rearrange("(j p) t -> p j t", p=128),
                           in_=evst[:, og % 2, :, :])
            if debug_phase == 11:
                dq_exp.flush()
                dq_out.dma(None, out=dbg_out, in_=zbufd)
                dq_out.flush()
                return
            sy.wait_ge(g("c_srow"), NTC)
            dq_srep.dma(None, out=srowd, in_=s_srow[:, :])
            dq_srep.dma("srep", out=s_rep[:, :], in_=srowd.partition_broadcast(128))
            for og in range(4):
                sy.wait_ge(g("c_ev1"), EV_EXP + 8 * (og + 1))
                dq_exp.dma(f"expm{og}", out=moed[og * 512:(og + 1) * 512, :].rearrange("(j p) t -> p j t", p=128),
                           in_=evst[:, og % 2, :, :])
            for og in range(16):
                sy.wait_ge(g("c_ev1"), EV_UP + 8 * (og + 1))
                dq_exp.dma(f"expu{og}", out=hactd[og * 512:(og + 1) * 512, :].rearrange("(j p) t -> p j t", p=128),
                           in_=evst[:, og % 2, :, :])
            if debug_phase == 1:
                dq_exp.flush()
                sy.wait_ge(g("q_cca"), 1)
                dq_out.dma(None, out=dbg_out, in_=zbufd)
                dq_out.flush()
                return

            sy.wait_ge(g("q_cca"), 1)
            sy.wait_ge(g("c_mm1"), EV_END)
            dq_exp.flush()
            for rr in range(4):
                srcx = ccA_out[rr * DIN:(rr + 1) * DIN, :].rearrange("(gg kd p) t -> gg p kd t", gg=4, p=128)[ts(myg, 1), :, :, :]
                dq_xd.dma("xd" if rr == 3 else None,
                          out=xdb[:, :, rr * 1024:(rr + 1) * 1024], in_=srcx)
            for gse in range(4):
                sy.wait_ge(g("c_ev2"), 8 * (gse + 1))
                dq_ssm.dma(f"ssm{gse}", out=ccC_in[gse * 1024:(gse + 1) * 1024, :].rearrange("(tc p) e -> p tc e", p=128),
                           in_=ssmst[:, :, :])
            sy.wait_ge(g("c_mm2"), LTC)
            dq_xcsp.dma("xcsp", out=xcspd.rearrange("(k p) t -> p k t", p=128), in_=xdb[:, :, :])
            if debug_phase == 21:
                sy.wait_ge(g("q_xcsp"), 16)
                dq_out.dma(None, out=dbg_out, in_=xcspd)
                dq_out.flush()
                return
            sy.wait_ge(g("q_xcsp"), 16)
            dq_moe2.dma("moe2", out=moesb[:, :, :], in_=moed.rearrange("(k p) t -> p k t", p=128))
            sy.wait_ge(g("q_ccc"), 1)
            if debug_phase == 22:
                dq_out.dma(None, out=dbg_out, in_=ccC_out)
                dq_out.flush()
                return
            dq_ssml.dma("ssml", out=ssm2[:, :, :], in_=ccC_out.rearrange("(tc p) e -> p tc e", p=128))
            for i in range(64):
                hc, th = divmod(i, 2)
                sy.wait_ge(g("c_evf"), i + 1)
                dq_rout.dma(f"rout{i}", out=routd[ts(hc, 128), ts(th, 512)], in_=fc2st[:, i % 2, :])
            sy.wait_ge(g("c_bcc"), 2 * LTC)
            dq_brow.dma(None, out=browd.rearrange("one (n tc j) -> one n tc j", n=NST, tc=LTC),
                        in_=btst[:, :, :])
            dq_brow.dma("brow", out=crowd.rearrange("one (n tc j) -> one n tc j", n=NST, tc=LTC),
                        in_=ctst[:, :, :])
            for i in range(2 * NKD):
                kd, hf = divmod(i, 2)
                sy.wait_ge(g("c_sp"), i + 1)
                dq_dtsp.dma(f"dtsp{i}", out=dtspd[ts(kd, 128), ts(hf, 2048)], in_=dtst[:, i % 2, :])
            if debug_phase == 2:
                dq_dtsp.flush()
                dq_out.dma(None, out=dbg_out, in_=dtspd)
                dq_out.flush()
                return

            dq_dtsp.flush()
            dq_brow.flush()
            dq_rout.flush()
            dq_hact.dma("hact0", out=hacth[:, :, :],
                        in_=hactd.rearrange("(k p) t -> p k t", p=128)[:, :, 0:512])

            def p4_interleave(i):
                if i < 64:
                    j = i
                    th4, hc4 = dwn_sched[j]
                    if j >= 3:
                        sy.wait_ge(g("c_ev4"), j - 2)
                    dq_routt.dma(f"routt{j}", out=routt[:, j % 3, :], in_=routd[ts(hc4, 128), ts(th4, 512)])
                je = i - 4
                if 0 <= je < 64:
                    th4, hc4 = dwn_sched[je]
                    sy.wait_ge(g("c_ev4b"), 4 * (je + 1))
                    dq_ffnT.dma(f"ffnT{je}",
                                out=ffnTd[th4 * 512:(th4 + 1) * 512, ts(hc4, 128)].rearrange("(sub p) c -> p sub c", p=128),
                                in_=ftst[:, je % 2, :, :])
                if i == 34:
                    sy.wait_ge(g("c_mm4"), 32)
                    dq_hact.dma("hact1", out=hacth[:, :, :],
                                in_=hactd.rearrange("(k p) t -> p k t", p=128)[:, :, 512:1024])

            for q in range(4):
                for n in range(NST):
                    if q > 0:
                        sy.wait_ge(g("c_b"), q * 256)
                        sy.wait_ge(g("c_hc"), q * 240)
                    dq_bc.dma(None, out=brep[:, n, :],
                              in_=browd[0:1, n * L + q * T:n * L + (q + 1) * T].partition_broadcast(128))
                    dq_bc.dma(f"bc{q}_{n}", out=crep[:, n, :],
                              in_=crowd[0:1, n * L + q * T:n * L + (q + 1) * T].partition_broadcast(128))
                for k in range(NKD):
                    i = q * NKD + k
                    if i >= 2:
                        sy.wait_ge(g("c_fin"), i - 1)
                    dq_sc.dma(None, out=dtq[:, i % 2, :], in_=dtspd[ts(k, 128), ts(q, T)])
                    dq_sc.dma(f"sc{i}", out=xcq[:, i % 2, :], in_=xcspd[ts(k, 128), ts(q, T)])
                    p4_interleave(i)
            for je in range(60, 64):
                th4, hc4 = dwn_sched[je]
                sy.wait_ge(g("c_ev4b"), 4 * (je + 1))
                dq_ffnT.dma(f"ffnT{je}",
                            out=ffnTd[th4 * 512:(th4 + 1) * 512, ts(hc4, 128)].rearrange("(sub p) c -> p sub c", p=128),
                            in_=ftst[:, je % 2, :, :])
            if debug_phase == 3:
                sy.wait_ge(g("q_yexp"), 16 * 64)
                dq_out.dma(None, out=dbg_out, in_=ccB_in)
                dq_out.flush()
                return

            dq_ffnT.flush()
            sy.wait_ge(g("c_fin"), 64)
            sy.wait_ge(g("q_yexp"), 16 * 64)
            dq_xpin.dma(None, out=gbr[:, 0, :], in_=lngamma.partition_broadcast(128))
            dq_xpin.dma("xpg", out=gbr[:, 1, :], in_=lnbeta.partition_broadcast(128))
            for tc in range(NTC):
                if tc >= 2:
                    sy.wait_ge(g("c_xp"), tc - 1)
                dq_xpin.dma(f"fft{tc}", out=fft[:, tc % 2, :], in_=ffnTd[ts(tc, 128), :])
                dq_xpin.dma(f"xrt{tc}", out=xrt[:, tc % 2, :], in_=x_tok[ts(tc, 128), :])
                if tc >= 2:
                    sy.wait_ge(g("c_xp"), tc - 1)
                    dq_xpout.dma(f"xpo{tc - 2}", out=xplusd[ts(tc - 2, 128), :], in_=xpt[:, tc % 2, :])
            for tc in range(NTC - 2, NTC):
                sy.wait_ge(g("c_xp"), tc + 1)
                dq_xpout.dma(f"xpo{tc}", out=xplusd[ts(tc, 128), :], in_=xpt[:, tc % 2, :])
            if debug_phase == 4:
                dq_xpout.flush()
                dq_out.dma(None, out=dbg_out, in_=xplusd)
                dq_out.flush()
                return

            sy.wait_ge(g("q_ccb"), 1)
            dq_xpout.flush()
            sy.wait_ge(g("c_mm4"), 64)
            for rr in range(4):
                srcy = ccB_out[rr * 4 * DSL:(rr + 1) * 4 * DSL, :].rearrange("(j kl p) t -> j p kl t", j=4, p=128)[ts(myg, 1), :, :, :]
                dq_y.dma("y" if rr == 3 else None,
                         out=ybuf[:, rr * 16:(rr + 1) * 16, :], in_=srcy)
            for i in range(64):
                if i >= 2:
                    sy.wait_ge(g("c_zs"), i - 1)
                dq_z.dma(f"z{i}", out=zkt[:, i % 2, :], in_=zbufd[ts(i, 128), :])
            def wo_load(i):
                hc4, qf = divmod(i, 4)
                if i >= 2:
                    sy.wait_ge(g("c_tr5"), (i - 1) * 16)
                dq_wo[i % 2].dma(f"wo{i}", out=wosl[:, i % 2, :], in_=outw[ts(hc4, 128), ts(qf, 2048)])
            wo_load(0)
            wo_load(1)
            for hc in range(32):
                if hc >= 2:
                    sy.wait_ge(g("c_asm"), (hc - 1) * 8)
                dq_xpc.dma(f"xpc{hc}", out=xpcs[:, hc % 2, :, :],
                           in_=xplusd.rearrange("(tc p) h -> p tc h", p=128)[:, :, ts(hc, 128)])
                for qq in range(4 * hc + 2, min(4 * hc + 6, 128)):
                    wo_load(qq)
                sy.wait_ge(g("c_asm"), (hc + 1) * 8)
                dq_out.dma(f"out{hc}", out=out_tok.rearrange("(tc p) h -> p tc h", p=128)[:, :, ts(hc, 128)],
                           in_=ost[:, hc % 2, :, :])
            dq_out.flush()

        @blk.gpsimd
        def _(gp):
            dq_gw = DQ(gp, "q_gw"); dq_w = [DQ(gp, "q_w0"), DQ(gp, "q_w1")]
            dq_wx = DQ(gp, "q_wx"); dq_fcs = DQ(gp, "q_fcs"); dq_dwn = DQ(gp, "q_dwn")
            dq_yexp = DQ(gp, "q_yexp")

            def slab_src(w, og, width=512):
                return w.rearrange("(k p) c -> p k c", p=128)[:, :, og * width:(og + 1) * width]
            if debug_phase == 10:
                return
            dq_gw.dma("gatew", out=gsb[:, :, :], in_=gatew.rearrange("(k p) e -> p k e", p=128))
            for og in range(32):
                if og >= 2:
                    gp.wait_ge(g("c_mm1"), EV_INP + 8 * (og - 1))
                dq_w[og % 2].dma(f"win{og}", out=wslab[:, og % 2, :, :], in_=w_in[og, :, :].rearrange("p (k c) -> p k c", c=512))
            if debug_phase == 11:
                return
            for og in range(4):
                gp.wait_ge(g("c_mm1"), (EV_INP + 256) if og < 2 else (EV_FC1 + 8 * (og - 1)))
                dq_w[og % 2].dma(f"fc1w{og}", out=wslab[:, og % 2, :, :], in_=fc1w[og, :, :].rearrange("p (k c) -> p k c", c=512))
            for og in range(4):
                gp.wait_ge(g("c_mm1"), (EV_FC1 + 32) if og < 2 else (EV_EXP + 8 * (og - 1)))
                dq_w[og % 2].dma(f"expw{og}", out=wslab[:, og % 2, 0:16, :], in_=expw[og, :, :].rearrange("p (k c) -> p k c", c=512))
            for og in range(16):
                gp.wait_ge(g("c_mm1"), (EV_EXP + 32) if og < 2 else (EV_UP + 8 * (og - 1)))
                dq_w[og % 2].dma(f"upw{og}", out=wslab[:, og % 2, :, :], in_=upw[og, :, :].rearrange("p (k c) -> p k c", c=512))
            wld(gp, "exp15")
            gp.collective_compute("AllGather", ALU.bypass, replica_groups=RG,
                                  ins=[ccA_in.opt()], outs=[ccA_out.opt()]).then_inc(g("q_cca"))
            if debug_phase == 1:
                return
            gp.wait_ge(g("c_mm1"), EV_END)
            wld(gp, "expu15")
            dq_wx.dma(None, out=wxa[:, :], in_=xprojw[0:128, :])
            dq_wx.dma(None, out=wxb[:, :], in_=xprojw[128:256, :])
            dq_wx.dma("wxc3", out=wxc[:, :], in_=xprojw[256:288, :])
            gp.wait_ge(g("q_xcsp"), 16)
            dq_wx.dma("wdtn", out=wdtnat[:, :, :], in_=dtprojw.rearrange("(k p) r -> p k r", p=128))
            if debug_phase == 21:
                return
            wld(gp, "ssm3")
            gp.collective_compute("AllReduce", ALU.add, replica_groups=RG,
                                  ins=[ccC_in.opt()], outs=[ccC_out.opt()]).then_inc(g("q_ccc"))
            if debug_phase == 22:
                return
            for hc in range(32):
                if hc >= 2:
                    gp.wait_ge(g("c_mmf"), 2 * (hc - 1))
                dq_fcs.dma(f"fcs{hc}", out=fcsl[:, hc % 2, :, :],
                           in_=fc2w[hc, :, :].rearrange("p (k c) -> p k c", c=128))
            if debug_phase in (2, 21, 22):
                return
            wld(gp, "dtsp31")
            wld(gp, "rout63")
            dwn_issued = 0
            for q in range(4):
                for k in range(NKD):
                    i = q * NKD + k
                    while dwn_issued < 64 and dwn_issued * 64 <= i * 64 + 32:
                        jj = dwn_issued
                        th4, hc4 = dwn_sched[jj]
                        if jj >= 2:
                            gp.wait_ge(g("c_mm4"), jj - 1)
                        dq_dwn.dma(f"dwn{jj}", out=dwsl[:, jj % 2, :, :],
                                   in_=downw[hc4, :, :].rearrange("p (k c) -> p k c", c=128))
                        dwn_issued += 1
                    for n in range(1, NST):
                        gp.wait_ge(g("c_hc"), i * 15 + n)
                        if n >= 2:
                            chp.join(gp)
                        gp.tensor_tensor(yac[:, :], yac[:, :], hcb[:, n % 2, :], ALU.add).then_inc(g("c_y"), 1)
                        chp.mark("c_y", i * 15 + n)
                    gp.wait_ge(g("c_fin"), i + 1)
                    dq_yexp.dma(f"yexp{i}", out=ccB_in[q * DSL + k * 128:q * DSL + (k + 1) * 128, :],
                                in_=y2t[:, :])
            while dwn_issued < 64:
                jj = dwn_issued
                th4, hc4 = dwn_sched[jj]
                if jj >= 2:
                    gp.wait_ge(g("c_mm4"), jj - 1)
                dq_dwn.dma(f"dwn{jj}", out=dwsl[:, jj % 2, :, :],
                           in_=downw[hc4, :, :].rearrange("p (k c) -> p k c", c=128))
                dwn_issued += 1
            dq_yexp.flush()
            gp.collective_compute("AllGather", ALU.bypass, replica_groups=RG,
                                  ins=[ccB_in.opt()], outs=[ccB_out.opt()]).then_inc(g("q_ccb"))

        @blk.tensor
        def _(t):
            wld(t, "misc")
            for tc in range(NTC):
                t.wait_ge(g("c_ln"), tc + 1)
                for hc in range(NKH):
                    i = tc * NKH + hc
                    if i >= 4:
                        t.wait_ge(g("c_xnf"), i - 3)
                    t.transpose(ps[:, i % 4, 0:128], xnstage[:, tc % 2, ts(hc, 128)], s_identf[:]).then_inc(g("c_tr0"), 1)
            def mm_slabs(n_og, nk, ev_base, wname, rhs_sb):
                for og in range(n_og):
                    wld(t, wname.format(og))
                    t.wait_ge(g("c_ev1"), ev_base + 8 * og)
                    for j in range(4):
                        for th in range(2):
                            last = None
                            for k in range(nk):
                                last = t.matmul(ps[:, j * 2 + th, :],
                                                wslab[:, og % 2, k, ts(j, 128)],
                                                rhs_sb[:, k, ts(th, 512)],
                                                start=(k == 0), stop=(k == nk - 1))
                            last.then_inc(g("c_mm1"), 1)
            if debug_phase == 10:
                return
            t.wait_ge(g("c_xnf"), 256)
            mm_slabs(32, NKH, EV_INP, "win{}", xnf)
            if debug_phase == 11:
                return
            wld(t, "gatew")
            for tc in range(NTC):
                t.wait_ge(g("c_ev1"), EV_GATE + tc)
                last = None
                for k in range(NKH):
                    last = t.matmul(ps[:, 6, 0:NE], xnf[:, k, ts(tc, 128)], gsb[:, k, :],
                                    start=(k == 0), stop=(k == NKH - 1))
                last.then_inc(g("c_mm1"), 1)
            for tc in range(NTC):
                t.wait_ge(g("c_scol"), tc + 1)
                if tc >= 1:
                    t.wait_ge(g("c_srow"), tc)
                t.transpose(ps[0:1, 7, 0:128], scols[:, tc:tc + 1], s_identf[:]).then_inc(g("c_str"), 1)
            t.wait_ge(g("c_srow"), NTC)
            mm_slabs(4, NKH, EV_FC1, "fc1w{}", xnf)
            t.wait_ge(g("c_ev1"), EV_FC1 + 32)
            mm_slabs(4, 16, EV_EXP, "expw{}", xlat)
            mm_slabs(16, NKH, EV_UP, "upw{}", xnf)
            if debug_phase == 1:
                return
            wld(t, "wxc3")
            for kd in range(NKD):
                if kd >= 1:
                    t.wait_ge(g("c_wxc"), 3 * kd)
                t.transpose(ps[:, 4, 0:128], wxa[:, ts(kd, 128)], s_identf[:])
                t.transpose(ps[:, 5, 0:128], wxb[:, ts(kd, 128)], s_identf[:])
                t.transpose(ps[:, 6, 0:32], wxc[0:32, ts(kd, 128)], s_identf[0:32, 0:32]).then_inc(g("c_trwx"), 3)
            wld(t, "xd")
            t.wait_ge(g("c_cvb"), NKD)
            t.wait_ge(g("c_wxc"), 3 * NKD)
            for tc in range(LTC):
                if tc >= 4:
                    t.wait_ge(g("c_ev2"), tc - 3)
                last = None
                for kd in range(NKD):
                    last = t.matmul(ps[:, tc % 4, 0:R + 2 * NST], xdb[:, kd, ts(tc, 128)], wxT[:, kd, :],
                                    start=(kd == 0), stop=(kd == NKD - 1))
                last.then_inc(g("c_mm2"), 1)
            if debug_phase in (21, 22):
                return
            t.wait_ge(g("c_ev2"), LTC)
            wld(t, "moe2")
            for i in range(64):
                hc, th = divmod(i, 2)
                wld(t, f"fcs{hc}")
                if i >= 4:
                    t.wait_ge(g("c_evf"), i - 3)
                last = None
                for k in range(16):
                    last = t.matmul(ps[:, i % 4, :], fcsl[:, hc % 2, k, :], moesb[:, k, ts(th, 512)],
                                    start=(k == 0), stop=(k == 15))
                last.then_inc(g("c_mmf"), 1)
            wld(t, "wdtn")
            for kd in range(NKD):
                for rc in range(2):
                    i = kd * 2 + rc
                    if i >= 1:
                        t.wait_ge(g("c_wdtc"), i)
                    t.transpose(ps[:, 7, 0:128], wdtnat[:, kd, ts(rc, 128)], s_identf[:]).then_inc(g("c_trwd"), 1)
            for tc in range(LTC):
                t.wait_ge(g("c_rms"), 3 * (tc + 1))
                if tc >= 1:
                    t.wait_ge(g("c_bcc"), 2 * tc)
                t.transpose(ps[0:16, 4, 0:128], bnt[:, tc % 4, :], s_identf[:])
                t.transpose(ps[0:16, 5, 0:128], cnt[:, tc % 4, :], s_identf[:]).then_inc(g("c_bct"), 2)
                for rc in range(2):
                    i = tc * 2 + rc
                    if i >= 1:
                        t.wait_ge(g("c_dtrc"), i)
                    t.transpose(ps[:, 6, 0:128], dtr32[:, tc % 4, ts(rc, 128)], s_identf[:]).then_inc(g("c_dtt"), 1)
            t.wait_ge(g("c_evf"), 64)
            t.wait_ge(g("c_dtrc"), 64)
            t.wait_ge(g("c_wdtc"), 32)
            for i in range(2 * NKD):
                kd, hf = divmod(i, 2)
                if i >= 1:
                    t.wait_ge(g("c_sp0"), i)
                last = None
                for f in range(4):
                    for rc in range(2):
                        last = t.matmul(ps[:, f, :], wdtT[:, rc, ts(kd, 128)],
                                        dtrT[:, rc, hf * 2048 + f * 512:hf * 2048 + (f + 1) * 512],
                                        start=(rc == 0), stop=(rc == 1))
                last.then_inc(g("c_mm3"), 1)
            if debug_phase in (2, 21, 22):
                return
            def p4trs(jt):
                t.wait_ge(g("c_ev4"), jt + 1)
                for sub in range(4):
                    si = jt * 4 + sub
                    if si >= 2:
                        t.wait_ge(g("c_ev4b"), si - 1)
                    t.transpose(ps[:, 6 + si % 2, 0:128], f4st[:, jt % 2, :][:, ts(sub, 128)],
                                s_identf[:]).then_inc(g("c_tr4"), 1)
            for j in range(64):
                if j % 32 == 0:
                    wld(t, f"hact{j // 32}")
                wld(t, f"dwn{j}")
                if j >= 2:
                    t.wait_ge(g("c_ev4"), j - 1)
                pb = (j % 2) * 2
                last = None
                for k in range(64):
                    last = t.matmul(ps[:, pb, :], dwsl[:, j % 2, k, :],
                                    hacth[:, k, :], start=(k == 0), stop=(k == 63))
                last.then_inc(g("c_mm4"), 1)
                if j >= 1:
                    p4trs(j - 1)
            p4trs(63)
            if debug_phase <= 4:
                return
            wld(t, "y")
            t.wait_ge(g("c_gate"), 64)
            for hc in range(32):
                for qf in range(4):
                    ii = hc * 4 + qf
                    wld(t, f"wo{ii}")
                    for k in range(16):
                        i = ii * 16 + k
                        if i >= 2:
                            t.wait_ge(g("c_ev5"), i - 1)
                        t.transpose(ps[:, 4 + i % 2, 0:128], wosl[:, ii % 2, ts(k, 128)],
                                    s_identf[:]).then_inc(g("c_tr5"), 1)
                t.wait_ge(g("c_ev5"), (hc + 1) * 64)
                if hc >= 2:
                    t.wait_ge(g("c_ev5b"), hc - 1)
                pb = (hc % 2) * 2
                for th in range(2):
                    last = None
                    for k in range(64):
                        last = t.matmul(ps[:, pb + th, :], wline[:, k, :], ybuf[:, k, ts(th, 512)],
                                        start=(k == 0), stop=(k == 63))
                    last.then_inc(g("c_mm5"), 1)
                t.wait_ge(g("c_ev5b"), hc + 1)
                for tcc in range(8):
                    i = hc * 8 + tcc
                    if i >= 2:
                        t.wait_ge(g("c_asm"), i - 1)
                    t.transpose(ps[:, 6 + i % 2, 0:128], a5st[:, ts(tcc, 128)],
                                s_identf[:]).then_inc(g("c_tr5b"), 1)

        @blk.scalar
        def _(s):
            def xnf_copies(tcb):
                for hc in range(NKH):
                    i = tcb * NKH + hc
                    s.wait_ge(g("c_tr0"), i + 1)
                    s.copy(xnf[:, hc, ts(tcb, 128)], ps[:, i % 4, 0:128]).then_inc(g("c_xnf"), 1)
            for tc in range(NTC):
                if tc >= 2:
                    xnf_copies(tc - 2)
                wld(s, f"x{tc}")
                if tc >= 1:
                    s.wait_ge(g("c_lnr"), tc)
                if tc >= 2:
                    s.wait_ge(g("c_tr0"), 32 * (tc - 1))
                s.activation(xnstage[:, tc % 2, :], xt[:, tc % 2, :], AF.Square).then_inc(g("c_sq0"), 1)
                s.wait_ge(g("c_veps"), tc + 1)
                chs.bump(s.activation(lncols[:, 7:8], lncols[:, 6:7], AF.Ln))
                chs.join(s)
                s.activation(lncols[:, 7:8], lncols[:, 7:8], AF.Exp, scale=-0.5).then_inc(g("c_rstd"), 1)
            xnf_copies(NTC - 2)
            xnf_copies(NTC - 1)
            if debug_phase == 10 or debug_phase == 11:
                return
            for og in range(16):
                for jt in range(8):
                    idx = og * 8 + jt
                    j, th = divmod(jt, 2)
                    s.wait_ge(g("c_relu"), idx + 1)
                    s.activation(evst[:, og % 2, j, th * 512:(th + 1) * 512], rlst[:, idx % 2, :],
                                 AF.Square).then_inc(g("c_ev1"), 1)
            if debug_phase == 1:
                return
            for k in range(NKD):
                s.wait_ge(g("c_tap"), 4 * (k + 1))
                if k >= 1:
                    s.wait_ge(g("c_cvb"), k)
                s.activation(cvst[:, :], cacc[:, :], AF.Silu).then_inc(g("c_silu"), 1)
            if debug_phase in (21, 22):
                return
            wld(s, "ssml")
            for tc in range(LTC):
                if tc >= 1:
                    s.wait_ge(g("c_red3"), tc)
                s.activation(sqst[:, :], ssm2[:, tc, :], AF.Square).then_inc(g("c_sq2"), 1)
                s.wait_ge(g("c_vep3"), tc + 1)
                chs.bump(s.activation(rmcols[:, 3:4], rmcols[:, 3:4], AF.Ln))
                chs.bump(s.activation(rmcols[:, 4:5], rmcols[:, 4:5], AF.Ln))
                chs.bump(s.activation(rmcols[:, 5:6], rmcols[:, 5:6], AF.Ln))
                chs.join(s)
                s.activation(rmcols[:, 3:4], rmcols[:, 3:4], AF.Exp, scale=-0.5)
                s.activation(rmcols[:, 4:5], rmcols[:, 4:5], AF.Exp, scale=-0.5)
                s.activation(rmcols[:, 5:6], rmcols[:, 5:6], AF.Exp, scale=-0.5).then_inc(g("c_rst3"), 1)
            for i in range(2 * NKD):
                s.wait_ge(g("c_sp0"), i + 1)
                chs.bump(s.activation(spa[:, i % 2, :], spb[:, i % 2, :], AF.Abs))
                chs.join(s)
                chs.bump(s.activation(spa[:, i % 2, :], spa[:, i % 2, :], AF.Exp, scale=-1.0))
                chs.join(s)
                s.activation(spa[:, i % 2, :], spa[:, i % 2, :], AF.Ln, bias=1.0).then_inc(g("c_spa"), 1)
            if debug_phase in (2, 21, 22):
                return
            wld(s, "misc")
            chs.bump(s.activation(s_A[:, :, :], s_A[:, :, :], AF.Exp))
            chs.join(s)
            chs.bump(s.mul(s_A[:, :, :], s_A[:, :, :], -1.0))
            chs.join(s)
            for q in range(4):
                for k in range(NKD):
                    i = q * NKD + k
                    wld(s, f"sc{i}")
                    for n in range(NST):
                        idx = i * NST + n
                        if idx >= 3:
                            s.wait_ge(g("c_h"), idx - 2)
                        s.activation(abuf[:, idx % 3, :], dtq[:, i % 2, :], AF.Exp,
                                     scale=s_A[:, k, n:n + 1]).then_inc(g("c_a"), 1)
                    j = i - 3
                    if 0 <= j < 64:
                        if j >= 2:
                            wld(s, f"ffnT{j - 2}")
                        for sub in range(4):
                            si = j * 4 + sub
                            s.wait_ge(g("c_tr4"), si + 1)
                            s.copy(ftst[:, j % 2, sub, :], ps[:, 6 + si % 2, 0:128]).then_inc(g("c_ev4b"), 1)
            for j in (61, 62, 63):
                wld(s, f"ffnT{j - 2}")
                for sub in range(4):
                    si = j * 4 + sub
                    s.wait_ge(g("c_tr4"), si + 1)
                    s.copy(ftst[:, j % 2, sub, :], ps[:, 6 + si % 2, 0:128]).then_inc(g("c_ev4b"), 1)
            if debug_phase == 3:
                return
            for tc in range(NTC):
                wld(s, f"fft{tc}")
                if tc >= 1:
                    s.wait_ge(g("c_xp"), tc)
                s.activation(sqxt[:, :], fft[:, tc % 2, :], AF.Square).then_inc(g("c_sqx"), 1)
                s.wait_ge(g("c_vepx"), tc + 1)
                chs.bump(s.activation(xpcols[:, 7:8], xpcols[:, 6:7], AF.Ln))
                chs.join(s)
                s.activation(xpcols[:, 7:8], xpcols[:, 7:8], AF.Exp, scale=-0.5).then_inc(g("c_rstx"), 1)
            if debug_phase <= 4:
                return
            for i in range(64):
                wld(s, f"z{i}")
                if i >= 2:
                    s.wait_ge(g("c_gate"), i - 1)
                s.activation(zst[:, i % 2, :], zkt[:, i % 2, :], AF.Silu).then_inc(g("c_zs"), 1)
            for i in range(2048):
                s.wait_ge(g("c_tr5"), i + 1)
                s.copy(wline[:, i % 64, :], ps[:, 4 + i % 2, 0:128]).then_inc(g("c_ev5"), 1)

        @blk.vector
        def _(v):
            for tc in range(NTC):
                wld(v, f"x{tc}")
                chv.bump(v.reduce_sum(lncols[:, 0:1], xt[:, tc % 2, :], axis=AX))
                chv.join(v)
                chv.bump(v.tensor_scalar(lncols[:, 1:2], lncols[:, 0:1], 1.0 / H, None, ALU.mult))
                v.wait_ge(g("c_sq0"), tc + 1)
                v.reduce_sum(lncols[:, 2:3], xnstage[:, tc % 2, :], axis=AX).then_inc(g("c_lnr"), 1)
                chv.mark("c_lnr", tc + 1)
                chv.join(v)
                chv.bump(v.tensor_scalar(lncols[:, 3:4], lncols[:, 2:3], 1.0 / H, None, ALU.mult))
                chv.bump(v.tensor_tensor(lncols[:, 4:5], lncols[:, 1:2], lncols[:, 1:2], ALU.mult))
                chv.join(v)
                chv.bump(v.tensor_tensor(lncols[:, 5:6], lncols[:, 3:4], lncols[:, 4:5], ALU.subtract))
                chv.join(v)
                v.tensor_scalar(lncols[:, 6:7], lncols[:, 5:6], 1.0, EPS_LN, ALU.mult, ALU.add).then_inc(g("c_veps"), 1)
                chv.mark("c_veps", tc + 1)
                if tc >= 2:
                    v.wait_ge(g("c_tr0"), 32 * (tc - 1))
                v.wait_ge(g("c_rstd"), tc + 1)
                v.tensor_scalar(xnstage[:, tc % 2, :], xt[:, tc % 2, :], lncols[:, 1:2], lncols[:, 7:8],
                                ALU.subtract, ALU.mult).then_inc(g("c_ln"), 1)
            if debug_phase == 10:
                return
            for og in range(32):
                if og >= 2:
                    wld(v, f"exp{og - 2}")
                for jt in range(8):
                    j, th = divmod(jt, 2)
                    v.wait_ge(g("c_mm1"), EV_INP + og * 8 + jt + 1)
                    v.tensor_copy(evst[:, og % 2, j, th * 512:(th + 1) * 512], ps[:, jt, :]).then_inc(g("c_ev1"), 1)
            if debug_phase == 11:
                return
            for tc in range(NTC):
                v.wait_ge(g("c_mm1"), EV_GATE + tc + 1)
                v.tensor_copy(logit[:, :], ps[:, 6, 0:NE]).then_inc(g("c_ev1"), 1)
                chv.mark("c_ev1", EV_GATE + tc + 1)
                chv.join(v)
                chv.bump(v.reduce_max(l_m[:, 0:1], logit[:, :], axis=AX))
                chv.join(v)
                chv.bump(v.tensor_scalar(lgm[:, :], logit[:, :], l_m[:, 0:1], -1e30, ALU.is_ge, ALU.mult))
                chv.join(v)
                chv.bump(v.tensor_tensor(lgm[:, :], lgm[:, :], logit[:, :], ALU.add))
                chv.join(v)
                chv.bump(v.reduce_max(l_m[:, 1:2], lgm[:, :], axis=AX))
                chv.join(v)
                v.tensor_tensor(scols[:, tc:tc + 1], l_m[:, 0:1], l_m[:, 1:2], ALU.add).then_inc(g("c_scol"), 1)
            for tc in range(NTC):
                v.wait_ge(g("c_str"), tc + 1)
                v.tensor_copy(s_srow[0:1, ts(tc, 128)], ps[0:1, 7, 0:128]).then_inc(g("c_srow"), 1)
            for og in range(4):
                for jt in range(8):
                    j, th = divmod(jt, 2)
                    v.wait_ge(g("c_mm1"), EV_FC1 + og * 8 + jt + 1)
                    v.tensor_copy(xlat[:, og * 4 + j, th * 512:(th + 1) * 512], ps[:, jt, :]).then_inc(g("c_ev1"), 1)
            wld(v, "srep")
            for og in range(4):
                wld(v, f"exp{30 + og}" if og < 2 else f"expm{og - 2}")
                for jt in range(8):
                    j, th = divmod(jt, 2)
                    v.wait_ge(g("c_mm1"), EV_EXP + og * 8 + jt + 1)
                    v.tensor_tensor(evst[:, og % 2, j, th * 512:(th + 1) * 512], ps[:, jt, :],
                                    s_rep[:, th * 512:(th + 1) * 512], ALU.mult).then_inc(g("c_ev1"), 1)
            for og in range(16):
                wld(v, f"expm{2 + og}" if og < 2 else f"expu{og - 2}")
                for jt in range(8):
                    idx = og * 8 + jt
                    v.wait_ge(g("c_mm1"), EV_UP + idx + 1)
                    if idx >= 2:
                        v.wait_ge(g("c_ev1"), EV_UP + idx - 1)
                    v.tensor_scalar(rlst[:, idx % 2, :], ps[:, jt, :], 0.0, None, ALU.max).then_inc(g("c_relu"), 1)
            if debug_phase == 1:
                return
            for kd in range(NKD):
                v.wait_ge(g("c_trwx"), 3 * (kd + 1))
                v.tensor_copy(wxT[:, kd, 0:128], ps[:, 4, 0:128])
                v.tensor_copy(wxT[:, kd, 128:256], ps[:, 5, 0:128])
                v.tensor_copy(wxT[:, kd, 256:288], ps[:, 6, 0:32]).then_inc(g("c_wxc"), 3)
            wld(v, "xd")
            wld(v, "misc")
            for k in range(NKD):
                v.tensor_scalar(cacc[:, :], xdb[:, k, :], s_convw[:, k, 3:4], s_convb[:, k:k + 1],
                                ALU.mult, ALU.add).then_inc(g("c_tap"), 1)
                chv.mark("c_tap", 4 * k + 1)
                for kk in range(3):
                    o = 3 - kk
                    chv.join(v)
                    v.scalar_tensor_tensor(cacc[:, o:L], xdb[:, k, 0:L - o], s_convw[:, k, kk:kk + 1],
                                           cacc[:, o:L], ALU.mult, ALU.add).then_inc(g("c_tap"), 1)
                    chv.mark("c_tap", 4 * k + 1 + kk + 1)
                v.wait_ge(g("c_silu"), k + 1)
                v.tensor_copy(xdb[:, k, :], cvst[:, :]).then_inc(g("c_cvb"), 1)
            for tc in range(LTC):
                v.wait_ge(g("c_mm2"), tc + 1)
                if tc >= 8:
                    wld(v, f"ssm{tc // 8 - 1}")
                v.tensor_copy(ssmst[:, tc % 8, :], ps[:, tc % 4, 0:R + 2 * NST]).then_inc(g("c_ev2"), 1)
            if debug_phase in (21, 22):
                return
            for i in range(64):
                hc, th = divmod(i, 2)
                v.wait_ge(g("c_mmf"), i + 1)
                if i >= 2:
                    wld(v, f"rout{i - 2}")
                v.tensor_copy(fc2st[:, i % 2, :], ps[:, i % 4, :]).then_inc(g("c_evf"), 1)
            for kd in range(NKD):
                for rc in range(2):
                    i = kd * 2 + rc
                    v.wait_ge(g("c_trwd"), i + 1)
                    v.tensor_copy(wdtT[:, rc, ts(kd, 128)], ps[:, 7, 0:128]).then_inc(g("c_wdtc"), 1)
            def bc_dtr_copies(tcb):
                v.wait_ge(g("c_bct"), 2 * (tcb + 1))
                v.tensor_copy(btst[:, tcb, :], ps[0:16, 4, 0:128])
                v.tensor_copy(ctst[:, tcb, :], ps[0:16, 5, 0:128]).then_inc(g("c_bcc"), 2)
                for rc in range(2):
                    i2 = tcb * 2 + rc
                    v.wait_ge(g("c_dtt"), i2 + 1)
                    v.tensor_copy(dtrT[:, rc, ts(tcb, 128)], ps[:, 6, 0:128]).then_inc(g("c_dtrc"), 1)
            wld(v, "ssml")
            for tc in range(LTC):
                v.wait_ge(g("c_sq2"), tc + 1)
                if tc >= 4:
                    v.wait_ge(g("c_dtt"), 2 * (tc - 3))
                    v.wait_ge(g("c_bct"), 2 * (tc - 3))
                v.reduce_sum(rmcols[:, 0:1], sqst[:, 0:R], axis=AX)
                v.reduce_sum(rmcols[:, 1:2], sqst[:, R:R + NST], axis=AX)
                v.reduce_sum(rmcols[:, 2:3], sqst[:, R + NST:], axis=AX).then_inc(g("c_red3"), 1)
                chv.mark("c_red3", tc + 1)
                chv.join(v)
                v.tensor_scalar(rmcols[:, 3:4], rmcols[:, 0:1], 1.0 / R, EPS_RMS, ALU.mult, ALU.add)
                v.tensor_scalar(rmcols[:, 4:5], rmcols[:, 1:2], 1.0 / NST, EPS_RMS, ALU.mult, ALU.add)
                v.tensor_scalar(rmcols[:, 5:6], rmcols[:, 2:3], 1.0 / NST, EPS_RMS, ALU.mult, ALU.add).then_inc(g("c_vep3"), 1)
                v.wait_ge(g("c_rst3"), tc + 1)
                v.scalar_tensor_tensor(dtr32[:, tc % 4, :], ssm2[:, tc, 0:R], rmcols[:, 3:4],
                                       s_gamr[:, 0:R], ALU.mult, ALU.mult).then_inc(g("c_rms"), 1)
                v.scalar_tensor_tensor(bnt[:, tc % 4, :], ssm2[:, tc, R:R + NST], rmcols[:, 4:5],
                                       s_gamr[:, R:R + NST], ALU.mult, ALU.mult).then_inc(g("c_rms"), 1)
                v.scalar_tensor_tensor(cnt[:, tc % 4, :], ssm2[:, tc, R + NST:], rmcols[:, 5:6],
                                       s_gamr[:, R + NST:], ALU.mult, ALU.mult).then_inc(g("c_rms"), 1)
                if tc >= 1:
                    bc_dtr_copies(tc - 1)
            bc_dtr_copies(LTC - 1)
            for i in range(2 * NKD):
                kd, hf = divmod(i, 2)
                v.wait_ge(g("c_mm3"), i + 1)
                if i >= 2:
                    wld(v, f"dtsp{i - 2}")
                v.tensor_scalar(spb[:, i % 2, :], ps[:, 0:4, :].rearrange("p b f -> p (b f)"),
                                s_dtbias[:, kd:kd + 1], None, ALU.add).then_inc(g("c_sp0"), 1)
                chv.mark("c_sp0", i + 1)
                v.wait_ge(g("c_spa"), i + 1)
                chv.join(v)
                chv.bump(v.tensor_scalar(spb[:, i % 2, :], spb[:, i % 2, :], 0.0, None, ALU.max))
                chv.join(v)
                v.tensor_tensor(dtst[:, i % 2, :], spb[:, i % 2, :], spa[:, i % 2, :], ALU.add).then_inc(g("c_sp"), 1)
            if debug_phase in (2, 21, 22):
                return
            v.memset(s_hstate[:, :], 0.0)
            for q in range(4):
                for k in range(NKD):
                    i = q * NKD + k
                    wld(v, f"sc{i}")
                    v.tensor_tensor(dtxq[:, :], dtq[:, i % 2, :], xcq[:, i % 2, :], ALU.mult).then_inc(g("c_dtx"), 1)
                    chv.mark("c_dtx", i + 1)
                    chv.join(v)
                    for n in range(NST):
                        idx = i * NST + n
                        kn = k * NST + n
                        v.wait_ge(g("c_a"), idx + 1)
                        wld(v, f"bc{q}_{n}")
                        v.tensor_tensor(bbuf[:, :], dtxq[:, :], brep[:, n, :], ALU.mult).then_inc(g("c_b"), 1)
                        chv.mark("c_b", i * NST + n + 1)
                        chv.join(v)
                        v.tensor_tensor_scan(hbuf[:, :], abuf[:, idx % 3, :], bbuf[:, :],
                                             s_hstate[:, kn:kn + 1], ALU.mult, ALU.add).then_inc(g("c_h"), 1)
                        chv.mark("c_h", idx + 1)
                        chv.join(v)
                        v.tensor_copy(s_hstate[:, kn:kn + 1], hbuf[:, T - 1:T])
                        if n == 0:
                            v.tensor_tensor(yac[:, :], hbuf[:, :], crep[:, 0, :], ALU.mult)
                        else:
                            if n >= 3:
                                v.wait_ge(g("c_y"), i * 15 + n - 2)
                            v.tensor_tensor(hcb[:, n % 2, :], hbuf[:, :], crep[:, n, :], ALU.mult).then_inc(g("c_hc"), 1)
                    v.wait_ge(g("c_y"), (i + 1) * 15)
                    if i >= 2:
                        wld(v, f"yexp{i - 2}")
                    v.scalar_tensor_tensor(y2t[:, :], xcq[:, i % 2, :], s_dpar[:, k:k + 1], yac[:, :],
                                           ALU.mult, ALU.add).then_inc(g("c_fin"), 1)
                    j = i - 2
                    if 0 <= j < 64:
                        v.wait_ge(g("c_mm4"), j + 1)
                        wld(v, f"routt{j}")
                        if j >= 2:
                            v.wait_ge(g("c_ev4b"), 4 * (j - 1))
                        pb4 = (j % 2) * 2
                        v.scalar_tensor_tensor(f4st[:, j % 2, :], ps[:, pb4, :], 1.0, routt[:, j % 3, :],
                                               ALU.mult, ALU.add).then_inc(g("c_ev4"), 1)
            for j in (62, 63):
                v.wait_ge(g("c_mm4"), j + 1)
                wld(v, f"routt{j}")
                v.wait_ge(g("c_ev4b"), 4 * (j - 1))
                pb4 = (j % 2) * 2
                v.scalar_tensor_tensor(f4st[:, j % 2, :], ps[:, pb4, :], 1.0, routt[:, j % 3, :],
                                       ALU.mult, ALU.add).then_inc(g("c_ev4"), 1)
            if debug_phase == 3:
                return
            wld(v, "xpg")
            for tc in range(NTC):
                wld(v, f"xrt{tc}")
                v.wait_ge(g("c_sqx"), tc + 1)
                if tc >= 2:
                    wld(v, f"xpo{tc - 2}")
                chv.bump(v.reduce_sum(xpcols[:, 0:1], fft[:, tc % 2, :], axis=AX))
                chv.join(v)
                chv.bump(v.tensor_scalar(xpcols[:, 1:2], xpcols[:, 0:1], 1.0 / H, None, ALU.mult))
                chv.bump(v.reduce_sum(xpcols[:, 2:3], sqxt[:, :], axis=AX))
                chv.join(v)
                chv.bump(v.tensor_scalar(xpcols[:, 3:4], xpcols[:, 2:3], 1.0 / H, None, ALU.mult))
                chv.bump(v.tensor_tensor(xpcols[:, 4:5], xpcols[:, 1:2], xpcols[:, 1:2], ALU.mult))
                chv.join(v)
                chv.bump(v.tensor_tensor(xpcols[:, 5:6], xpcols[:, 3:4], xpcols[:, 4:5], ALU.subtract))
                chv.join(v)
                v.tensor_scalar(xpcols[:, 6:7], xpcols[:, 5:6], 1.0, EPS_LN, ALU.mult, ALU.add).then_inc(g("c_vepx"), 1)
                chv.mark("c_vepx", tc + 1)
                v.wait_ge(g("c_rstx"), tc + 1)
                chv.bump(v.tensor_scalar(xpt[:, tc % 2, :], fft[:, tc % 2, :], xpcols[:, 1:2], xpcols[:, 7:8],
                                         ALU.subtract, ALU.mult))
                chv.join(v)
                chv.bump(v.tensor_tensor(xpt[:, tc % 2, :], xpt[:, tc % 2, :], gbr[:, 0, :], ALU.mult))
                chv.join(v)
                chv.bump(v.tensor_tensor(xpt[:, tc % 2, :], xpt[:, tc % 2, :], gbr[:, 1, :], ALU.add))
                chv.join(v)
                v.tensor_tensor(xpt[:, tc % 2, :], xpt[:, tc % 2, :], xrt[:, tc % 2, :], ALU.add).then_inc(g("c_xp"), 1)
            if debug_phase <= 4:
                return
            wld(v, "y")
            for i in range(64):
                v.wait_ge(g("c_zs"), i + 1)
                v.tensor_tensor(ybuf[:, i, :], ybuf[:, i, :], zst[:, i % 2, :], ALU.mult).then_inc(g("c_gate"), 1)
            for hc in range(32):
                v.wait_ge(g("c_mm5"), 2 * (hc + 1))
                if hc >= 1:
                    v.wait_ge(g("c_tr5b"), 8 * hc)
                pb = (hc % 2) * 2
                v.tensor_copy(a5st[:, :], ps[:, pb:pb + 2, :].rearrange("p b f -> p (b f)")).then_inc(g("c_ev5b"), 1)
                wld(v, f"xpc{hc}")
                if hc >= 2:
                    wld(v, f"out{hc - 2}")
                for tcc in range(8):
                    i = hc * 8 + tcc
                    v.wait_ge(g("c_tr5b"), i + 1)
                    v.tensor_tensor(ost[:, hc % 2, tcc, :], ps[:, 6 + i % 2, 0:128], xpcs[:, hc % 2, tcc, :],
                                    ALU.add).then_inc(g("c_asm"), 1)

    return nc


_CACHE = {}


def kernel(**inputs):
    nc = _CACHE.get("nc99")
    if nc is None:
        nc = build_nc()
        _CACHE["nc99"] = nc
    in_maps = make_in_maps(inputs)
    res = run_bass_kernel_spmd(nc, in_maps, core_ids=list(range(8)))
    out = np.empty((B, L, H), dtype=np.float32)
    for c in range(8):
        b, r = divmod(c, 4)
        out[b, r * T:(r + 1) * T, :] = res.results[c]["out_tok"]
    return out


def make_in_maps(inputs):
    f32 = np.float32
    X = np.ascontiguousarray(np.asarray(inputs["L0_attn_hidden_states"], dtype=f32))
    asf = lambda k: np.ascontiguousarray(np.asarray(inputs[k], dtype=f32))
    def blockw(w, ogw):
        KK, CC = w.shape
        nk, nog = KK // 128, CC // ogw
        return np.ascontiguousarray(w.reshape(nk, 128, nog, ogw).transpose(2, 1, 0, 3).reshape(nog, 128, nk * ogw))
    shared = dict(
        w_in=blockw(asf("L0_attn_in_proj_w"), 512),
        outw=asf("L0_attn_out_proj_w"),
        dtlnw=asf("L0_attn_dt_ln_w").reshape(1, R),
        blnw=asf("L0_attn_B_ln_w").reshape(1, NST),
        clnw=asf("L0_attn_C_ln_w").reshape(1, NST),
        gatew=np.ascontiguousarray(asf("L0_ffn_gate_w").reshape(NKH, 128, NE).transpose(1, 0, 2).reshape(128, NKH * NE)),
        upw=blockw(asf("L0_ffn_up_w"), 512),
        downw=blockw(asf("L0_ffn_down_w"), 128),
        fc1w=blockw(asf("L0_ffn_fc1_w"), 512),
        fc2w=blockw(asf("L0_ffn_fc2_w"), 128),
        expw=blockw(asf("L0_ffn_expert_w"), 512),
        lngamma=asf("L0_ffn_ln_gamma").reshape(1, H),
        lnbeta=asf("L0_ffn_ln_beta").reshape(1, H),
        identf=np.eye(128, dtype=f32),
    )
    cw = asf("L0_attn_conv_w"); cb = asf("L0_attn_conv_b")
    xpw = asf("L0_attn_x_proj_w"); dpw = asf("L0_attn_dt_proj_w")
    dtb = asf("L0_attn_dt_bias"); al = asf("L0_attn_A_log"); dp = asf("L0_attn_D_param")
    in_maps = []
    for c in range(8):
        b, r = divmod(c, 4)
        sl = slice(r * DSL, (r + 1) * DSL)
        m = dict(shared)
        m["x_tok"] = np.ascontiguousarray(X[b, r * T:(r + 1) * T, :])
        m["convw"] = np.ascontiguousarray(cw[sl])
        m["convb"] = np.ascontiguousarray(cb[sl].reshape(NKD, 128).T)
        m["xprojw"] = np.ascontiguousarray(xpw[:, sl])
        m["dtprojw"] = np.ascontiguousarray(dpw[sl])
        m["dtbias"] = np.ascontiguousarray(dtb[sl].reshape(NKD, 128).T)
        m["alog"] = np.ascontiguousarray(al[sl])
        m["dpar"] = np.ascontiguousarray(dp[sl].reshape(NKD, 128).T)
        in_maps.append(m)
    return in_maps


# version 03
# speedup vs baseline: 1.0407x; 1.0319x over previous
import numpy as np
import ml_dtypes
import concourse.bass as bass
import concourse.mybir as mybir
from concourse.bass_utils import run_bass_kernel_spmd

F32 = mybir.dt.float32
BF16 = mybir.dt.bfloat16
AF = mybir.ActivationFunctionType
ALU = mybir.AluOpType
SP = mybir.EngineType.SP
AX = mybir.AxisListType.X

H = 4096; DIN = 8192; NST = 16; KCV = 4; R = 256
B = 2; L = 4096; T = 1024; DSL = 2048; E2 = 2 * DIN
NE = 8; INT = 8192; LAT = 2048
EPS_LN = 1e-5; EPS_RMS = 1e-6
NTC = 8; LTC = 32; NKH = 32; NKD = 16
RG = [[0, 1, 2, 3], [4, 5, 6, 7]]
ts = bass.ts

EV_INP, EV_GATE, EV_FC1, EV_EXP, EV_UP, EV_END = 0, 256, 264, 296, 328, 456


def build_nc(debug_phase=99, race_detect=True):
    nc = bass.Bass("TRN2", target_bir_lowering=False, detect_race_conditions=race_detect)

    def I(name, shape):
        return nc.dram_tensor(name, shape, F32, kind="ExternalInput").ap()
    x_tok = I("x_tok", [T, H]); w_in = I("w_in", [32, 128, E2])
    convw = I("convw", [DSL, KCV]); convb = I("convb", [128, NKD])
    xprojw = I("xprojw", [R + 2 * NST, DSL]); dtprojw = I("dtprojw", [DSL, R])
    dtbias = I("dtbias", [128, NKD]); alog = I("alog", [DSL, NST]); dpar = I("dpar", [128, NKD])
    outw = I("outw", [H, DIN])
    dtlnw = I("dtlnw", [1, R]); blnw = I("blnw", [1, NST]); clnw = I("clnw", [1, NST])
    gatew = I("gatew", [128, NKH * NE]); upw = I("upw", [16, 128, 2 * INT]); downw = I("downw", [32, 128, INT // 128 * 128])
    fc1w = I("fc1w", [4, 128, NKH * 512]); fc2w = I("fc2w", [32, 128, LAT]); expw = I("expw", [4, 128, 16 * 512])
    lngamma = I("lngamma", [1, H]); lnbeta = I("lnbeta", [1, H])
    identf = I("identf", [128, 128])

    out_tok = nc.dram_tensor("out_tok", [T, H], F32, kind="ExternalOutput").ap()
    dbg_out = None
    if debug_phase < 99:
        shp = {1: ([DIN, T], BF16), 2: ([DSL, L], BF16), 3: ([4 * DSL, T], BF16),
               4: ([T, H], F32), 10: ([128, NKH * T], BF16), 11: ([DIN, T], BF16),
               21: ([DSL, L], BF16), 22: ([L, R + 2 * NST], F32)}[debug_phase]
        dbg_out = nc.dram_tensor("dbg_out", shp[0], shp[1], kind="ExternalOutput").ap()

    D = nc.dram_tensor
    ccA_in = D("ccA_in", [DIN, T], BF16, kind="Internal").ap()
    ccA_out = D("ccA_out", [4 * DIN, T], BF16, kind="Internal").ap()
    ccB_in = D("ccB_in", [4 * DSL, T], BF16, kind="Internal").ap()
    ccB_out = D("ccB_out", [16 * DSL, T], BF16, kind="Internal").ap()
    ccC_in = D("ccC_in", [L, R + 2 * NST], F32, kind="Internal").ap()
    ccC_out = D("ccC_out", [L, R + 2 * NST], F32, kind="Internal").ap()
    zbufd = D("zbufd", [DIN, T], BF16, kind="Internal").ap()
    hactd = D("hactd", [INT, T], BF16, kind="Internal").ap()
    moed = D("moed", [LAT, T], BF16, kind="Internal").ap()
    routd = D("routd", [H, T], BF16, kind="Internal").ap()
    dtspd = D("dtspd", [DSL, L], BF16, kind="Internal").ap()
    xcspd = D("xcspd", [DSL, L], BF16, kind="Internal").ap()
    ffnTd = D("ffnTd", [T, H], BF16, kind="Internal").ap()
    xplusd = D("xplusd", [T, H], F32, kind="Internal").ap()
    browd = D("browd", [1, NST * L], BF16, kind="Internal").ap()
    crowd = D("crowd", [1, NST * L], BF16, kind="Internal").ap()
    srowd = D("srowd", [1, T], BF16, kind="Internal").ap()

    ps = nc.alloc_psum_tensor([128, 8, 512], F32)

    AL = nc.alloc_sbuf_tensor
    s_identf = AL([128, 128], F32)
    s_A = AL([128, NKD, NST], F32)
    s_convw = AL([128, NKD, KCV], F32)
    s_convb = AL([128, NKD], F32)
    s_dtbias = AL([128, NKD], F32)
    s_dpar = AL([128, NKD], F32)
    s_hstate = AL([128, NKD * NST], F32)
    s_gamr = AL([128, R + 2 * NST], F32)
    s_srow = AL([1, T], BF16)
    gsb = AL([128, NKH, NE], BF16)
    base0 = nc.sbuf_base

    xnf = AL([128, NKH, T], BF16)
    wslab = AL([128, 2, NKH, 512], BF16)
    ovl0 = nc.sbuf_base
    xt = AL([128, 2, H], F32)
    xnstage = AL([128, 2, H], F32)
    lncols = AL([128, 16], F32)
    p0_end = nc.sbuf_base
    nc.sbuf_base = ovl0
    evst = AL([128, 2, 4, 1024], BF16)
    xlat = AL([128, 16, T], BF16)
    rlst = AL([128, 2, 512], BF16)
    s_rep = AL([128, T], BF16)
    logit = AL([128, NE], F32)
    lgm = AL([128, NE], F32)
    l_m = AL([128, 4], F32)
    scols = AL([128, NTC], F32)
    p1_end = nc.sbuf_base

    nc.sbuf_base = base0
    xdb = AL([128, NKD, L], BF16)
    cacc = AL([128, L], F32)
    cvst = AL([128, L], BF16)
    wxa = AL([128, DSL], F32)
    wxb = AL([128, DSL], F32)
    wxc = AL([32, DSL], F32)
    wxT = AL([128, NKD, R + 2 * NST], BF16)
    ssmst = AL([128, 8, R + 2 * NST], F32)
    p2a_end = nc.sbuf_base

    nc.sbuf_base = base0
    ssm2 = AL([128, LTC, R + 2 * NST], F32)
    sqst = AL([128, R + 2 * NST], F32)
    rmcols = AL([128, 8], F32)
    dtr32 = AL([128, 4, R], F32)
    bnt = AL([128, 4, NST], F32)
    cnt = AL([128, 4, NST], F32)
    btst = AL([16, LTC, 128], BF16)
    ctst = AL([16, LTC, 128], BF16)
    dtrT = AL([128, 2, L], BF16)
    wdtnat = AL([128, NKD, R], F32)
    wdtT = AL([128, 2, DSL], BF16)
    moesb = AL([128, 16, T], BF16)
    fcsl = AL([128, 2, 16, 128], BF16)
    fc2st = AL([128, 2, 512], BF16)
    spb = AL([128, 2, DSL], F32)
    spa = AL([128, 2, DSL], F32)
    dtst = AL([128, 2, DSL], BF16)
    p2b_end = nc.sbuf_base

    nc.sbuf_base = base0
    brep = AL([128, NST, T], BF16)
    crep = AL([128, NST, T], BF16)
    dtq = AL([128, 2, T], BF16)
    xcq = AL([128, 2, T], BF16)
    dtxq = AL([128, T], BF16)
    abuf = AL([128, 3, T], BF16)
    bbuf = AL([128, T], BF16)
    hbuf = AL([128, T], BF16)
    hcb = AL([128, 2, T], BF16)
    yac = AL([128, T], F32)
    y2t = AL([128, T], F32)
    hacth = AL([128, 64, 512], BF16)
    dwsl = AL([128, 2, 64, 128], BF16)
    routt = AL([128, 3, 512], BF16)
    f4st = AL([128, 2, 512], F32)
    ftst = AL([128, 2, 4, 128], BF16)
    p34_end = nc.sbuf_base

    nc.sbuf_base = base0
    gbr = AL([128, 2, H], F32)
    fft = AL([128, 2, H], BF16)
    xrt = AL([128, 2, H], F32)
    xpt = AL([128, 2, H], F32)
    sqxt = AL([128, H], F32)
    xpcols = AL([128, 8], F32)
    p4b_end = nc.sbuf_base

    nc.sbuf_base = base0
    ybuf = AL([128, 64, T], BF16)
    wosl = AL([128, 2, 2048], F32)
    wline = AL([128, 64, 128], BF16)
    zkt = AL([128, 2, T], BF16)
    zst = AL([128, 2, T], BF16)
    xpcs = AL([128, 2, 8, 128], F32)
    ost = AL([128, 2, 8, 128], F32)
    a5st = AL([128, 1024], F32)
    p5_end = nc.sbuf_base

    LIMIT = 229344
    for nm, e in [("p0", p0_end), ("p1", p1_end), ("p2a", p2a_end), ("p2b", p2b_end),
                  ("p34", p34_end), ("p4b", p4b_end), ("p5", p5_end)]:
        assert e <= LIMIT, f"SBUF overflow {nm}: {e} > {LIMIT}"
    nc.sbuf_base = max(p0_end, p1_end, p2a_end, p2b_end, p34_end, p4b_end, p5_end)

    names = """q_misc q_x c_sq0 c_lnr c_ln c_tr0 c_xnf q_w0 q_w1 q_gw c_mm1 c_ev1 q_exp
    c_scol c_str c_srow q_srep c_relu q_cca q_xd c_tap c_silu c_cvb
    q_wx c_trwx c_wxc c_trwd c_wdtc c_mm2 c_ev2 q_ssm q_ccc q_ssml c_sq2 c_rms
    c_bct c_bcc c_dtt c_dtrc q_brow q_moe2 q_fcs c_mmf c_evf q_rout c_mm3 c_sp0 c_spa c_sp
    q_dtsp q_xcsp q_bc q_sc c_dtx c_a c_b c_h c_hc c_y c_fin q_yexp q_ccb
    q_hact q_dwn q_routt c_mm4 c_ev4 c_tr4 c_ev4b q_ffnT
    q_xp_in c_sqx c_xp q_xp_out q_y q_z c_zs c_gate q_wo0 q_wo1 c_tr5 c_ev5 c_mm5
    c_ev5b c_tr5b q_xpc c_asm q_out
    c_veps c_rstd c_red3 c_vep3 c_rst3 c_vepx c_rstx ch_v ch_s ch_p""".split()
    S = {n: nc.alloc_semaphore(n) for n in names}
    g = lambda n: S[n]

    LD = {}

    class DQ:
        def __init__(self, eng, semname):
            self.eng, self.sem, self.n = eng, S[semname], 0

        def dma(self, name, out, in_, **kw):
            if self.n >= 1:
                self.eng.wait_ge(self.sem, 16 * self.n)
            self.n += 1
            r = self.eng.dma_start(out=out, in_=in_, **kw).then_inc(self.sem, 16)
            if name is not None:
                LD[name] = (self.sem, 16 * self.n)
            return r

        def flush(self):
            if self.n >= 1:
                self.eng.wait_ge(self.sem, 16 * self.n)

    def wld(eng, name):
        sem, cnt = LD[name]
        eng.wait_ge(sem, cnt)

    class Chain:
        def __init__(self, semname):
            self.sem, self.n, self.last = S[semname], 0, None

        def bump(self, ins):
            ins.then_inc(self.sem, 1)
            self.n += 1
            self.last = (self.sem, self.n)
            return ins

        def mark(self, semname, val):
            self.last = (S[semname], val)

        def join(self, eng):
            if self.last is not None:
                eng.wait_ge(*self.last)

    chv, chs, chp = Chain("ch_v"), Chain("ch_s"), Chain("ch_p")

    dwn_sched = [(th, hc) for th in range(2) for hc in range(32)]

    with nc.Block() as blk:

        @blk.sync
        def _(sy):
            pid = nc.partition_id([SP])
            myg = pid % 4
            dq_misc = DQ(sy, "q_misc"); dq_x = DQ(sy, "q_x"); dq_exp = DQ(sy, "q_exp")
            dq_srep = DQ(sy, "q_srep"); dq_xd = DQ(sy, "q_xd"); dq_ssm = DQ(sy, "q_ssm")
            dq_xcsp = DQ(sy, "q_xcsp"); dq_ssml = DQ(sy, "q_ssml"); dq_moe2 = DQ(sy, "q_moe2")
            dq_brow = DQ(sy, "q_brow"); dq_rout = DQ(sy, "q_rout"); dq_dtsp = DQ(sy, "q_dtsp")
            dq_bc = DQ(sy, "q_bc"); dq_sc = DQ(sy, "q_sc"); dq_hact = DQ(sy, "q_hact")
            dq_routt = DQ(sy, "q_routt"); dq_ffnT = DQ(sy, "q_ffnT"); dq_xpin = DQ(sy, "q_xp_in")
            dq_xpout = DQ(sy, "q_xp_out"); dq_y = DQ(sy, "q_y"); dq_z = DQ(sy, "q_z")
            dq_wo = [DQ(sy, "q_wo0"), DQ(sy, "q_wo1")]
            dq_xpc = DQ(sy, "q_xpc"); dq_out = DQ(sy, "q_out")

            dq_misc.dma(None, out=s_identf[:], in_=identf)
            dq_misc.dma(None, out=s_convw[:, :, :], in_=convw.rearrange("(k p) j -> p k j", p=128))
            dq_misc.dma(None, out=s_convb[:, :], in_=convb)
            dq_misc.dma(None, out=s_dtbias[:, :], in_=dtbias)
            dq_misc.dma(None, out=s_dpar[:, :], in_=dpar)
            dq_misc.dma(None, out=s_A[:, :, :], in_=alog.rearrange("(k p) n -> p k n", p=128))
            dq_misc.dma(None, out=s_gamr[:, 0:R], in_=dtlnw.partition_broadcast(128))
            dq_misc.dma(None, out=s_gamr[:, R:R + NST], in_=blnw.partition_broadcast(128))
            dq_misc.dma("misc", out=s_gamr[:, R + NST:], in_=clnw.partition_broadcast(128))
            for tc in range(NTC):
                if tc >= 2:
                    sy.wait_ge(g("c_ln"), tc - 1)
                dq_x.dma(f"x{tc}", out=xt[:, tc % 2, :], in_=x_tok[ts(tc, 128), :])
            if debug_phase == 10:
                sy.wait_ge(g("c_xnf"), 256)
                dq_out.dma(None, out=dbg_out, in_=xnf[:, :, :].rearrange("p a b -> p (a b)"))
                dq_out.flush()
                return
            for og in range(32):
                sy.wait_ge(g("c_ev1"), EV_INP + 8 * (og + 1))
                dst, ro = (ccA_in, og * 512) if og < 16 else (zbufd, (og - 16) * 512)
                dq_exp.dma(f"exp{og}", out=dst[ro:ro + 512, :].rearrange("(j p) t -> p j t", p=128),
                           in_=evst[:, og % 2, :, :])
            if debug_phase == 11:
                dq_exp.flush()
                dq_out.dma(None, out=dbg_out, in_=zbufd)
                dq_out.flush()
                return
            sy.wait_ge(g("c_srow"), NTC)
            dq_srep.dma(None, out=srowd, in_=s_srow[:, :])
            dq_srep.dma("srep", out=s_rep[:, :], in_=srowd.partition_broadcast(128))
            for og in range(4):
                sy.wait_ge(g("c_ev1"), EV_EXP + 8 * (og + 1))
                dq_exp.dma(f"expm{og}", out=moed[og * 512:(og + 1) * 512, :].rearrange("(j p) t -> p j t", p=128),
                           in_=evst[:, og % 2, :, :])
            for og in range(16):
                sy.wait_ge(g("c_ev1"), EV_UP + 8 * (og + 1))
                dq_exp.dma(f"expu{og}", out=hactd[og * 512:(og + 1) * 512, :].rearrange("(j p) t -> p j t", p=128),
                           in_=evst[:, og % 2, :, :])
            if debug_phase == 1:
                dq_exp.flush()
                sy.wait_ge(g("q_cca"), 1)
                dq_out.dma(None, out=dbg_out, in_=zbufd)
                dq_out.flush()
                return

            sy.wait_ge(g("q_cca"), 1)
            sy.wait_ge(g("c_mm1"), EV_END)
            dq_exp.flush()
            for rr in range(4):
                srcx = ccA_out[rr * DIN:(rr + 1) * DIN, :].rearrange("(gg kd p) t -> gg p kd t", gg=4, p=128)[ts(myg, 1), :, :, :]
                dq_xd.dma("xd" if rr == 3 else None,
                          out=xdb[:, :, rr * 1024:(rr + 1) * 1024], in_=srcx)
            for gse in range(4):
                sy.wait_ge(g("c_ev2"), 8 * (gse + 1))
                dq_ssm.dma(f"ssm{gse}", out=ccC_in[gse * 1024:(gse + 1) * 1024, :].rearrange("(tc p) e -> p tc e", p=128),
                           in_=ssmst[:, :, :])
            sy.wait_ge(g("c_mm2"), LTC)
            dq_xcsp.dma("xcsp", out=xcspd.rearrange("(k p) t -> p k t", p=128), in_=xdb[:, :, :])
            if debug_phase == 21:
                sy.wait_ge(g("q_xcsp"), 16)
                dq_out.dma(None, out=dbg_out, in_=xcspd)
                dq_out.flush()
                return
            sy.wait_ge(g("q_xcsp"), 16)
            dq_moe2.dma("moe2", out=moesb[:, :, :], in_=moed.rearrange("(k p) t -> p k t", p=128))
            sy.wait_ge(g("q_ccc"), 1)
            if debug_phase == 22:
                dq_out.dma(None, out=dbg_out, in_=ccC_out)
                dq_out.flush()
                return
            dq_ssml.dma("ssml", out=ssm2[:, :, :], in_=ccC_out.rearrange("(tc p) e -> p tc e", p=128))
            for i in range(64):
                hc, th = divmod(i, 2)
                sy.wait_ge(g("c_evf"), i + 1)
                dq_rout.dma(f"rout{i}", out=routd[ts(hc, 128), ts(th, 512)], in_=fc2st[:, i % 2, :])
            sy.wait_ge(g("c_bcc"), 2 * LTC)
            dq_brow.dma(None, out=browd.rearrange("one (n tc j) -> one n tc j", n=NST, tc=LTC),
                        in_=btst[:, :, :])
            dq_brow.dma("brow", out=crowd.rearrange("one (n tc j) -> one n tc j", n=NST, tc=LTC),
                        in_=ctst[:, :, :])
            for i in range(2 * NKD):
                kd, hf = divmod(i, 2)
                sy.wait_ge(g("c_sp"), i + 1)
                dq_dtsp.dma(f"dtsp{i}", out=dtspd[ts(kd, 128), ts(hf, 2048)], in_=dtst[:, i % 2, :])
            if debug_phase == 2:
                dq_dtsp.flush()
                dq_out.dma(None, out=dbg_out, in_=dtspd)
                dq_out.flush()
                return

            dq_dtsp.flush()
            dq_brow.flush()
            dq_rout.flush()
            dq_hact.dma("hact0", out=hacth[:, :, :],
                        in_=hactd.rearrange("(k p) t -> p k t", p=128)[:, :, 0:512])

            def p4_interleave(i):
                if i < 64:
                    j = i
                    th4, hc4 = dwn_sched[j]
                    if j >= 3:
                        sy.wait_ge(g("c_ev4"), j - 2)
                    dq_routt.dma(f"routt{j}", out=routt[:, j % 3, :], in_=routd[ts(hc4, 128), ts(th4, 512)])
                je = i - 4
                if 0 <= je < 64:
                    th4, hc4 = dwn_sched[je]
                    sy.wait_ge(g("c_ev4b"), 4 * (je + 1))
                    dq_ffnT.dma(f"ffnT{je}",
                                out=ffnTd[th4 * 512:(th4 + 1) * 512, ts(hc4, 128)].rearrange("(sub p) c -> p sub c", p=128),
                                in_=ftst[:, je % 2, :, :])
                if i == 34:
                    sy.wait_ge(g("c_mm4"), 32)
                    dq_hact.dma("hact1", out=hacth[:, :, :],
                                in_=hactd.rearrange("(k p) t -> p k t", p=128)[:, :, 512:1024])

            for q in range(4):
                for n in range(NST):
                    if q > 0:
                        lt = 16 * q - 1
                        sy.wait_ge(g("c_b"), lt * 16 + n + 1)
                        if n == 0:
                            sy.wait_ge(g("c_h"), lt * 16 + 2)
                        else:
                            sy.wait_ge(g("c_hc"), lt * 15 + n)
                    dq_bc.dma(None, out=brep[:, n, :],
                              in_=browd[0:1, n * L + q * T:n * L + (q + 1) * T].partition_broadcast(128))
                    dq_bc.dma(f"bc{q}_{n}", out=crep[:, n, :],
                              in_=crowd[0:1, n * L + q * T:n * L + (q + 1) * T].partition_broadcast(128))
                for k in range(NKD):
                    i = q * NKD + k
                    if i >= 2:
                        sy.wait_ge(g("c_fin"), i - 1)
                    dq_sc.dma(None, out=dtq[:, i % 2, :], in_=dtspd[ts(k, 128), ts(q, T)])
                    dq_sc.dma(f"sc{i}", out=xcq[:, i % 2, :], in_=xcspd[ts(k, 128), ts(q, T)])
                    p4_interleave(i)
            for je in range(60, 64):
                th4, hc4 = dwn_sched[je]
                sy.wait_ge(g("c_ev4b"), 4 * (je + 1))
                dq_ffnT.dma(f"ffnT{je}",
                            out=ffnTd[th4 * 512:(th4 + 1) * 512, ts(hc4, 128)].rearrange("(sub p) c -> p sub c", p=128),
                            in_=ftst[:, je % 2, :, :])
            if debug_phase == 3:
                sy.wait_ge(g("q_yexp"), 16 * 64)
                dq_out.dma(None, out=dbg_out, in_=ccB_in)
                dq_out.flush()
                return

            dq_ffnT.flush()
            sy.wait_ge(g("c_fin"), 64)
            sy.wait_ge(g("q_yexp"), 16 * 64)
            dq_xpin.dma(None, out=gbr[:, 0, :], in_=lngamma.partition_broadcast(128))
            dq_xpin.dma("xpg", out=gbr[:, 1, :], in_=lnbeta.partition_broadcast(128))
            for tc in range(NTC):
                if tc >= 2:
                    sy.wait_ge(g("c_xp"), tc - 1)
                dq_xpin.dma(f"fft{tc}", out=fft[:, tc % 2, :], in_=ffnTd[ts(tc, 128), :])
                dq_xpin.dma(f"xrt{tc}", out=xrt[:, tc % 2, :], in_=x_tok[ts(tc, 128), :])
                if tc >= 2:
                    sy.wait_ge(g("c_xp"), tc - 1)
                    dq_xpout.dma(f"xpo{tc - 2}", out=xplusd[ts(tc - 2, 128), :], in_=xpt[:, tc % 2, :])
            for tc in range(NTC - 2, NTC):
                sy.wait_ge(g("c_xp"), tc + 1)
                dq_xpout.dma(f"xpo{tc}", out=xplusd[ts(tc, 128), :], in_=xpt[:, tc % 2, :])
            if debug_phase == 4:
                dq_xpout.flush()
                dq_out.dma(None, out=dbg_out, in_=xplusd)
                dq_out.flush()
                return

            sy.wait_ge(g("q_ccb"), 1)
            dq_xpout.flush()
            sy.wait_ge(g("c_mm4"), 64)
            for rr in range(4):
                srcy = ccB_out[rr * 4 * DSL:(rr + 1) * 4 * DSL, :].rearrange("(j kl p) t -> j p kl t", j=4, p=128)[ts(myg, 1), :, :, :]
                dq_y.dma("y" if rr == 3 else None,
                         out=ybuf[:, rr * 16:(rr + 1) * 16, :], in_=srcy)
            for i in range(64):
                if i >= 2:
                    sy.wait_ge(g("c_zs"), i - 1)
                dq_z.dma(f"z{i}", out=zkt[:, i % 2, :], in_=zbufd[ts(i, 128), :])
            def wo_load(i):
                hc4, qf = divmod(i, 4)
                if i >= 2:
                    sy.wait_ge(g("c_tr5"), (i - 1) * 16)
                dq_wo[i % 2].dma(f"wo{i}", out=wosl[:, i % 2, :], in_=outw[ts(hc4, 128), ts(qf, 2048)])
            wo_load(0)
            wo_load(1)
            for hc in range(32):
                if hc >= 2:
                    sy.wait_ge(g("c_asm"), (hc - 1) * 8)
                dq_xpc.dma(f"xpc{hc}", out=xpcs[:, hc % 2, :, :],
                           in_=xplusd.rearrange("(tc p) h -> p tc h", p=128)[:, :, ts(hc, 128)])
                for qq in range(4 * hc + 2, min(4 * hc + 6, 128)):
                    wo_load(qq)
                sy.wait_ge(g("c_asm"), (hc + 1) * 8)
                dq_out.dma(f"out{hc}", out=out_tok.rearrange("(tc p) h -> p tc h", p=128)[:, :, ts(hc, 128)],
                           in_=ost[:, hc % 2, :, :])
            dq_out.flush()

        @blk.gpsimd
        def _(gp):
            dq_gw = DQ(gp, "q_gw"); dq_w = [DQ(gp, "q_w0"), DQ(gp, "q_w1")]
            dq_wx = DQ(gp, "q_wx"); dq_fcs = DQ(gp, "q_fcs"); dq_dwn = DQ(gp, "q_dwn")
            dq_yexp = DQ(gp, "q_yexp")

            def slab_src(w, og, width=512):
                return w.rearrange("(k p) c -> p k c", p=128)[:, :, og * width:(og + 1) * width]
            if debug_phase == 10:
                return
            dq_gw.dma("gatew", out=gsb[:, :, :], in_=gatew.rearrange("(k p) e -> p k e", p=128))
            for og in range(32):
                if og >= 2:
                    gp.wait_ge(g("c_mm1"), EV_INP + 8 * (og - 1))
                dq_w[og % 2].dma(f"win{og}", out=wslab[:, og % 2, :, :], in_=w_in[og, :, :].rearrange("p (k c) -> p k c", c=512))
            if debug_phase == 11:
                return
            for og in range(4):
                gp.wait_ge(g("c_mm1"), (EV_INP + 256) if og < 2 else (EV_FC1 + 8 * (og - 1)))
                dq_w[og % 2].dma(f"fc1w{og}", out=wslab[:, og % 2, :, :], in_=fc1w[og, :, :].rearrange("p (k c) -> p k c", c=512))
            for og in range(4):
                gp.wait_ge(g("c_mm1"), (EV_FC1 + 32) if og < 2 else (EV_EXP + 8 * (og - 1)))
                dq_w[og % 2].dma(f"expw{og}", out=wslab[:, og % 2, 0:16, :], in_=expw[og, :, :].rearrange("p (k c) -> p k c", c=512))
            for og in range(16):
                gp.wait_ge(g("c_mm1"), (EV_EXP + 32) if og < 2 else (EV_UP + 8 * (og - 1)))
                dq_w[og % 2].dma(f"upw{og}", out=wslab[:, og % 2, :, :], in_=upw[og, :, :].rearrange("p (k c) -> p k c", c=512))
            wld(gp, "exp15")
            gp.collective_compute("AllGather", ALU.bypass, replica_groups=RG,
                                  ins=[ccA_in.opt()], outs=[ccA_out.opt()]).then_inc(g("q_cca"))
            if debug_phase == 1:
                return
            gp.wait_ge(g("c_mm1"), EV_END)
            wld(gp, "expu15")
            dq_wx.dma(None, out=wxa[:, :], in_=xprojw[0:128, :])
            dq_wx.dma(None, out=wxb[:, :], in_=xprojw[128:256, :])
            dq_wx.dma("wxc3", out=wxc[:, :], in_=xprojw[256:288, :])
            gp.wait_ge(g("q_xcsp"), 16)
            dq_wx.dma("wdtn", out=wdtnat[:, :, :], in_=dtprojw.rearrange("(k p) r -> p k r", p=128))
            if debug_phase == 21:
                return
            wld(gp, "ssm3")
            gp.collective_compute("AllReduce", ALU.add, replica_groups=RG,
                                  ins=[ccC_in.opt()], outs=[ccC_out.opt()]).then_inc(g("q_ccc"))
            if debug_phase == 22:
                return
            for hc in range(32):
                if hc >= 2:
                    gp.wait_ge(g("c_mmf"), 2 * (hc - 1))
                dq_fcs.dma(f"fcs{hc}", out=fcsl[:, hc % 2, :, :],
                           in_=fc2w[hc, :, :].rearrange("p (k c) -> p k c", c=128))
            if debug_phase in (2, 21, 22):
                return
            wld(gp, "dtsp31")
            wld(gp, "rout63")
            dwn_issued = 0
            for q in range(4):
                for k in range(NKD):
                    i = q * NKD + k
                    while dwn_issued < 64 and dwn_issued * 64 <= i * 64 + 32:
                        jj = dwn_issued
                        th4, hc4 = dwn_sched[jj]
                        if jj >= 2:
                            gp.wait_ge(g("c_mm4"), jj - 1)
                        dq_dwn.dma(f"dwn{jj}", out=dwsl[:, jj % 2, :, :],
                                   in_=downw[hc4, :, :].rearrange("p (k c) -> p k c", c=128))
                        dwn_issued += 1
                    for n in range(1, NST):
                        gp.wait_ge(g("c_hc"), i * 15 + n)
                        if n >= 2:
                            chp.join(gp)
                        gp.tensor_tensor(yac[:, :], yac[:, :], hcb[:, n % 2, :], ALU.add).then_inc(g("c_y"), 1)
                        chp.mark("c_y", i * 15 + n)
                    gp.wait_ge(g("c_fin"), i + 1)
                    dq_yexp.dma(f"yexp{i}", out=ccB_in[q * DSL + k * 128:q * DSL + (k + 1) * 128, :],
                                in_=y2t[:, :])
            while dwn_issued < 64:
                jj = dwn_issued
                th4, hc4 = dwn_sched[jj]
                if jj >= 2:
                    gp.wait_ge(g("c_mm4"), jj - 1)
                dq_dwn.dma(f"dwn{jj}", out=dwsl[:, jj % 2, :, :],
                           in_=downw[hc4, :, :].rearrange("p (k c) -> p k c", c=128))
                dwn_issued += 1
            dq_yexp.flush()
            gp.collective_compute("AllGather", ALU.bypass, replica_groups=RG,
                                  ins=[ccB_in.opt()], outs=[ccB_out.opt()]).then_inc(g("q_ccb"))

        @blk.tensor
        def _(t):
            wld(t, "misc")
            for tc in range(NTC):
                t.wait_ge(g("c_ln"), tc + 1)
                for hc in range(NKH):
                    i = tc * NKH + hc
                    if i >= 4:
                        t.wait_ge(g("c_xnf"), i - 3)
                    t.transpose(ps[:, i % 4, 0:128], xnstage[:, tc % 2, ts(hc, 128)], s_identf[:]).then_inc(g("c_tr0"), 1)
            def mm_slabs(n_og, nk, ev_base, wname, rhs_sb):
                for og in range(n_og):
                    wld(t, wname.format(og))
                    t.wait_ge(g("c_ev1"), ev_base + 8 * og)
                    for j in range(4):
                        for th in range(2):
                            last = None
                            for k in range(nk):
                                last = t.matmul(ps[:, j * 2 + th, :],
                                                wslab[:, og % 2, k, ts(j, 128)],
                                                rhs_sb[:, k, ts(th, 512)],
                                                start=(k == 0), stop=(k == nk - 1))
                            last.then_inc(g("c_mm1"), 1)
            if debug_phase == 10:
                return
            t.wait_ge(g("c_xnf"), 256)
            mm_slabs(32, NKH, EV_INP, "win{}", xnf)
            if debug_phase == 11:
                return
            wld(t, "gatew")
            for tc in range(NTC):
                t.wait_ge(g("c_ev1"), EV_GATE + tc)
                last = None
                for k in range(NKH):
                    last = t.matmul(ps[:, 6, 0:NE], xnf[:, k, ts(tc, 128)], gsb[:, k, :],
                                    start=(k == 0), stop=(k == NKH - 1))
                last.then_inc(g("c_mm1"), 1)
            for tc in range(NTC):
                t.wait_ge(g("c_scol"), tc + 1)
                if tc >= 1:
                    t.wait_ge(g("c_srow"), tc)
                t.transpose(ps[0:1, 7, 0:128], scols[:, tc:tc + 1], s_identf[:]).then_inc(g("c_str"), 1)
            t.wait_ge(g("c_srow"), NTC)
            mm_slabs(4, NKH, EV_FC1, "fc1w{}", xnf)
            t.wait_ge(g("c_ev1"), EV_FC1 + 32)
            mm_slabs(4, 16, EV_EXP, "expw{}", xlat)
            mm_slabs(16, NKH, EV_UP, "upw{}", xnf)
            if debug_phase == 1:
                return
            wld(t, "wxc3")
            for kd in range(NKD):
                if kd >= 1:
                    t.wait_ge(g("c_wxc"), 3 * kd)
                t.transpose(ps[:, 4, 0:128], wxa[:, ts(kd, 128)], s_identf[:])
                t.transpose(ps[:, 5, 0:128], wxb[:, ts(kd, 128)], s_identf[:])
                t.transpose(ps[:, 6, 0:32], wxc[0:32, ts(kd, 128)], s_identf[0:32, 0:32]).then_inc(g("c_trwx"), 3)
            wld(t, "xd")
            t.wait_ge(g("c_cvb"), NKD)
            t.wait_ge(g("c_wxc"), 3 * NKD)
            for tc in range(LTC):
                if tc >= 4:
                    t.wait_ge(g("c_ev2"), tc - 3)
                last = None
                for kd in range(NKD):
                    last = t.matmul(ps[:, tc % 4, 0:R + 2 * NST], xdb[:, kd, ts(tc, 128)], wxT[:, kd, :],
                                    start=(kd == 0), stop=(kd == NKD - 1))
                last.then_inc(g("c_mm2"), 1)
            if debug_phase in (21, 22):
                return
            t.wait_ge(g("c_ev2"), LTC)
            wld(t, "moe2")
            for i in range(64):
                hc, th = divmod(i, 2)
                wld(t, f"fcs{hc}")
                if i >= 4:
                    t.wait_ge(g("c_evf"), i - 3)
                last = None
                for k in range(16):
                    last = t.matmul(ps[:, i % 4, :], fcsl[:, hc % 2, k, :], moesb[:, k, ts(th, 512)],
                                    start=(k == 0), stop=(k == 15))
                last.then_inc(g("c_mmf"), 1)
            wld(t, "wdtn")
            for kd in range(NKD):
                for rc in range(2):
                    i = kd * 2 + rc
                    if i >= 1:
                        t.wait_ge(g("c_wdtc"), i)
                    t.transpose(ps[:, 7, 0:128], wdtnat[:, kd, ts(rc, 128)], s_identf[:]).then_inc(g("c_trwd"), 1)
            for tc in range(LTC):
                t.wait_ge(g("c_rms"), 3 * (tc + 1))
                if tc >= 1:
                    t.wait_ge(g("c_bcc"), 2 * tc)
                t.transpose(ps[0:16, 4, 0:128], bnt[:, tc % 4, :], s_identf[:])
                t.transpose(ps[0:16, 5, 0:128], cnt[:, tc % 4, :], s_identf[:]).then_inc(g("c_bct"), 2)
                for rc in range(2):
                    i = tc * 2 + rc
                    if i >= 1:
                        t.wait_ge(g("c_dtrc"), i)
                    t.transpose(ps[:, 6, 0:128], dtr32[:, tc % 4, ts(rc, 128)], s_identf[:]).then_inc(g("c_dtt"), 1)
            t.wait_ge(g("c_evf"), 64)
            t.wait_ge(g("c_dtrc"), 64)
            t.wait_ge(g("c_wdtc"), 32)
            for i in range(2 * NKD):
                kd, hf = divmod(i, 2)
                if i >= 1:
                    t.wait_ge(g("c_sp0"), i)
                last = None
                for f in range(4):
                    for rc in range(2):
                        last = t.matmul(ps[:, f, :], wdtT[:, rc, ts(kd, 128)],
                                        dtrT[:, rc, hf * 2048 + f * 512:hf * 2048 + (f + 1) * 512],
                                        start=(rc == 0), stop=(rc == 1))
                last.then_inc(g("c_mm3"), 1)
            if debug_phase in (2, 21, 22):
                return
            def p4trs(jt):
                t.wait_ge(g("c_ev4"), jt + 1)
                for sub in range(4):
                    si = jt * 4 + sub
                    if si >= 2:
                        t.wait_ge(g("c_ev4b"), si - 1)
                    t.transpose(ps[:, 6 + si % 2, 0:128], f4st[:, jt % 2, :][:, ts(sub, 128)],
                                s_identf[:]).then_inc(g("c_tr4"), 1)
            for j in range(64):
                if j % 32 == 0:
                    wld(t, f"hact{j // 32}")
                wld(t, f"dwn{j}")
                if j >= 2:
                    t.wait_ge(g("c_ev4"), j - 1)
                pb = (j % 2) * 2
                last = None
                for k in range(64):
                    last = t.matmul(ps[:, pb, :], dwsl[:, j % 2, k, :],
                                    hacth[:, k, :], start=(k == 0), stop=(k == 63))
                last.then_inc(g("c_mm4"), 1)
                if j >= 1:
                    p4trs(j - 1)
            p4trs(63)
            if debug_phase <= 4:
                return
            wld(t, "y")
            t.wait_ge(g("c_gate"), 64)
            for hc in range(32):
                for qf in range(4):
                    ii = hc * 4 + qf
                    wld(t, f"wo{ii}")
                    for k in range(16):
                        i = ii * 16 + k
                        if i >= 2:
                            t.wait_ge(g("c_ev5"), i - 1)
                        t.transpose(ps[:, 4 + i % 2, 0:128], wosl[:, ii % 2, ts(k, 128)],
                                    s_identf[:]).then_inc(g("c_tr5"), 1)
                t.wait_ge(g("c_ev5"), (hc + 1) * 64)
                if hc >= 2:
                    t.wait_ge(g("c_ev5b"), hc - 1)
                pb = (hc % 2) * 2
                for th in range(2):
                    last = None
                    for k in range(64):
                        last = t.matmul(ps[:, pb + th, :], wline[:, k, :], ybuf[:, k, ts(th, 512)],
                                        start=(k == 0), stop=(k == 63))
                    last.then_inc(g("c_mm5"), 1)
                t.wait_ge(g("c_ev5b"), hc + 1)
                for tcc in range(8):
                    i = hc * 8 + tcc
                    if i >= 2:
                        t.wait_ge(g("c_asm"), i - 1)
                    t.transpose(ps[:, 6 + i % 2, 0:128], a5st[:, ts(tcc, 128)],
                                s_identf[:]).then_inc(g("c_tr5b"), 1)

        @blk.scalar
        def _(s):
            def xnf_copies(tcb):
                for hc in range(NKH):
                    i = tcb * NKH + hc
                    s.wait_ge(g("c_tr0"), i + 1)
                    s.copy(xnf[:, hc, ts(tcb, 128)], ps[:, i % 4, 0:128]).then_inc(g("c_xnf"), 1)
            for tc in range(NTC):
                if tc >= 2:
                    xnf_copies(tc - 2)
                wld(s, f"x{tc}")
                if tc >= 1:
                    s.wait_ge(g("c_lnr"), tc)
                if tc >= 2:
                    s.wait_ge(g("c_tr0"), 32 * (tc - 1))
                s.activation(xnstage[:, tc % 2, :], xt[:, tc % 2, :], AF.Square).then_inc(g("c_sq0"), 1)
                s.wait_ge(g("c_veps"), tc + 1)
                chs.bump(s.activation(lncols[:, 7:8], lncols[:, 6:7], AF.Ln))
                chs.join(s)
                s.activation(lncols[:, 7:8], lncols[:, 7:8], AF.Exp, scale=-0.5).then_inc(g("c_rstd"), 1)
            xnf_copies(NTC - 2)
            xnf_copies(NTC - 1)
            if debug_phase == 10 or debug_phase == 11:
                return
            for og in range(16):
                for jt in range(8):
                    idx = og * 8 + jt
                    j, th = divmod(jt, 2)
                    s.wait_ge(g("c_relu"), idx + 1)
                    s.activation(evst[:, og % 2, j, th * 512:(th + 1) * 512], rlst[:, idx % 2, :],
                                 AF.Square).then_inc(g("c_ev1"), 1)
            if debug_phase == 1:
                return
            for k in range(NKD):
                s.wait_ge(g("c_tap"), 4 * (k + 1))
                if k >= 1:
                    s.wait_ge(g("c_cvb"), k)
                s.activation(cvst[:, :], cacc[:, :], AF.Silu).then_inc(g("c_silu"), 1)
            if debug_phase in (21, 22):
                return
            wld(s, "ssml")
            for tc in range(LTC):
                if tc >= 1:
                    s.wait_ge(g("c_red3"), tc)
                s.activation(sqst[:, :], ssm2[:, tc, :], AF.Square).then_inc(g("c_sq2"), 1)
                s.wait_ge(g("c_vep3"), tc + 1)
                chs.bump(s.activation(rmcols[:, 3:4], rmcols[:, 3:4], AF.Ln))
                chs.bump(s.activation(rmcols[:, 4:5], rmcols[:, 4:5], AF.Ln))
                chs.bump(s.activation(rmcols[:, 5:6], rmcols[:, 5:6], AF.Ln))
                chs.join(s)
                s.activation(rmcols[:, 3:4], rmcols[:, 3:4], AF.Exp, scale=-0.5)
                s.activation(rmcols[:, 4:5], rmcols[:, 4:5], AF.Exp, scale=-0.5)
                s.activation(rmcols[:, 5:6], rmcols[:, 5:6], AF.Exp, scale=-0.5).then_inc(g("c_rst3"), 1)
            for i in range(2 * NKD):
                s.wait_ge(g("c_sp0"), i + 1)
                chs.bump(s.activation(spa[:, i % 2, :], spb[:, i % 2, :], AF.Abs))
                chs.join(s)
                chs.bump(s.activation(spa[:, i % 2, :], spa[:, i % 2, :], AF.Exp, scale=-1.0))
                chs.join(s)
                s.activation(spa[:, i % 2, :], spa[:, i % 2, :], AF.Ln, bias=1.0).then_inc(g("c_spa"), 1)
            if debug_phase in (2, 21, 22):
                return
            wld(s, "misc")
            chs.bump(s.activation(s_A[:, :, :], s_A[:, :, :], AF.Exp))
            chs.join(s)
            chs.bump(s.mul(s_A[:, :, :], s_A[:, :, :], -1.0))
            chs.join(s)
            for q in range(4):
                for k in range(NKD):
                    i = q * NKD + k
                    wld(s, f"sc{i}")
                    for n in range(NST):
                        idx = i * NST + n
                        if idx >= 3:
                            s.wait_ge(g("c_h"), idx - 2)
                        s.activation(abuf[:, idx % 3, :], dtq[:, i % 2, :], AF.Exp,
                                     scale=s_A[:, k, n:n + 1]).then_inc(g("c_a"), 1)
                    j = i - 3
                    if 0 <= j < 64:
                        if j >= 2:
                            wld(s, f"ffnT{j - 2}")
                        for sub in range(4):
                            si = j * 4 + sub
                            s.wait_ge(g("c_tr4"), si + 1)
                            s.copy(ftst[:, j % 2, sub, :], ps[:, 6 + si % 2, 0:128]).then_inc(g("c_ev4b"), 1)
            for j in (61, 62, 63):
                wld(s, f"ffnT{j - 2}")
                for sub in range(4):
                    si = j * 4 + sub
                    s.wait_ge(g("c_tr4"), si + 1)
                    s.copy(ftst[:, j % 2, sub, :], ps[:, 6 + si % 2, 0:128]).then_inc(g("c_ev4b"), 1)
            if debug_phase == 3:
                return
            for tc in range(NTC):
                wld(s, f"fft{tc}")
                if tc >= 1:
                    s.wait_ge(g("c_xp"), tc)
                s.activation(sqxt[:, :], fft[:, tc % 2, :], AF.Square).then_inc(g("c_sqx"), 1)
                s.wait_ge(g("c_vepx"), tc + 1)
                chs.bump(s.activation(xpcols[:, 7:8], xpcols[:, 6:7], AF.Ln))
                chs.join(s)
                s.activation(xpcols[:, 7:8], xpcols[:, 7:8], AF.Exp, scale=-0.5).then_inc(g("c_rstx"), 1)
            if debug_phase <= 4:
                return
            for i in range(64):
                wld(s, f"z{i}")
                if i >= 2:
                    s.wait_ge(g("c_gate"), i - 1)
                s.activation(zst[:, i % 2, :], zkt[:, i % 2, :], AF.Silu).then_inc(g("c_zs"), 1)
            for i in range(2048):
                s.wait_ge(g("c_tr5"), i + 1)
                s.copy(wline[:, i % 64, :], ps[:, 4 + i % 2, 0:128]).then_inc(g("c_ev5"), 1)

        @blk.vector
        def _(v):
            for tc in range(NTC):
                wld(v, f"x{tc}")
                chv.bump(v.reduce_sum(lncols[:, 0:1], xt[:, tc % 2, :], axis=AX))
                chv.join(v)
                chv.bump(v.tensor_scalar(lncols[:, 1:2], lncols[:, 0:1], 1.0 / H, None, ALU.mult))
                v.wait_ge(g("c_sq0"), tc + 1)
                v.reduce_sum(lncols[:, 2:3], xnstage[:, tc % 2, :], axis=AX).then_inc(g("c_lnr"), 1)
                chv.mark("c_lnr", tc + 1)
                chv.join(v)
                chv.bump(v.tensor_scalar(lncols[:, 3:4], lncols[:, 2:3], 1.0 / H, None, ALU.mult))
                chv.bump(v.tensor_tensor(lncols[:, 4:5], lncols[:, 1:2], lncols[:, 1:2], ALU.mult))
                chv.join(v)
                chv.bump(v.tensor_tensor(lncols[:, 5:6], lncols[:, 3:4], lncols[:, 4:5], ALU.subtract))
                chv.join(v)
                v.tensor_scalar(lncols[:, 6:7], lncols[:, 5:6], 1.0, EPS_LN, ALU.mult, ALU.add).then_inc(g("c_veps"), 1)
                chv.mark("c_veps", tc + 1)
                if tc >= 2:
                    v.wait_ge(g("c_tr0"), 32 * (tc - 1))
                v.wait_ge(g("c_rstd"), tc + 1)
                v.tensor_scalar(xnstage[:, tc % 2, :], xt[:, tc % 2, :], lncols[:, 1:2], lncols[:, 7:8],
                                ALU.subtract, ALU.mult).then_inc(g("c_ln"), 1)
            if debug_phase == 10:
                return
            for og in range(32):
                if og >= 2:
                    wld(v, f"exp{og - 2}")
                for jt in range(8):
                    j, th = divmod(jt, 2)
                    v.wait_ge(g("c_mm1"), EV_INP + og * 8 + jt + 1)
                    v.tensor_copy(evst[:, og % 2, j, th * 512:(th + 1) * 512], ps[:, jt, :]).then_inc(g("c_ev1"), 1)
            if debug_phase == 11:
                return
            for tc in range(NTC):
                v.wait_ge(g("c_mm1"), EV_GATE + tc + 1)
                v.tensor_copy(logit[:, :], ps[:, 6, 0:NE]).then_inc(g("c_ev1"), 1)
                chv.mark("c_ev1", EV_GATE + tc + 1)
                chv.join(v)
                chv.bump(v.reduce_max(l_m[:, 0:1], logit[:, :], axis=AX))
                chv.join(v)
                chv.bump(v.tensor_scalar(lgm[:, :], logit[:, :], l_m[:, 0:1], -1e30, ALU.is_ge, ALU.mult))
                chv.join(v)
                chv.bump(v.tensor_tensor(lgm[:, :], lgm[:, :], logit[:, :], ALU.add))
                chv.join(v)
                chv.bump(v.reduce_max(l_m[:, 1:2], lgm[:, :], axis=AX))
                chv.join(v)
                v.tensor_tensor(scols[:, tc:tc + 1], l_m[:, 0:1], l_m[:, 1:2], ALU.add).then_inc(g("c_scol"), 1)
            for tc in range(NTC):
                v.wait_ge(g("c_str"), tc + 1)
                v.tensor_copy(s_srow[0:1, ts(tc, 128)], ps[0:1, 7, 0:128]).then_inc(g("c_srow"), 1)
            for og in range(4):
                for jt in range(8):
                    j, th = divmod(jt, 2)
                    v.wait_ge(g("c_mm1"), EV_FC1 + og * 8 + jt + 1)
                    v.tensor_copy(xlat[:, og * 4 + j, th * 512:(th + 1) * 512], ps[:, jt, :]).then_inc(g("c_ev1"), 1)
            wld(v, "srep")
            for og in range(4):
                wld(v, f"exp{30 + og}" if og < 2 else f"expm{og - 2}")
                for jt in range(8):
                    j, th = divmod(jt, 2)
                    v.wait_ge(g("c_mm1"), EV_EXP + og * 8 + jt + 1)
                    v.tensor_tensor(evst[:, og % 2, j, th * 512:(th + 1) * 512], ps[:, jt, :],
                                    s_rep[:, th * 512:(th + 1) * 512], ALU.mult).then_inc(g("c_ev1"), 1)
            for og in range(16):
                wld(v, f"expm{2 + og}" if og < 2 else f"expu{og - 2}")
                for jt in range(8):
                    idx = og * 8 + jt
                    v.wait_ge(g("c_mm1"), EV_UP + idx + 1)
                    if idx >= 2:
                        v.wait_ge(g("c_ev1"), EV_UP + idx - 1)
                    v.tensor_scalar(rlst[:, idx % 2, :], ps[:, jt, :], 0.0, None, ALU.max).then_inc(g("c_relu"), 1)
            if debug_phase == 1:
                return
            for kd in range(NKD):
                v.wait_ge(g("c_trwx"), 3 * (kd + 1))
                v.tensor_copy(wxT[:, kd, 0:128], ps[:, 4, 0:128])
                v.tensor_copy(wxT[:, kd, 128:256], ps[:, 5, 0:128])
                v.tensor_copy(wxT[:, kd, 256:288], ps[:, 6, 0:32]).then_inc(g("c_wxc"), 3)
            wld(v, "xd")
            wld(v, "misc")
            for k in range(NKD):
                v.tensor_scalar(cacc[:, :], xdb[:, k, :], s_convw[:, k, 3:4], s_convb[:, k:k + 1],
                                ALU.mult, ALU.add).then_inc(g("c_tap"), 1)
                chv.mark("c_tap", 4 * k + 1)
                for kk in range(3):
                    o = 3 - kk
                    chv.join(v)
                    v.scalar_tensor_tensor(cacc[:, o:L], xdb[:, k, 0:L - o], s_convw[:, k, kk:kk + 1],
                                           cacc[:, o:L], ALU.mult, ALU.add).then_inc(g("c_tap"), 1)
                    chv.mark("c_tap", 4 * k + 1 + kk + 1)
                v.wait_ge(g("c_silu"), k + 1)
                v.tensor_copy(xdb[:, k, :], cvst[:, :]).then_inc(g("c_cvb"), 1)
            for tc in range(LTC):
                v.wait_ge(g("c_mm2"), tc + 1)
                if tc >= 8:
                    wld(v, f"ssm{tc // 8 - 1}")
                v.tensor_copy(ssmst[:, tc % 8, :], ps[:, tc % 4, 0:R + 2 * NST]).then_inc(g("c_ev2"), 1)
            if debug_phase in (21, 22):
                return
            for i in range(64):
                hc, th = divmod(i, 2)
                v.wait_ge(g("c_mmf"), i + 1)
                if i >= 2:
                    wld(v, f"rout{i - 2}")
                v.tensor_copy(fc2st[:, i % 2, :], ps[:, i % 4, :]).then_inc(g("c_evf"), 1)
            for kd in range(NKD):
                for rc in range(2):
                    i = kd * 2 + rc
                    v.wait_ge(g("c_trwd"), i + 1)
                    v.tensor_copy(wdtT[:, rc, ts(kd, 128)], ps[:, 7, 0:128]).then_inc(g("c_wdtc"), 1)
            def bc_dtr_copies(tcb):
                v.wait_ge(g("c_bct"), 2 * (tcb + 1))
                v.tensor_copy(btst[:, tcb, :], ps[0:16, 4, 0:128])
                v.tensor_copy(ctst[:, tcb, :], ps[0:16, 5, 0:128]).then_inc(g("c_bcc"), 2)
                for rc in range(2):
                    i2 = tcb * 2 + rc
                    v.wait_ge(g("c_dtt"), i2 + 1)
                    v.tensor_copy(dtrT[:, rc, ts(tcb, 128)], ps[:, 6, 0:128]).then_inc(g("c_dtrc"), 1)
            wld(v, "ssml")
            for tc in range(LTC):
                v.wait_ge(g("c_sq2"), tc + 1)
                if tc >= 4:
                    v.wait_ge(g("c_dtt"), 2 * (tc - 3))
                    v.wait_ge(g("c_bct"), 2 * (tc - 3))
                v.reduce_sum(rmcols[:, 0:1], sqst[:, 0:R], axis=AX)
                v.reduce_sum(rmcols[:, 1:2], sqst[:, R:R + NST], axis=AX)
                v.reduce_sum(rmcols[:, 2:3], sqst[:, R + NST:], axis=AX).then_inc(g("c_red3"), 1)
                chv.mark("c_red3", tc + 1)
                chv.join(v)
                v.tensor_scalar(rmcols[:, 3:4], rmcols[:, 0:1], 1.0 / R, EPS_RMS, ALU.mult, ALU.add)
                v.tensor_scalar(rmcols[:, 4:5], rmcols[:, 1:2], 1.0 / NST, EPS_RMS, ALU.mult, ALU.add)
                v.tensor_scalar(rmcols[:, 5:6], rmcols[:, 2:3], 1.0 / NST, EPS_RMS, ALU.mult, ALU.add).then_inc(g("c_vep3"), 1)
                v.wait_ge(g("c_rst3"), tc + 1)
                v.scalar_tensor_tensor(dtr32[:, tc % 4, :], ssm2[:, tc, 0:R], rmcols[:, 3:4],
                                       s_gamr[:, 0:R], ALU.mult, ALU.mult).then_inc(g("c_rms"), 1)
                v.scalar_tensor_tensor(bnt[:, tc % 4, :], ssm2[:, tc, R:R + NST], rmcols[:, 4:5],
                                       s_gamr[:, R:R + NST], ALU.mult, ALU.mult).then_inc(g("c_rms"), 1)
                v.scalar_tensor_tensor(cnt[:, tc % 4, :], ssm2[:, tc, R + NST:], rmcols[:, 5:6],
                                       s_gamr[:, R + NST:], ALU.mult, ALU.mult).then_inc(g("c_rms"), 1)
                if tc >= 1:
                    bc_dtr_copies(tc - 1)
            bc_dtr_copies(LTC - 1)
            for i in range(2 * NKD):
                kd, hf = divmod(i, 2)
                v.wait_ge(g("c_mm3"), i + 1)
                if i >= 2:
                    wld(v, f"dtsp{i - 2}")
                v.tensor_scalar(spb[:, i % 2, :], ps[:, 0:4, :].rearrange("p b f -> p (b f)"),
                                s_dtbias[:, kd:kd + 1], None, ALU.add).then_inc(g("c_sp0"), 1)
                chv.mark("c_sp0", i + 1)
                v.wait_ge(g("c_spa"), i + 1)
                chv.join(v)
                chv.bump(v.tensor_scalar(spb[:, i % 2, :], spb[:, i % 2, :], 0.0, None, ALU.max))
                chv.join(v)
                v.tensor_tensor(dtst[:, i % 2, :], spb[:, i % 2, :], spa[:, i % 2, :], ALU.add).then_inc(g("c_sp"), 1)
            if debug_phase in (2, 21, 22):
                return
            v.memset(s_hstate[:, :], 0.0)
            for q in range(4):
                for k in range(NKD):
                    i = q * NKD + k
                    wld(v, f"sc{i}")
                    v.tensor_tensor(dtxq[:, :], dtq[:, i % 2, :], xcq[:, i % 2, :], ALU.mult).then_inc(g("c_dtx"), 1)
                    chv.mark("c_dtx", i + 1)
                    chv.join(v)
                    for n in range(NST):
                        idx = i * NST + n
                        kn = k * NST + n
                        v.wait_ge(g("c_a"), idx + 1)
                        wld(v, f"bc{q}_{n}")
                        v.tensor_tensor(bbuf[:, :], dtxq[:, :], brep[:, n, :], ALU.mult).then_inc(g("c_b"), 1)
                        chv.mark("c_b", i * NST + n + 1)
                        if n >= 1:
                            pn = n - 1
                            v.wait_ge(g("c_h"), idx)
                            v.tensor_copy(s_hstate[:, k * NST + pn:k * NST + pn + 1], hbuf[:, T - 1:T])
                            if pn == 0:
                                v.tensor_tensor(yac[:, :], hbuf[:, :], crep[:, 0, :], ALU.mult)
                            else:
                                if pn >= 3:
                                    v.wait_ge(g("c_y"), i * 15 + pn - 2)
                                v.tensor_tensor(hcb[:, pn % 2, :], hbuf[:, :], crep[:, pn, :], ALU.mult).then_inc(g("c_hc"), 1)
                        chv.join(v)
                        v.tensor_tensor_scan(hbuf[:, :], abuf[:, idx % 3, :], bbuf[:, :],
                                             s_hstate[:, kn:kn + 1], ALU.mult, ALU.add).then_inc(g("c_h"), 1)
                        chv.mark("c_h", idx + 1)
                    pn = NST - 1
                    v.wait_ge(g("c_h"), i * NST + NST)
                    v.tensor_copy(s_hstate[:, k * NST + pn:k * NST + pn + 1], hbuf[:, T - 1:T])
                    v.wait_ge(g("c_y"), i * 15 + pn - 2)
                    v.tensor_tensor(hcb[:, pn % 2, :], hbuf[:, :], crep[:, pn, :], ALU.mult).then_inc(g("c_hc"), 1)
                    v.wait_ge(g("c_y"), (i + 1) * 15)
                    if i >= 2:
                        wld(v, f"yexp{i - 2}")
                    v.scalar_tensor_tensor(y2t[:, :], xcq[:, i % 2, :], s_dpar[:, k:k + 1], yac[:, :],
                                           ALU.mult, ALU.add).then_inc(g("c_fin"), 1)
                    j = i - 2
                    if 0 <= j < 64:
                        v.wait_ge(g("c_mm4"), j + 1)
                        wld(v, f"routt{j}")
                        if j >= 2:
                            v.wait_ge(g("c_ev4b"), 4 * (j - 1))
                        pb4 = (j % 2) * 2
                        v.scalar_tensor_tensor(f4st[:, j % 2, :], ps[:, pb4, :], 1.0, routt[:, j % 3, :],
                                               ALU.mult, ALU.add).then_inc(g("c_ev4"), 1)
            for j in (62, 63):
                v.wait_ge(g("c_mm4"), j + 1)
                wld(v, f"routt{j}")
                v.wait_ge(g("c_ev4b"), 4 * (j - 1))
                pb4 = (j % 2) * 2
                v.scalar_tensor_tensor(f4st[:, j % 2, :], ps[:, pb4, :], 1.0, routt[:, j % 3, :],
                                       ALU.mult, ALU.add).then_inc(g("c_ev4"), 1)
            if debug_phase == 3:
                return
            wld(v, "xpg")
            for tc in range(NTC):
                wld(v, f"xrt{tc}")
                v.wait_ge(g("c_sqx"), tc + 1)
                if tc >= 2:
                    wld(v, f"xpo{tc - 2}")
                chv.bump(v.reduce_sum(xpcols[:, 0:1], fft[:, tc % 2, :], axis=AX))
                chv.join(v)
                chv.bump(v.tensor_scalar(xpcols[:, 1:2], xpcols[:, 0:1], 1.0 / H, None, ALU.mult))
                chv.bump(v.reduce_sum(xpcols[:, 2:3], sqxt[:, :], axis=AX))
                chv.join(v)
                chv.bump(v.tensor_scalar(xpcols[:, 3:4], xpcols[:, 2:3], 1.0 / H, None, ALU.mult))
                chv.bump(v.tensor_tensor(xpcols[:, 4:5], xpcols[:, 1:2], xpcols[:, 1:2], ALU.mult))
                chv.join(v)
                chv.bump(v.tensor_tensor(xpcols[:, 5:6], xpcols[:, 3:4], xpcols[:, 4:5], ALU.subtract))
                chv.join(v)
                v.tensor_scalar(xpcols[:, 6:7], xpcols[:, 5:6], 1.0, EPS_LN, ALU.mult, ALU.add).then_inc(g("c_vepx"), 1)
                chv.mark("c_vepx", tc + 1)
                v.wait_ge(g("c_rstx"), tc + 1)
                chv.bump(v.tensor_scalar(xpt[:, tc % 2, :], fft[:, tc % 2, :], xpcols[:, 1:2], xpcols[:, 7:8],
                                         ALU.subtract, ALU.mult))
                chv.join(v)
                chv.bump(v.tensor_tensor(xpt[:, tc % 2, :], xpt[:, tc % 2, :], gbr[:, 0, :], ALU.mult))
                chv.join(v)
                chv.bump(v.tensor_tensor(xpt[:, tc % 2, :], xpt[:, tc % 2, :], gbr[:, 1, :], ALU.add))
                chv.join(v)
                v.tensor_tensor(xpt[:, tc % 2, :], xpt[:, tc % 2, :], xrt[:, tc % 2, :], ALU.add).then_inc(g("c_xp"), 1)
            if debug_phase <= 4:
                return
            wld(v, "y")
            for i in range(64):
                v.wait_ge(g("c_zs"), i + 1)
                v.tensor_tensor(ybuf[:, i, :], ybuf[:, i, :], zst[:, i % 2, :], ALU.mult).then_inc(g("c_gate"), 1)
            for hc in range(32):
                v.wait_ge(g("c_mm5"), 2 * (hc + 1))
                if hc >= 1:
                    v.wait_ge(g("c_tr5b"), 8 * hc)
                pb = (hc % 2) * 2
                v.tensor_copy(a5st[:, :], ps[:, pb:pb + 2, :].rearrange("p b f -> p (b f)")).then_inc(g("c_ev5b"), 1)
                wld(v, f"xpc{hc}")
                if hc >= 2:
                    wld(v, f"out{hc - 2}")
                for tcc in range(8):
                    i = hc * 8 + tcc
                    v.wait_ge(g("c_tr5b"), i + 1)
                    v.tensor_tensor(ost[:, hc % 2, tcc, :], ps[:, 6 + i % 2, 0:128], xpcs[:, hc % 2, tcc, :],
                                    ALU.add).then_inc(g("c_asm"), 1)

    return nc


_CACHE = {}


def kernel(**inputs):
    nc = _CACHE.get("nc99")
    if nc is None:
        nc = build_nc()
        _CACHE["nc99"] = nc
    in_maps = make_in_maps(inputs)
    res = run_bass_kernel_spmd(nc, in_maps, core_ids=list(range(8)))
    out = np.empty((B, L, H), dtype=np.float32)
    for c in range(8):
        b, r = divmod(c, 4)
        out[b, r * T:(r + 1) * T, :] = res.results[c]["out_tok"]
    return out


def make_in_maps(inputs):
    f32 = np.float32
    X = np.ascontiguousarray(np.asarray(inputs["L0_attn_hidden_states"], dtype=f32))
    asf = lambda k: np.ascontiguousarray(np.asarray(inputs[k], dtype=f32))
    def blockw(w, ogw):
        KK, CC = w.shape
        nk, nog = KK // 128, CC // ogw
        return np.ascontiguousarray(w.reshape(nk, 128, nog, ogw).transpose(2, 1, 0, 3).reshape(nog, 128, nk * ogw))
    shared = dict(
        w_in=blockw(asf("L0_attn_in_proj_w"), 512),
        outw=asf("L0_attn_out_proj_w"),
        dtlnw=asf("L0_attn_dt_ln_w").reshape(1, R),
        blnw=asf("L0_attn_B_ln_w").reshape(1, NST),
        clnw=asf("L0_attn_C_ln_w").reshape(1, NST),
        gatew=np.ascontiguousarray(asf("L0_ffn_gate_w").reshape(NKH, 128, NE).transpose(1, 0, 2).reshape(128, NKH * NE)),
        upw=blockw(asf("L0_ffn_up_w"), 512),
        downw=blockw(asf("L0_ffn_down_w"), 128),
        fc1w=blockw(asf("L0_ffn_fc1_w"), 512),
        fc2w=blockw(asf("L0_ffn_fc2_w"), 128),
        expw=blockw(asf("L0_ffn_expert_w"), 512),
        lngamma=asf("L0_ffn_ln_gamma").reshape(1, H),
        lnbeta=asf("L0_ffn_ln_beta").reshape(1, H),
        identf=np.eye(128, dtype=f32),
    )
    cw = asf("L0_attn_conv_w"); cb = asf("L0_attn_conv_b")
    xpw = asf("L0_attn_x_proj_w"); dpw = asf("L0_attn_dt_proj_w")
    dtb = asf("L0_attn_dt_bias"); al = asf("L0_attn_A_log"); dp = asf("L0_attn_D_param")
    in_maps = []
    for c in range(8):
        b, r = divmod(c, 4)
        sl = slice(r * DSL, (r + 1) * DSL)
        m = dict(shared)
        m["x_tok"] = np.ascontiguousarray(X[b, r * T:(r + 1) * T, :])
        m["convw"] = np.ascontiguousarray(cw[sl])
        m["convb"] = np.ascontiguousarray(cb[sl].reshape(NKD, 128).T)
        m["xprojw"] = np.ascontiguousarray(xpw[:, sl])
        m["dtprojw"] = np.ascontiguousarray(dpw[sl])
        m["dtbias"] = np.ascontiguousarray(dtb[sl].reshape(NKD, 128).T)
        m["alog"] = np.ascontiguousarray(al[sl])
        m["dpar"] = np.ascontiguousarray(dp[sl].reshape(NKD, 128).T)
        in_maps.append(m)
    return in_maps


# version 04
# speedup vs baseline: 1.0872x; 1.0446x over previous
import numpy as np
import ml_dtypes
import concourse.bass as bass
import concourse.mybir as mybir
from concourse.bass_utils import run_bass_kernel_spmd

F32 = mybir.dt.float32
BF16 = mybir.dt.bfloat16
AF = mybir.ActivationFunctionType
ALU = mybir.AluOpType
SP = mybir.EngineType.SP
AX = mybir.AxisListType.X

H = 4096; DIN = 8192; NST = 16; KCV = 4; R = 256
B = 2; L = 4096; T = 1024; DSL = 2048; E2 = 2 * DIN
NE = 8; INT = 8192; LAT = 2048
EPS_LN = 1e-5; EPS_RMS = 1e-6
NTC = 8; LTC = 32; NKH = 32; NKD = 16
RG = [[0, 1, 2, 3], [4, 5, 6, 7]]
ts = bass.ts

EV_INP, EV_GATE, EV_FC1, EV_EXP, EV_UP, EV_END = 0, 256, 264, 296, 328, 456


def build_nc(debug_phase=99, race_detect=True):
    nc = bass.Bass("TRN2", target_bir_lowering=False, detect_race_conditions=race_detect)

    def I(name, shape):
        return nc.dram_tensor(name, shape, F32, kind="ExternalInput").ap()
    x_tok = I("x_tok", [T, H]); w_in = I("w_in", [32, 128, E2])
    convw = I("convw", [DSL, KCV]); convb = I("convb", [128, NKD])
    xprojw = I("xprojw", [R + 2 * NST, DSL]); dtprojw = I("dtprojw", [DSL, R])
    dtbias = I("dtbias", [128, NKD]); alog = I("alog", [DSL, NST]); dpar = I("dpar", [128, NKD])
    outwT = I("outwT", [32, 128, 8192])
    dtlnw = I("dtlnw", [1, R]); blnw = I("blnw", [1, NST]); clnw = I("clnw", [1, NST])
    gatew = I("gatew", [128, NKH * NE]); upw = I("upw", [16, 128, 2 * INT]); downw = I("downw", [32, 128, INT // 128 * 128])
    fc1w = I("fc1w", [4, 128, NKH * 512]); fc2w = I("fc2w", [32, 128, LAT]); expw = I("expw", [4, 128, 16 * 512])
    lngamma = I("lngamma", [1, H]); lnbeta = I("lnbeta", [1, H])
    identf = I("identf", [128, 128])

    out_tok = nc.dram_tensor("out_tok", [T, H], F32, kind="ExternalOutput").ap()
    dbg_out = None
    if debug_phase < 99:
        shp = {1: ([DIN, T], BF16), 2: ([DSL, L], BF16), 3: ([4 * DSL, T], BF16),
               4: ([T, H], F32), 10: ([128, NKH * T], BF16), 11: ([DIN, T], BF16),
               21: ([DSL, L], BF16), 22: ([L, R + 2 * NST], F32)}[debug_phase]
        dbg_out = nc.dram_tensor("dbg_out", shp[0], shp[1], kind="ExternalOutput").ap()

    D = nc.dram_tensor
    ccA_in = D("ccA_in", [DIN, T], BF16, kind="Internal").ap()
    ccA_out = D("ccA_out", [4 * DIN, T], BF16, kind="Internal").ap()
    ccB_in = D("ccB_in", [4 * DSL, T], BF16, kind="Internal").ap()
    ccB_out = D("ccB_out", [16 * DSL, T], BF16, kind="Internal").ap()
    ccC_in = D("ccC_in", [L, R + 2 * NST], F32, kind="Internal").ap()
    ccC_out = D("ccC_out", [L, R + 2 * NST], F32, kind="Internal").ap()
    zbufd = D("zbufd", [DIN, T], BF16, kind="Internal").ap()
    hactd = D("hactd", [INT, T], BF16, kind="Internal").ap()
    moed = D("moed", [LAT, T], BF16, kind="Internal").ap()
    routd = D("routd", [H, T], BF16, kind="Internal").ap()
    dtspd = D("dtspd", [DSL, L], BF16, kind="Internal").ap()
    xcspd = D("xcspd", [DSL, L], BF16, kind="Internal").ap()
    ffnTd = D("ffnTd", [T, H], BF16, kind="Internal").ap()
    xplusd = D("xplusd", [T, H], F32, kind="Internal").ap()
    browd = D("browd", [1, NST * L], BF16, kind="Internal").ap()
    crowd = D("crowd", [1, NST * L], BF16, kind="Internal").ap()
    srowd = D("srowd", [1, T], BF16, kind="Internal").ap()

    ps = nc.alloc_psum_tensor([128, 8, 512], F32)

    AL = nc.alloc_sbuf_tensor
    s_identf = AL([128, 128], F32)
    s_A = AL([128, NKD, NST], F32)
    s_convw = AL([128, NKD, KCV], F32)
    s_convb = AL([128, NKD], F32)
    s_dtbias = AL([128, NKD], F32)
    s_dpar = AL([128, NKD], F32)
    s_hstate = AL([128, NKD * NST], F32)
    s_gamr = AL([128, R + 2 * NST], F32)
    s_srow = AL([1, T], BF16)
    gsb = AL([128, NKH, NE], BF16)
    base0 = nc.sbuf_base

    xnf = AL([128, NKH, T], BF16)
    wslab = AL([128, 2, NKH, 512], BF16)
    ovl0 = nc.sbuf_base
    xt = AL([128, 2, H], F32)
    xnstage = AL([128, 2, H], F32)
    lncols = AL([128, 16], F32)
    p0_end = nc.sbuf_base
    nc.sbuf_base = ovl0
    evst = AL([128, 2, 4, 1024], BF16)
    xlat = AL([128, 16, T], BF16)
    rlst = AL([128, 2, 512], BF16)
    s_rep = AL([128, T], BF16)
    logit = AL([128, NE], F32)
    lgm = AL([128, NE], F32)
    l_m = AL([128, 4], F32)
    scols = AL([128, NTC], F32)
    p1_end = nc.sbuf_base

    nc.sbuf_base = base0
    xdb = AL([128, NKD, L], BF16)
    cacc = AL([128, L], F32)
    cvst = AL([128, L], BF16)
    wxa = AL([128, DSL], F32)
    wxb = AL([128, DSL], F32)
    wxc = AL([32, DSL], F32)
    wxT = AL([128, NKD, R + 2 * NST], BF16)
    ssmst = AL([128, 8, R + 2 * NST], F32)
    p2a_end = nc.sbuf_base

    nc.sbuf_base = base0
    ssm2 = AL([128, LTC, R + 2 * NST], F32)
    sqst = AL([128, R + 2 * NST], F32)
    rmcols = AL([128, 8], F32)
    dtr32 = AL([128, 4, R], F32)
    bnt = AL([128, 4, NST], F32)
    cnt = AL([128, 4, NST], F32)
    btst = AL([16, LTC, 128], BF16)
    ctst = AL([16, LTC, 128], BF16)
    dtrT = AL([128, 2, L], BF16)
    wdtnat = AL([128, NKD, R], F32)
    wdtT = AL([128, 2, DSL], BF16)
    moesb = AL([128, 16, T], BF16)
    fcsl = AL([128, 2, 16, 128], BF16)
    fc2st = AL([128, 2, 512], BF16)
    spb = AL([128, 2, DSL], F32)
    spa = AL([128, 2, DSL], F32)
    dtst = AL([128, 2, DSL], BF16)
    p2b_end = nc.sbuf_base

    nc.sbuf_base = base0
    brep = AL([128, NST, T], BF16)
    crep = AL([128, NST, T], BF16)
    dtq = AL([128, 2, T], BF16)
    xcq = AL([128, 2, T], BF16)
    dtxq = AL([128, T], BF16)
    abuf = AL([128, 3, T], BF16)
    bbuf = AL([128, T], BF16)
    hbuf = AL([128, T], BF16)
    hcb = AL([128, 2, T], BF16)
    yac = AL([128, T], F32)
    y2t = AL([128, T], F32)
    hacth = AL([128, 64, 512], BF16)
    dwsl = AL([128, 2, 64, 128], BF16)
    routt = AL([128, 3, 512], BF16)
    f4st = AL([128, 2, 512], F32)
    ftst = AL([128, 2, 4, 128], BF16)
    p34_end = nc.sbuf_base

    nc.sbuf_base = base0
    gbr = AL([128, 2, H], F32)
    fft = AL([128, 2, H], BF16)
    xrt = AL([128, 2, H], F32)
    xpt = AL([128, 2, H], F32)
    sqxt = AL([128, H], F32)
    xpcols = AL([128, 8], F32)
    p4b_end = nc.sbuf_base

    nc.sbuf_base = base0
    ybuf = AL([128, 64, T], BF16)
    wosl = AL([128, 2, 2048], F32)
    wline = AL([128, 64, 128], BF16)
    zkt = AL([128, 2, T], BF16)
    zst = AL([128, 2, T], BF16)
    xpcs = AL([128, 2, 8, 128], F32)
    ost = AL([128, 2, 8, 128], F32)
    a5st = AL([128, 1024], F32)
    p5_end = nc.sbuf_base

    LIMIT = 229344
    for nm, e in [("p0", p0_end), ("p1", p1_end), ("p2a", p2a_end), ("p2b", p2b_end),
                  ("p34", p34_end), ("p4b", p4b_end), ("p5", p5_end)]:
        assert e <= LIMIT, f"SBUF overflow {nm}: {e} > {LIMIT}"
    nc.sbuf_base = max(p0_end, p1_end, p2a_end, p2b_end, p34_end, p4b_end, p5_end)

    names = """q_misc q_x c_sq0 c_lnr c_ln c_tr0 c_xnf q_w0 q_w1 q_gw c_mm1 c_ev1 q_exp
    c_scol c_str c_srow q_srep c_relu q_cca q_xd c_tap c_silu c_cvb
    q_wx c_trwx c_wxc c_trwd c_wdtc c_mm2 c_ev2 q_ssm q_ccc q_ssml c_sq2 c_rms
    c_bct c_bcc c_dtt c_dtrc q_brow q_moe2 q_fcs c_mmf c_evf q_rout c_mm3 c_sp0 c_spa c_sp
    q_dtsp q_xcsp q_bc q_sc c_dtx c_a c_b c_h c_hc c_y c_fin q_yexp q_ccb
    q_hact q_dwn q_routt c_mm4 c_ev4 c_tr4 c_ev4b q_ffnT
    q_xp_in c_sqx c_xp q_xp_out q_y q_z c_zs c_gate q_wo0 q_wo1 c_tr5 c_ev5 c_mm5
    c_ev5b c_tr5b q_xpc c_asm q_out
    c_veps c_rstd c_red3 c_vep3 c_rst3 c_vepx c_rstx ch_v ch_s ch_p""".split()
    S = {n: nc.alloc_semaphore(n) for n in names}
    g = lambda n: S[n]

    LD = {}

    class DQ:
        def __init__(self, eng, semname):
            self.eng, self.sem, self.n = eng, S[semname], 0

        def dma(self, name, out, in_, **kw):
            if self.n >= 1:
                self.eng.wait_ge(self.sem, 16 * self.n)
            self.n += 1
            r = self.eng.dma_start(out=out, in_=in_, **kw).then_inc(self.sem, 16)
            if name is not None:
                LD[name] = (self.sem, 16 * self.n)
            return r

        def flush(self):
            if self.n >= 1:
                self.eng.wait_ge(self.sem, 16 * self.n)

    def wld(eng, name):
        sem, cnt = LD[name]
        eng.wait_ge(sem, cnt)

    class Chain:
        def __init__(self, semname):
            self.sem, self.n, self.last = S[semname], 0, None

        def bump(self, ins):
            ins.then_inc(self.sem, 1)
            self.n += 1
            self.last = (self.sem, self.n)
            return ins

        def mark(self, semname, val):
            self.last = (S[semname], val)

        def join(self, eng):
            if self.last is not None:
                eng.wait_ge(*self.last)

    chv, chs, chp = Chain("ch_v"), Chain("ch_s"), Chain("ch_p")

    dwn_sched = [(th, hc) for th in range(2) for hc in range(32)]

    with nc.Block() as blk:

        @blk.sync
        def _(sy):
            pid = nc.partition_id([SP])
            myg = pid % 4
            dq_misc = DQ(sy, "q_misc"); dq_x = DQ(sy, "q_x"); dq_exp = DQ(sy, "q_exp")
            dq_srep = DQ(sy, "q_srep"); dq_xd = DQ(sy, "q_xd"); dq_ssm = DQ(sy, "q_ssm")
            dq_xcsp = DQ(sy, "q_xcsp"); dq_ssml = DQ(sy, "q_ssml"); dq_moe2 = DQ(sy, "q_moe2")
            dq_brow = DQ(sy, "q_brow"); dq_rout = DQ(sy, "q_rout"); dq_dtsp = DQ(sy, "q_dtsp")
            dq_bc = DQ(sy, "q_bc"); dq_sc = DQ(sy, "q_sc"); dq_hact = DQ(sy, "q_hact")
            dq_routt = DQ(sy, "q_routt"); dq_ffnT = DQ(sy, "q_ffnT"); dq_xpin = DQ(sy, "q_xp_in")
            dq_xpout = DQ(sy, "q_xp_out"); dq_y = DQ(sy, "q_y"); dq_z = DQ(sy, "q_z")
            dq_wo = [DQ(sy, "q_wo0"), DQ(sy, "q_wo1")]
            dq_xpc = DQ(sy, "q_xpc"); dq_out = DQ(sy, "q_out")

            dq_misc.dma(None, out=s_identf[:], in_=identf)
            dq_misc.dma(None, out=s_convw[:, :, :], in_=convw.rearrange("(k p) j -> p k j", p=128))
            dq_misc.dma(None, out=s_convb[:, :], in_=convb)
            dq_misc.dma(None, out=s_dtbias[:, :], in_=dtbias)
            dq_misc.dma(None, out=s_dpar[:, :], in_=dpar)
            dq_misc.dma(None, out=s_A[:, :, :], in_=alog.rearrange("(k p) n -> p k n", p=128))
            dq_misc.dma(None, out=s_gamr[:, 0:R], in_=dtlnw.partition_broadcast(128))
            dq_misc.dma(None, out=s_gamr[:, R:R + NST], in_=blnw.partition_broadcast(128))
            dq_misc.dma("misc", out=s_gamr[:, R + NST:], in_=clnw.partition_broadcast(128))
            for tc in range(NTC):
                if tc >= 2:
                    sy.wait_ge(g("c_ln"), tc - 1)
                dq_x.dma(f"x{tc}", out=xt[:, tc % 2, :], in_=x_tok[ts(tc, 128), :])
            if debug_phase == 10:
                sy.wait_ge(g("c_xnf"), 256)
                dq_out.dma(None, out=dbg_out, in_=xnf[:, :, :].rearrange("p a b -> p (a b)"))
                dq_out.flush()
                return
            for og in range(32):
                sy.wait_ge(g("c_ev1"), EV_INP + 8 * (og + 1))
                dst, ro = (ccA_in, og * 512) if og < 16 else (zbufd, (og - 16) * 512)
                dq_exp.dma(f"exp{og}", out=dst[ro:ro + 512, :].rearrange("(j p) t -> p j t", p=128),
                           in_=evst[:, og % 2, :, :])
            if debug_phase == 11:
                dq_exp.flush()
                dq_out.dma(None, out=dbg_out, in_=zbufd)
                dq_out.flush()
                return
            sy.wait_ge(g("c_srow"), NTC)
            dq_srep.dma(None, out=srowd, in_=s_srow[:, :])
            dq_srep.dma("srep", out=s_rep[:, :], in_=srowd.partition_broadcast(128))
            for og in range(4):
                sy.wait_ge(g("c_ev1"), EV_EXP + 8 * (og + 1))
                dq_exp.dma(f"expm{og}", out=moed[og * 512:(og + 1) * 512, :].rearrange("(j p) t -> p j t", p=128),
                           in_=evst[:, og % 2, :, :])
            for og in range(16):
                sy.wait_ge(g("c_ev1"), EV_UP + 8 * (og + 1))
                dq_exp.dma(f"expu{og}", out=hactd[og * 512:(og + 1) * 512, :].rearrange("(j p) t -> p j t", p=128),
                           in_=evst[:, og % 2, :, :])
            if debug_phase == 1:
                dq_exp.flush()
                sy.wait_ge(g("q_cca"), 1)
                dq_out.dma(None, out=dbg_out, in_=zbufd)
                dq_out.flush()
                return

            sy.wait_ge(g("q_cca"), 1)
            sy.wait_ge(g("c_mm1"), EV_END)
            dq_exp.flush()
            for rr in range(4):
                srcx = ccA_out[rr * DIN:(rr + 1) * DIN, :].rearrange("(gg kd p) t -> gg p kd t", gg=4, p=128)[ts(myg, 1), :, :, :]
                dq_xd.dma("xd" if rr == 3 else None,
                          out=xdb[:, :, rr * 1024:(rr + 1) * 1024], in_=srcx)
            for gse in range(4):
                sy.wait_ge(g("c_ev2"), 8 * (gse + 1))
                dq_ssm.dma(f"ssm{gse}", out=ccC_in[gse * 1024:(gse + 1) * 1024, :].rearrange("(tc p) e -> p tc e", p=128),
                           in_=ssmst[:, :, :])
            sy.wait_ge(g("c_mm2"), LTC)
            dq_xcsp.dma("xcsp", out=xcspd.rearrange("(k p) t -> p k t", p=128), in_=xdb[:, :, :])
            if debug_phase == 21:
                sy.wait_ge(g("q_xcsp"), 16)
                dq_out.dma(None, out=dbg_out, in_=xcspd)
                dq_out.flush()
                return
            sy.wait_ge(g("q_xcsp"), 16)
            dq_moe2.dma("moe2", out=moesb[:, :, :], in_=moed.rearrange("(k p) t -> p k t", p=128))
            sy.wait_ge(g("q_ccc"), 1)
            if debug_phase == 22:
                dq_out.dma(None, out=dbg_out, in_=ccC_out)
                dq_out.flush()
                return
            dq_ssml.dma("ssml", out=ssm2[:, :, :], in_=ccC_out.rearrange("(tc p) e -> p tc e", p=128))
            for i in range(64):
                hc, th = divmod(i, 2)
                sy.wait_ge(g("c_evf"), i + 1)
                dq_rout.dma(f"rout{i}", out=routd[ts(hc, 128), ts(th, 512)], in_=fc2st[:, i % 2, :])
            sy.wait_ge(g("c_bcc"), 2 * LTC)
            dq_brow.dma(None, out=browd.rearrange("one (n tc j) -> one n tc j", n=NST, tc=LTC),
                        in_=btst[:, :, :])
            dq_brow.dma("brow", out=crowd.rearrange("one (n tc j) -> one n tc j", n=NST, tc=LTC),
                        in_=ctst[:, :, :])
            for i in range(2 * NKD):
                kd, hf = divmod(i, 2)
                sy.wait_ge(g("c_sp"), i + 1)
                dq_dtsp.dma(f"dtsp{i}", out=dtspd[ts(kd, 128), ts(hf, 2048)], in_=dtst[:, i % 2, :])
            if debug_phase == 2:
                dq_dtsp.flush()
                dq_out.dma(None, out=dbg_out, in_=dtspd)
                dq_out.flush()
                return

            dq_dtsp.flush()
            dq_brow.flush()
            dq_rout.flush()
            dq_hact.dma("hact0", out=hacth[:, :, :],
                        in_=hactd.rearrange("(k p) t -> p k t", p=128)[:, :, 0:512])

            def p4_interleave(i):
                if i < 64:
                    j = i
                    th4, hc4 = dwn_sched[j]
                    if j >= 3:
                        sy.wait_ge(g("c_ev4"), j - 2)
                    dq_routt.dma(f"routt{j}", out=routt[:, j % 3, :], in_=routd[ts(hc4, 128), ts(th4, 512)])
                je = i - 4
                if 0 <= je < 64:
                    th4, hc4 = dwn_sched[je]
                    sy.wait_ge(g("c_ev4b"), 4 * (je + 1))
                    dq_ffnT.dma(f"ffnT{je}",
                                out=ffnTd[th4 * 512:(th4 + 1) * 512, ts(hc4, 128)].rearrange("(sub p) c -> p sub c", p=128),
                                in_=ftst[:, je % 2, :, :])
                if i == 34:
                    sy.wait_ge(g("c_mm4"), 32)
                    dq_hact.dma("hact1", out=hacth[:, :, :],
                                in_=hactd.rearrange("(k p) t -> p k t", p=128)[:, :, 512:1024])

            for q in range(4):
                for n in range(NST):
                    if q > 0:
                        lt = 16 * q - 1
                        sy.wait_ge(g("c_b"), lt * 16 + n + 1)
                        if n == 0:
                            sy.wait_ge(g("c_h"), lt * 16 + 2)
                        else:
                            sy.wait_ge(g("c_hc"), lt * 15 + n)
                    dq_bc.dma(None, out=brep[:, n, :],
                              in_=browd[0:1, n * L + q * T:n * L + (q + 1) * T].partition_broadcast(128))
                    dq_bc.dma(f"bc{q}_{n}", out=crep[:, n, :],
                              in_=crowd[0:1, n * L + q * T:n * L + (q + 1) * T].partition_broadcast(128))
                for k in range(NKD):
                    i = q * NKD + k
                    if i >= 2:
                        sy.wait_ge(g("c_fin"), i - 1)
                    dq_sc.dma(None, out=dtq[:, i % 2, :], in_=dtspd[ts(k, 128), ts(q, T)])
                    dq_sc.dma(f"sc{i}", out=xcq[:, i % 2, :], in_=xcspd[ts(k, 128), ts(q, T)])
                    p4_interleave(i)
            for je in range(60, 64):
                th4, hc4 = dwn_sched[je]
                sy.wait_ge(g("c_ev4b"), 4 * (je + 1))
                dq_ffnT.dma(f"ffnT{je}",
                            out=ffnTd[th4 * 512:(th4 + 1) * 512, ts(hc4, 128)].rearrange("(sub p) c -> p sub c", p=128),
                            in_=ftst[:, je % 2, :, :])
            if debug_phase == 3:
                sy.wait_ge(g("q_yexp"), 16 * 64)
                dq_out.dma(None, out=dbg_out, in_=ccB_in)
                dq_out.flush()
                return

            dq_ffnT.flush()
            sy.wait_ge(g("c_fin"), 64)
            sy.wait_ge(g("q_yexp"), 16 * 64)
            dq_xpin.dma(None, out=gbr[:, 0, :], in_=lngamma.partition_broadcast(128))
            dq_xpin.dma("xpg", out=gbr[:, 1, :], in_=lnbeta.partition_broadcast(128))
            for tc in range(NTC):
                if tc >= 2:
                    sy.wait_ge(g("c_xp"), tc - 1)
                dq_xpin.dma(f"fft{tc}", out=fft[:, tc % 2, :], in_=ffnTd[ts(tc, 128), :])
                dq_xpin.dma(f"xrt{tc}", out=xrt[:, tc % 2, :], in_=x_tok[ts(tc, 128), :])
                if tc >= 2:
                    sy.wait_ge(g("c_xp"), tc - 1)
                    dq_xpout.dma(f"xpo{tc - 2}", out=xplusd[ts(tc - 2, 128), :], in_=xpt[:, tc % 2, :])
            for tc in range(NTC - 2, NTC):
                sy.wait_ge(g("c_xp"), tc + 1)
                dq_xpout.dma(f"xpo{tc}", out=xplusd[ts(tc, 128), :], in_=xpt[:, tc % 2, :])
            if debug_phase == 4:
                dq_xpout.flush()
                dq_out.dma(None, out=dbg_out, in_=xplusd)
                dq_out.flush()
                return

            sy.wait_ge(g("q_ccb"), 1)
            dq_xpout.flush()
            sy.wait_ge(g("c_mm4"), 64)
            for rr in range(4):
                srcy = ccB_out[rr * 4 * DSL:(rr + 1) * 4 * DSL, :].rearrange("(j kl p) t -> j p kl t", j=4, p=128)[ts(myg, 1), :, :, :]
                dq_y.dma("y" if rr == 3 else None,
                         out=ybuf[:, rr * 16:(rr + 1) * 16, :], in_=srcy)
            for i in range(64):
                if i >= 2:
                    sy.wait_ge(g("c_zs"), i - 1)
                dq_z.dma(f"z{i}", out=zkt[:, i % 2, :], in_=zbufd[ts(i, 128), :])
            def wo_load(i):
                hc4, qf = divmod(i, 4)
                if i >= 2:
                    sy.wait_ge(g("c_ev5"), i - 1)
                dq_wo[i % 2].dma(f"wo{i}", out=wosl[:, i % 2, :], in_=outwT[hc4, :, ts(qf, 2048)])
            wo_load(0)
            wo_load(1)
            for hc in range(32):
                if hc >= 2:
                    sy.wait_ge(g("c_asm"), (hc - 1) * 8)
                dq_xpc.dma(f"xpc{hc}", out=xpcs[:, hc % 2, :, :],
                           in_=xplusd.rearrange("(tc p) h -> p tc h", p=128)[:, :, ts(hc, 128)])
                for qq in range(4 * hc + 2, min(4 * hc + 6, 128)):
                    wo_load(qq)
                sy.wait_ge(g("c_asm"), (hc + 1) * 8)
                dq_out.dma(f"out{hc}", out=out_tok.rearrange("(tc p) h -> p tc h", p=128)[:, :, ts(hc, 128)],
                           in_=ost[:, hc % 2, :, :])
            dq_out.flush()

        @blk.gpsimd
        def _(gp):
            dq_gw = DQ(gp, "q_gw"); dq_w = [DQ(gp, "q_w0"), DQ(gp, "q_w1")]
            dq_wx = DQ(gp, "q_wx"); dq_fcs = DQ(gp, "q_fcs"); dq_dwn = DQ(gp, "q_dwn")
            dq_yexp = DQ(gp, "q_yexp")

            def slab_src(w, og, width=512):
                return w.rearrange("(k p) c -> p k c", p=128)[:, :, og * width:(og + 1) * width]
            if debug_phase == 10:
                return
            dq_gw.dma("gatew", out=gsb[:, :, :], in_=gatew.rearrange("(k p) e -> p k e", p=128))
            for og in range(32):
                if og >= 2:
                    gp.wait_ge(g("c_mm1"), EV_INP + 8 * (og - 1))
                dq_w[og % 2].dma(f"win{og}", out=wslab[:, og % 2, :, :], in_=w_in[og, :, :].rearrange("p (k c) -> p k c", c=512))
            if debug_phase == 11:
                return
            for og in range(4):
                gp.wait_ge(g("c_mm1"), (EV_INP + 256) if og < 2 else (EV_FC1 + 8 * (og - 1)))
                dq_w[og % 2].dma(f"fc1w{og}", out=wslab[:, og % 2, :, :], in_=fc1w[og, :, :].rearrange("p (k c) -> p k c", c=512))
            for og in range(4):
                gp.wait_ge(g("c_mm1"), (EV_FC1 + 32) if og < 2 else (EV_EXP + 8 * (og - 1)))
                dq_w[og % 2].dma(f"expw{og}", out=wslab[:, og % 2, 0:16, :], in_=expw[og, :, :].rearrange("p (k c) -> p k c", c=512))
            for og in range(16):
                gp.wait_ge(g("c_mm1"), (EV_EXP + 32) if og < 2 else (EV_UP + 8 * (og - 1)))
                dq_w[og % 2].dma(f"upw{og}", out=wslab[:, og % 2, :, :], in_=upw[og, :, :].rearrange("p (k c) -> p k c", c=512))
            wld(gp, "exp15")
            gp.collective_compute("AllGather", ALU.bypass, replica_groups=RG,
                                  ins=[ccA_in.opt()], outs=[ccA_out.opt()]).then_inc(g("q_cca"))
            if debug_phase == 1:
                return
            gp.wait_ge(g("c_mm1"), EV_END)
            wld(gp, "expu15")
            dq_wx.dma(None, out=wxa[:, :], in_=xprojw[0:128, :])
            dq_wx.dma(None, out=wxb[:, :], in_=xprojw[128:256, :])
            dq_wx.dma("wxc3", out=wxc[:, :], in_=xprojw[256:288, :])
            gp.wait_ge(g("q_xcsp"), 16)
            dq_wx.dma("wdtn", out=wdtnat[:, :, :], in_=dtprojw.rearrange("(k p) r -> p k r", p=128))
            if debug_phase == 21:
                return
            wld(gp, "ssm3")
            gp.collective_compute("AllReduce", ALU.add, replica_groups=RG,
                                  ins=[ccC_in.opt()], outs=[ccC_out.opt()]).then_inc(g("q_ccc"))
            if debug_phase == 22:
                return
            for hc in range(32):
                if hc >= 2:
                    gp.wait_ge(g("c_mmf"), 2 * (hc - 1))
                dq_fcs.dma(f"fcs{hc}", out=fcsl[:, hc % 2, :, :],
                           in_=fc2w[hc, :, :].rearrange("p (k c) -> p k c", c=128))
            if debug_phase in (2, 21, 22):
                return
            wld(gp, "dtsp31")
            wld(gp, "rout63")
            dwn_issued = 0
            for q in range(4):
                for k in range(NKD):
                    i = q * NKD + k
                    while dwn_issued < 64 and dwn_issued * 64 <= i * 64 + 32:
                        jj = dwn_issued
                        th4, hc4 = dwn_sched[jj]
                        if jj >= 2:
                            gp.wait_ge(g("c_mm4"), jj - 1)
                        dq_dwn.dma(f"dwn{jj}", out=dwsl[:, jj % 2, :, :],
                                   in_=downw[hc4, :, :].rearrange("p (k c) -> p k c", c=128))
                        dwn_issued += 1
                    for n in range(1, NST):
                        gp.wait_ge(g("c_hc"), i * 15 + n)
                        if n >= 2:
                            chp.join(gp)
                        gp.tensor_tensor(yac[:, :], yac[:, :], hcb[:, n % 2, :], ALU.add).then_inc(g("c_y"), 1)
                        chp.mark("c_y", i * 15 + n)
                    gp.wait_ge(g("c_fin"), i + 1)
                    dq_yexp.dma(f"yexp{i}", out=ccB_in[q * DSL + k * 128:q * DSL + (k + 1) * 128, :],
                                in_=y2t[:, :])
            while dwn_issued < 64:
                jj = dwn_issued
                th4, hc4 = dwn_sched[jj]
                if jj >= 2:
                    gp.wait_ge(g("c_mm4"), jj - 1)
                dq_dwn.dma(f"dwn{jj}", out=dwsl[:, jj % 2, :, :],
                           in_=downw[hc4, :, :].rearrange("p (k c) -> p k c", c=128))
                dwn_issued += 1
            dq_yexp.flush()
            gp.collective_compute("AllGather", ALU.bypass, replica_groups=RG,
                                  ins=[ccB_in.opt()], outs=[ccB_out.opt()]).then_inc(g("q_ccb"))

        @blk.tensor
        def _(t):
            wld(t, "misc")
            for tc in range(NTC):
                t.wait_ge(g("c_ln"), tc + 1)
                for hc in range(NKH):
                    i = tc * NKH + hc
                    if i >= 4:
                        t.wait_ge(g("c_xnf"), i - 3)
                    t.transpose(ps[:, i % 4, 0:128], xnstage[:, tc % 2, ts(hc, 128)], s_identf[:]).then_inc(g("c_tr0"), 1)
            def mm_slabs(n_og, nk, ev_base, wname, rhs_sb):
                for og in range(n_og):
                    wld(t, wname.format(og))
                    t.wait_ge(g("c_ev1"), ev_base + 8 * og)
                    for j in range(4):
                        for th in range(2):
                            last = None
                            for k in range(nk):
                                last = t.matmul(ps[:, j * 2 + th, :],
                                                wslab[:, og % 2, k, ts(j, 128)],
                                                rhs_sb[:, k, ts(th, 512)],
                                                start=(k == 0), stop=(k == nk - 1))
                            last.then_inc(g("c_mm1"), 1)
            if debug_phase == 10:
                return
            t.wait_ge(g("c_xnf"), 256)
            mm_slabs(32, NKH, EV_INP, "win{}", xnf)
            if debug_phase == 11:
                return
            wld(t, "gatew")
            for tc in range(NTC):
                t.wait_ge(g("c_ev1"), EV_GATE + tc)
                last = None
                for k in range(NKH):
                    last = t.matmul(ps[:, 6, 0:NE], xnf[:, k, ts(tc, 128)], gsb[:, k, :],
                                    start=(k == 0), stop=(k == NKH - 1))
                last.then_inc(g("c_mm1"), 1)
            for tc in range(NTC):
                t.wait_ge(g("c_scol"), tc + 1)
                if tc >= 1:
                    t.wait_ge(g("c_srow"), tc)
                t.transpose(ps[0:1, 7, 0:128], scols[:, tc:tc + 1], s_identf[:]).then_inc(g("c_str"), 1)
            t.wait_ge(g("c_srow"), NTC)
            mm_slabs(4, NKH, EV_FC1, "fc1w{}", xnf)
            t.wait_ge(g("c_ev1"), EV_FC1 + 32)
            mm_slabs(4, 16, EV_EXP, "expw{}", xlat)
            mm_slabs(16, NKH, EV_UP, "upw{}", xnf)
            if debug_phase == 1:
                return
            wld(t, "wxc3")
            for kd in range(NKD):
                if kd >= 1:
                    t.wait_ge(g("c_wxc"), 3 * kd)
                t.transpose(ps[:, 4, 0:128], wxa[:, ts(kd, 128)], s_identf[:])
                t.transpose(ps[:, 5, 0:128], wxb[:, ts(kd, 128)], s_identf[:])
                t.transpose(ps[:, 6, 0:32], wxc[0:32, ts(kd, 128)], s_identf[0:32, 0:32]).then_inc(g("c_trwx"), 3)
            wld(t, "xd")
            t.wait_ge(g("c_cvb"), NKD)
            t.wait_ge(g("c_wxc"), 3 * NKD)
            for tc in range(LTC):
                if tc >= 4:
                    t.wait_ge(g("c_ev2"), tc - 3)
                last = None
                for kd in range(NKD):
                    last = t.matmul(ps[:, tc % 4, 0:R + 2 * NST], xdb[:, kd, ts(tc, 128)], wxT[:, kd, :],
                                    start=(kd == 0), stop=(kd == NKD - 1))
                last.then_inc(g("c_mm2"), 1)
            if debug_phase in (21, 22):
                return
            t.wait_ge(g("c_ev2"), LTC)
            wld(t, "moe2")
            for i in range(64):
                hc, th = divmod(i, 2)
                wld(t, f"fcs{hc}")
                if i >= 4:
                    t.wait_ge(g("c_evf"), i - 3)
                last = None
                for k in range(16):
                    last = t.matmul(ps[:, i % 4, :], fcsl[:, hc % 2, k, :], moesb[:, k, ts(th, 512)],
                                    start=(k == 0), stop=(k == 15))
                last.then_inc(g("c_mmf"), 1)
            wld(t, "wdtn")
            for kd in range(NKD):
                for rc in range(2):
                    i = kd * 2 + rc
                    if i >= 1:
                        t.wait_ge(g("c_wdtc"), i)
                    t.transpose(ps[:, 7, 0:128], wdtnat[:, kd, ts(rc, 128)], s_identf[:]).then_inc(g("c_trwd"), 1)
            for tc in range(LTC):
                t.wait_ge(g("c_rms"), 3 * (tc + 1))
                if tc >= 1:
                    t.wait_ge(g("c_bcc"), 2 * tc)
                t.transpose(ps[0:16, 4, 0:128], bnt[:, tc % 4, :], s_identf[:])
                t.transpose(ps[0:16, 5, 0:128], cnt[:, tc % 4, :], s_identf[:]).then_inc(g("c_bct"), 2)
                for rc in range(2):
                    i = tc * 2 + rc
                    if i >= 1:
                        t.wait_ge(g("c_dtrc"), i)
                    t.transpose(ps[:, 6, 0:128], dtr32[:, tc % 4, ts(rc, 128)], s_identf[:]).then_inc(g("c_dtt"), 1)
            t.wait_ge(g("c_evf"), 64)
            t.wait_ge(g("c_dtrc"), 64)
            t.wait_ge(g("c_wdtc"), 32)
            for i in range(2 * NKD):
                kd, hf = divmod(i, 2)
                if i >= 1:
                    t.wait_ge(g("c_sp0"), i)
                last = None
                for f in range(4):
                    for rc in range(2):
                        last = t.matmul(ps[:, f, :], wdtT[:, rc, ts(kd, 128)],
                                        dtrT[:, rc, hf * 2048 + f * 512:hf * 2048 + (f + 1) * 512],
                                        start=(rc == 0), stop=(rc == 1))
                last.then_inc(g("c_mm3"), 1)
            if debug_phase in (2, 21, 22):
                return
            def p4trs(jt):
                t.wait_ge(g("c_ev4"), jt + 1)
                for sub in range(4):
                    si = jt * 4 + sub
                    if si >= 2:
                        t.wait_ge(g("c_ev4b"), si - 1)
                    t.transpose(ps[:, 6 + si % 2, 0:128], f4st[:, jt % 2, :][:, ts(sub, 128)],
                                s_identf[:]).then_inc(g("c_tr4"), 1)
            for j in range(64):
                if j % 32 == 0:
                    wld(t, f"hact{j // 32}")
                wld(t, f"dwn{j}")
                if j >= 2:
                    t.wait_ge(g("c_ev4"), j - 1)
                pb = (j % 2) * 2
                last = None
                for k in range(64):
                    last = t.matmul(ps[:, pb, :], dwsl[:, j % 2, k, :],
                                    hacth[:, k, :], start=(k == 0), stop=(k == 63))
                last.then_inc(g("c_mm4"), 1)
                if j >= 1:
                    p4trs(j - 1)
            p4trs(63)
            if debug_phase <= 4:
                return
            wld(t, "y")
            t.wait_ge(g("c_gate"), 64)
            for hc in range(32):
                t.wait_ge(g("c_ev5"), (hc + 1) * 4)
                if hc >= 2:
                    t.wait_ge(g("c_ev5b"), hc - 1)
                pb = (hc % 2) * 2
                last0 = last1 = None
                for k in range(64):
                    last0 = t.matmul(ps[:, pb, :], wline[:, k, :], ybuf[:, k, 0:512],
                                     start=(k == 0), stop=(k == 63))
                    last1 = t.matmul(ps[:, pb + 1, :], wline[:, k, :], ybuf[:, k, 512:1024],
                                     start=(k == 0), stop=(k == 63))
                    if k in (15, 31, 47):
                        last1.then_inc(g("c_tr5"), 1)
                last0.then_inc(g("c_mm5"), 1)
                last1.then_inc(g("c_mm5"), 1)
                t.wait_ge(g("c_ev5b"), hc + 1)
                for tcc in range(8):
                    i = hc * 8 + tcc
                    if i >= 2:
                        t.wait_ge(g("c_asm"), i - 1)
                    t.transpose(ps[:, 6 + i % 2, 0:128], a5st[:, ts(tcc, 128)],
                                s_identf[:]).then_inc(g("c_tr5b"), 1)

        @blk.scalar
        def _(s):
            def xnf_copies(tcb):
                for hc in range(NKH):
                    i = tcb * NKH + hc
                    s.wait_ge(g("c_tr0"), i + 1)
                    s.copy(xnf[:, hc, ts(tcb, 128)], ps[:, i % 4, 0:128]).then_inc(g("c_xnf"), 1)
            for tc in range(NTC):
                if tc >= 2:
                    xnf_copies(tc - 2)
                wld(s, f"x{tc}")
                if tc >= 1:
                    s.wait_ge(g("c_lnr"), tc)
                if tc >= 2:
                    s.wait_ge(g("c_tr0"), 32 * (tc - 1))
                s.activation(xnstage[:, tc % 2, :], xt[:, tc % 2, :], AF.Square).then_inc(g("c_sq0"), 1)
                s.wait_ge(g("c_veps"), tc + 1)
                chs.bump(s.activation(lncols[:, 7:8], lncols[:, 6:7], AF.Ln))
                chs.join(s)
                s.activation(lncols[:, 7:8], lncols[:, 7:8], AF.Exp, scale=-0.5).then_inc(g("c_rstd"), 1)
            xnf_copies(NTC - 2)
            xnf_copies(NTC - 1)
            if debug_phase == 10 or debug_phase == 11:
                return
            for og in range(16):
                for jt in range(8):
                    idx = og * 8 + jt
                    j, th = divmod(jt, 2)
                    s.wait_ge(g("c_relu"), idx + 1)
                    s.activation(evst[:, og % 2, j, th * 512:(th + 1) * 512], rlst[:, idx % 2, :],
                                 AF.Square).then_inc(g("c_ev1"), 1)
            if debug_phase == 1:
                return
            for k in range(NKD):
                s.wait_ge(g("c_tap"), 4 * (k + 1))
                if k >= 1:
                    s.wait_ge(g("c_cvb"), k)
                s.activation(cvst[:, :], cacc[:, :], AF.Silu).then_inc(g("c_silu"), 1)
            if debug_phase in (21, 22):
                return
            wld(s, "ssml")
            for tc in range(LTC):
                if tc >= 1:
                    s.wait_ge(g("c_red3"), tc)
                s.activation(sqst[:, :], ssm2[:, tc, :], AF.Square).then_inc(g("c_sq2"), 1)
                s.wait_ge(g("c_vep3"), tc + 1)
                chs.bump(s.activation(rmcols[:, 3:4], rmcols[:, 3:4], AF.Ln))
                chs.bump(s.activation(rmcols[:, 4:5], rmcols[:, 4:5], AF.Ln))
                chs.bump(s.activation(rmcols[:, 5:6], rmcols[:, 5:6], AF.Ln))
                chs.join(s)
                s.activation(rmcols[:, 3:4], rmcols[:, 3:4], AF.Exp, scale=-0.5)
                s.activation(rmcols[:, 4:5], rmcols[:, 4:5], AF.Exp, scale=-0.5)
                s.activation(rmcols[:, 5:6], rmcols[:, 5:6], AF.Exp, scale=-0.5).then_inc(g("c_rst3"), 1)
            for i in range(2 * NKD):
                s.wait_ge(g("c_sp0"), i + 1)
                chs.bump(s.activation(spa[:, i % 2, :], spb[:, i % 2, :], AF.Abs))
                chs.join(s)
                chs.bump(s.activation(spa[:, i % 2, :], spa[:, i % 2, :], AF.Exp, scale=-1.0))
                chs.join(s)
                s.activation(spa[:, i % 2, :], spa[:, i % 2, :], AF.Ln, bias=1.0).then_inc(g("c_spa"), 1)
            if debug_phase in (2, 21, 22):
                return
            wld(s, "misc")
            chs.bump(s.activation(s_A[:, :, :], s_A[:, :, :], AF.Exp))
            chs.join(s)
            chs.bump(s.mul(s_A[:, :, :], s_A[:, :, :], -1.0))
            chs.join(s)
            for q in range(4):
                for k in range(NKD):
                    i = q * NKD + k
                    wld(s, f"sc{i}")
                    for n in range(NST):
                        idx = i * NST + n
                        if idx >= 3:
                            s.wait_ge(g("c_h"), idx - 2)
                        s.activation(abuf[:, idx % 3, :], dtq[:, i % 2, :], AF.Exp,
                                     scale=s_A[:, k, n:n + 1]).then_inc(g("c_a"), 1)
                    j = i - 3
                    if 0 <= j < 64:
                        if j >= 2:
                            wld(s, f"ffnT{j - 2}")
                        for sub in range(4):
                            si = j * 4 + sub
                            s.wait_ge(g("c_tr4"), si + 1)
                            s.copy(ftst[:, j % 2, sub, :], ps[:, 6 + si % 2, 0:128]).then_inc(g("c_ev4b"), 1)
            for j in (61, 62, 63):
                wld(s, f"ffnT{j - 2}")
                for sub in range(4):
                    si = j * 4 + sub
                    s.wait_ge(g("c_tr4"), si + 1)
                    s.copy(ftst[:, j % 2, sub, :], ps[:, 6 + si % 2, 0:128]).then_inc(g("c_ev4b"), 1)
            if debug_phase == 3:
                return
            for tc in range(NTC):
                wld(s, f"fft{tc}")
                if tc >= 1:
                    s.wait_ge(g("c_xp"), tc)
                s.activation(sqxt[:, :], fft[:, tc % 2, :], AF.Square).then_inc(g("c_sqx"), 1)
                s.wait_ge(g("c_vepx"), tc + 1)
                chs.bump(s.activation(xpcols[:, 7:8], xpcols[:, 6:7], AF.Ln))
                chs.join(s)
                s.activation(xpcols[:, 7:8], xpcols[:, 7:8], AF.Exp, scale=-0.5).then_inc(g("c_rstx"), 1)
            if debug_phase <= 4:
                return
            for i in range(64):
                wld(s, f"z{i}")
                if i >= 2:
                    s.wait_ge(g("c_gate"), i - 1)
                s.activation(zst[:, i % 2, :], zkt[:, i % 2, :], AF.Silu).then_inc(g("c_zs"), 1)
            for i in range(128):
                hc5, qf = divmod(i, 4)
                wld(s, f"wo{i}")
                if hc5 >= 1:
                    if qf < 3:
                        s.wait_ge(g("c_tr5"), (hc5 - 1) * 3 + qf + 1)
                    else:
                        s.wait_ge(g("c_mm5"), 2 * hc5)
                s.copy(wline[:, qf * 16:(qf + 1) * 16, :].rearrange("p a b -> p (a b)"),
                       wosl[:, i % 2, :]).then_inc(g("c_ev5"), 1)

        @blk.vector
        def _(v):
            for tc in range(NTC):
                wld(v, f"x{tc}")
                chv.bump(v.reduce_sum(lncols[:, 0:1], xt[:, tc % 2, :], axis=AX))
                chv.join(v)
                chv.bump(v.tensor_scalar(lncols[:, 1:2], lncols[:, 0:1], 1.0 / H, None, ALU.mult))
                v.wait_ge(g("c_sq0"), tc + 1)
                v.reduce_sum(lncols[:, 2:3], xnstage[:, tc % 2, :], axis=AX).then_inc(g("c_lnr"), 1)
                chv.mark("c_lnr", tc + 1)
                chv.join(v)
                chv.bump(v.tensor_scalar(lncols[:, 3:4], lncols[:, 2:3], 1.0 / H, None, ALU.mult))
                chv.bump(v.tensor_tensor(lncols[:, 4:5], lncols[:, 1:2], lncols[:, 1:2], ALU.mult))
                chv.join(v)
                chv.bump(v.tensor_tensor(lncols[:, 5:6], lncols[:, 3:4], lncols[:, 4:5], ALU.subtract))
                chv.join(v)
                v.tensor_scalar(lncols[:, 6:7], lncols[:, 5:6], 1.0, EPS_LN, ALU.mult, ALU.add).then_inc(g("c_veps"), 1)
                chv.mark("c_veps", tc + 1)
                if tc >= 2:
                    v.wait_ge(g("c_tr0"), 32 * (tc - 1))
                v.wait_ge(g("c_rstd"), tc + 1)
                v.tensor_scalar(xnstage[:, tc % 2, :], xt[:, tc % 2, :], lncols[:, 1:2], lncols[:, 7:8],
                                ALU.subtract, ALU.mult).then_inc(g("c_ln"), 1)
            if debug_phase == 10:
                return
            for og in range(32):
                if og >= 2:
                    wld(v, f"exp{og - 2}")
                for jt in range(8):
                    j, th = divmod(jt, 2)
                    v.wait_ge(g("c_mm1"), EV_INP + og * 8 + jt + 1)
                    v.tensor_copy(evst[:, og % 2, j, th * 512:(th + 1) * 512], ps[:, jt, :]).then_inc(g("c_ev1"), 1)
            if debug_phase == 11:
                return
            for tc in range(NTC):
                v.wait_ge(g("c_mm1"), EV_GATE + tc + 1)
                v.tensor_copy(logit[:, :], ps[:, 6, 0:NE]).then_inc(g("c_ev1"), 1)
                chv.mark("c_ev1", EV_GATE + tc + 1)
                chv.join(v)
                chv.bump(v.reduce_max(l_m[:, 0:1], logit[:, :], axis=AX))
                chv.join(v)
                chv.bump(v.tensor_scalar(lgm[:, :], logit[:, :], l_m[:, 0:1], -1e30, ALU.is_ge, ALU.mult))
                chv.join(v)
                chv.bump(v.tensor_tensor(lgm[:, :], lgm[:, :], logit[:, :], ALU.add))
                chv.join(v)
                chv.bump(v.reduce_max(l_m[:, 1:2], lgm[:, :], axis=AX))
                chv.join(v)
                v.tensor_tensor(scols[:, tc:tc + 1], l_m[:, 0:1], l_m[:, 1:2], ALU.add).then_inc(g("c_scol"), 1)
            for tc in range(NTC):
                v.wait_ge(g("c_str"), tc + 1)
                v.tensor_copy(s_srow[0:1, ts(tc, 128)], ps[0:1, 7, 0:128]).then_inc(g("c_srow"), 1)
            for og in range(4):
                for jt in range(8):
                    j, th = divmod(jt, 2)
                    v.wait_ge(g("c_mm1"), EV_FC1 + og * 8 + jt + 1)
                    v.tensor_copy(xlat[:, og * 4 + j, th * 512:(th + 1) * 512], ps[:, jt, :]).then_inc(g("c_ev1"), 1)
            wld(v, "srep")
            for og in range(4):
                wld(v, f"exp{30 + og}" if og < 2 else f"expm{og - 2}")
                for jt in range(8):
                    j, th = divmod(jt, 2)
                    v.wait_ge(g("c_mm1"), EV_EXP + og * 8 + jt + 1)
                    v.tensor_tensor(evst[:, og % 2, j, th * 512:(th + 1) * 512], ps[:, jt, :],
                                    s_rep[:, th * 512:(th + 1) * 512], ALU.mult).then_inc(g("c_ev1"), 1)
            for og in range(16):
                wld(v, f"expm{2 + og}" if og < 2 else f"expu{og - 2}")
                for jt in range(8):
                    idx = og * 8 + jt
                    v.wait_ge(g("c_mm1"), EV_UP + idx + 1)
                    if idx >= 2:
                        v.wait_ge(g("c_ev1"), EV_UP + idx - 1)
                    v.tensor_scalar(rlst[:, idx % 2, :], ps[:, jt, :], 0.0, None, ALU.max).then_inc(g("c_relu"), 1)
            if debug_phase == 1:
                return
            for kd in range(NKD):
                v.wait_ge(g("c_trwx"), 3 * (kd + 1))
                v.tensor_copy(wxT[:, kd, 0:128], ps[:, 4, 0:128])
                v.tensor_copy(wxT[:, kd, 128:256], ps[:, 5, 0:128])
                v.tensor_copy(wxT[:, kd, 256:288], ps[:, 6, 0:32]).then_inc(g("c_wxc"), 3)
            wld(v, "xd")
            wld(v, "misc")
            for k in range(NKD):
                v.tensor_scalar(cacc[:, :], xdb[:, k, :], s_convw[:, k, 3:4], s_convb[:, k:k + 1],
                                ALU.mult, ALU.add).then_inc(g("c_tap"), 1)
                chv.mark("c_tap", 4 * k + 1)
                for kk in range(3):
                    o = 3 - kk
                    chv.join(v)
                    v.scalar_tensor_tensor(cacc[:, o:L], xdb[:, k, 0:L - o], s_convw[:, k, kk:kk + 1],
                                           cacc[:, o:L], ALU.mult, ALU.add).then_inc(g("c_tap"), 1)
                    chv.mark("c_tap", 4 * k + 1 + kk + 1)
                v.wait_ge(g("c_silu"), k + 1)
                v.tensor_copy(xdb[:, k, :], cvst[:, :]).then_inc(g("c_cvb"), 1)
            for tc in range(LTC):
                v.wait_ge(g("c_mm2"), tc + 1)
                if tc >= 8:
                    wld(v, f"ssm{tc // 8 - 1}")
                v.tensor_copy(ssmst[:, tc % 8, :], ps[:, tc % 4, 0:R + 2 * NST]).then_inc(g("c_ev2"), 1)
            if debug_phase in (21, 22):
                return
            for i in range(64):
                hc, th = divmod(i, 2)
                v.wait_ge(g("c_mmf"), i + 1)
                if i >= 2:
                    wld(v, f"rout{i - 2}")
                v.tensor_copy(fc2st[:, i % 2, :], ps[:, i % 4, :]).then_inc(g("c_evf"), 1)
            for kd in range(NKD):
                for rc in range(2):
                    i = kd * 2 + rc
                    v.wait_ge(g("c_trwd"), i + 1)
                    v.tensor_copy(wdtT[:, rc, ts(kd, 128)], ps[:, 7, 0:128]).then_inc(g("c_wdtc"), 1)
            def bc_dtr_copies(tcb):
                v.wait_ge(g("c_bct"), 2 * (tcb + 1))
                v.tensor_copy(btst[:, tcb, :], ps[0:16, 4, 0:128])
                v.tensor_copy(ctst[:, tcb, :], ps[0:16, 5, 0:128]).then_inc(g("c_bcc"), 2)
                for rc in range(2):
                    i2 = tcb * 2 + rc
                    v.wait_ge(g("c_dtt"), i2 + 1)
                    v.tensor_copy(dtrT[:, rc, ts(tcb, 128)], ps[:, 6, 0:128]).then_inc(g("c_dtrc"), 1)
            wld(v, "ssml")
            for tc in range(LTC):
                v.wait_ge(g("c_sq2"), tc + 1)
                if tc >= 4:
                    v.wait_ge(g("c_dtt"), 2 * (tc - 3))
                    v.wait_ge(g("c_bct"), 2 * (tc - 3))
                v.reduce_sum(rmcols[:, 0:1], sqst[:, 0:R], axis=AX)
                v.reduce_sum(rmcols[:, 1:2], sqst[:, R:R + NST], axis=AX)
                v.reduce_sum(rmcols[:, 2:3], sqst[:, R + NST:], axis=AX).then_inc(g("c_red3"), 1)
                chv.mark("c_red3", tc + 1)
                chv.join(v)
                v.tensor_scalar(rmcols[:, 3:4], rmcols[:, 0:1], 1.0 / R, EPS_RMS, ALU.mult, ALU.add)
                v.tensor_scalar(rmcols[:, 4:5], rmcols[:, 1:2], 1.0 / NST, EPS_RMS, ALU.mult, ALU.add)
                v.tensor_scalar(rmcols[:, 5:6], rmcols[:, 2:3], 1.0 / NST, EPS_RMS, ALU.mult, ALU.add).then_inc(g("c_vep3"), 1)
                v.wait_ge(g("c_rst3"), tc + 1)
                v.scalar_tensor_tensor(dtr32[:, tc % 4, :], ssm2[:, tc, 0:R], rmcols[:, 3:4],
                                       s_gamr[:, 0:R], ALU.mult, ALU.mult).then_inc(g("c_rms"), 1)
                v.scalar_tensor_tensor(bnt[:, tc % 4, :], ssm2[:, tc, R:R + NST], rmcols[:, 4:5],
                                       s_gamr[:, R:R + NST], ALU.mult, ALU.mult).then_inc(g("c_rms"), 1)
                v.scalar_tensor_tensor(cnt[:, tc % 4, :], ssm2[:, tc, R + NST:], rmcols[:, 5:6],
                                       s_gamr[:, R + NST:], ALU.mult, ALU.mult).then_inc(g("c_rms"), 1)
                if tc >= 1:
                    bc_dtr_copies(tc - 1)
            bc_dtr_copies(LTC - 1)
            for i in range(2 * NKD):
                kd, hf = divmod(i, 2)
                v.wait_ge(g("c_mm3"), i + 1)
                if i >= 2:
                    wld(v, f"dtsp{i - 2}")
                v.tensor_scalar(spb[:, i % 2, :], ps[:, 0:4, :].rearrange("p b f -> p (b f)"),
                                s_dtbias[:, kd:kd + 1], None, ALU.add).then_inc(g("c_sp0"), 1)
                chv.mark("c_sp0", i + 1)
                v.wait_ge(g("c_spa"), i + 1)
                chv.join(v)
                chv.bump(v.tensor_scalar(spb[:, i % 2, :], spb[:, i % 2, :], 0.0, None, ALU.max))
                chv.join(v)
                v.tensor_tensor(dtst[:, i % 2, :], spb[:, i % 2, :], spa[:, i % 2, :], ALU.add).then_inc(g("c_sp"), 1)
            if debug_phase in (2, 21, 22):
                return
            v.memset(s_hstate[:, :], 0.0)
            for q in range(4):
                for k in range(NKD):
                    i = q * NKD + k
                    wld(v, f"sc{i}")
                    v.tensor_tensor(dtxq[:, :], dtq[:, i % 2, :], xcq[:, i % 2, :], ALU.mult).then_inc(g("c_dtx"), 1)
                    chv.mark("c_dtx", i + 1)
                    chv.join(v)
                    for n in range(NST):
                        idx = i * NST + n
                        kn = k * NST + n
                        v.wait_ge(g("c_a"), idx + 1)
                        wld(v, f"bc{q}_{n}")
                        v.tensor_tensor(bbuf[:, :], dtxq[:, :], brep[:, n, :], ALU.mult).then_inc(g("c_b"), 1)
                        chv.mark("c_b", i * NST + n + 1)
                        if n >= 1:
                            pn = n - 1
                            v.wait_ge(g("c_h"), idx)
                            v.tensor_copy(s_hstate[:, k * NST + pn:k * NST + pn + 1], hbuf[:, T - 1:T])
                            if pn == 0:
                                v.tensor_tensor(yac[:, :], hbuf[:, :], crep[:, 0, :], ALU.mult)
                            else:
                                if pn >= 3:
                                    v.wait_ge(g("c_y"), i * 15 + pn - 2)
                                v.tensor_tensor(hcb[:, pn % 2, :], hbuf[:, :], crep[:, pn, :], ALU.mult).then_inc(g("c_hc"), 1)
                        chv.join(v)
                        v.tensor_tensor_scan(hbuf[:, :], abuf[:, idx % 3, :], bbuf[:, :],
                                             s_hstate[:, kn:kn + 1], ALU.mult, ALU.add).then_inc(g("c_h"), 1)
                        chv.mark("c_h", idx + 1)
                    pn = NST - 1
                    v.wait_ge(g("c_h"), i * NST + NST)
                    v.tensor_copy(s_hstate[:, k * NST + pn:k * NST + pn + 1], hbuf[:, T - 1:T])
                    v.wait_ge(g("c_y"), i * 15 + pn - 2)
                    v.tensor_tensor(hcb[:, pn % 2, :], hbuf[:, :], crep[:, pn, :], ALU.mult).then_inc(g("c_hc"), 1)
                    v.wait_ge(g("c_y"), (i + 1) * 15)
                    if i >= 2:
                        wld(v, f"yexp{i - 2}")
                    v.scalar_tensor_tensor(y2t[:, :], xcq[:, i % 2, :], s_dpar[:, k:k + 1], yac[:, :],
                                           ALU.mult, ALU.add).then_inc(g("c_fin"), 1)
                    j = i - 2
                    if 0 <= j < 64:
                        v.wait_ge(g("c_mm4"), j + 1)
                        wld(v, f"routt{j}")
                        if j >= 2:
                            v.wait_ge(g("c_ev4b"), 4 * (j - 1))
                        pb4 = (j % 2) * 2
                        v.scalar_tensor_tensor(f4st[:, j % 2, :], ps[:, pb4, :], 1.0, routt[:, j % 3, :],
                                               ALU.mult, ALU.add).then_inc(g("c_ev4"), 1)
            for j in (62, 63):
                v.wait_ge(g("c_mm4"), j + 1)
                wld(v, f"routt{j}")
                v.wait_ge(g("c_ev4b"), 4 * (j - 1))
                pb4 = (j % 2) * 2
                v.scalar_tensor_tensor(f4st[:, j % 2, :], ps[:, pb4, :], 1.0, routt[:, j % 3, :],
                                       ALU.mult, ALU.add).then_inc(g("c_ev4"), 1)
            if debug_phase == 3:
                return
            wld(v, "xpg")
            for tc in range(NTC):
                wld(v, f"xrt{tc}")
                v.wait_ge(g("c_sqx"), tc + 1)
                if tc >= 2:
                    wld(v, f"xpo{tc - 2}")
                chv.bump(v.reduce_sum(xpcols[:, 0:1], fft[:, tc % 2, :], axis=AX))
                chv.join(v)
                chv.bump(v.tensor_scalar(xpcols[:, 1:2], xpcols[:, 0:1], 1.0 / H, None, ALU.mult))
                chv.bump(v.reduce_sum(xpcols[:, 2:3], sqxt[:, :], axis=AX))
                chv.join(v)
                chv.bump(v.tensor_scalar(xpcols[:, 3:4], xpcols[:, 2:3], 1.0 / H, None, ALU.mult))
                chv.bump(v.tensor_tensor(xpcols[:, 4:5], xpcols[:, 1:2], xpcols[:, 1:2], ALU.mult))
                chv.join(v)
                chv.bump(v.tensor_tensor(xpcols[:, 5:6], xpcols[:, 3:4], xpcols[:, 4:5], ALU.subtract))
                chv.join(v)
                v.tensor_scalar(xpcols[:, 6:7], xpcols[:, 5:6], 1.0, EPS_LN, ALU.mult, ALU.add).then_inc(g("c_vepx"), 1)
                chv.mark("c_vepx", tc + 1)
                v.wait_ge(g("c_rstx"), tc + 1)
                chv.bump(v.tensor_scalar(xpt[:, tc % 2, :], fft[:, tc % 2, :], xpcols[:, 1:2], xpcols[:, 7:8],
                                         ALU.subtract, ALU.mult))
                chv.join(v)
                chv.bump(v.tensor_tensor(xpt[:, tc % 2, :], xpt[:, tc % 2, :], gbr[:, 0, :], ALU.mult))
                chv.join(v)
                chv.bump(v.tensor_tensor(xpt[:, tc % 2, :], xpt[:, tc % 2, :], gbr[:, 1, :], ALU.add))
                chv.join(v)
                v.tensor_tensor(xpt[:, tc % 2, :], xpt[:, tc % 2, :], xrt[:, tc % 2, :], ALU.add).then_inc(g("c_xp"), 1)
            if debug_phase <= 4:
                return
            wld(v, "y")
            for i in range(64):
                v.wait_ge(g("c_zs"), i + 1)
                v.tensor_tensor(ybuf[:, i, :], ybuf[:, i, :], zst[:, i % 2, :], ALU.mult).then_inc(g("c_gate"), 1)
            for hc in range(32):
                v.wait_ge(g("c_mm5"), 2 * (hc + 1))
                if hc >= 1:
                    v.wait_ge(g("c_tr5b"), 8 * hc)
                pb = (hc % 2) * 2
                v.tensor_copy(a5st[:, :], ps[:, pb:pb + 2, :].rearrange("p b f -> p (b f)")).then_inc(g("c_ev5b"), 1)
                wld(v, f"xpc{hc}")
                if hc >= 2:
                    wld(v, f"out{hc - 2}")
                for tcc in range(8):
                    i = hc * 8 + tcc
                    v.wait_ge(g("c_tr5b"), i + 1)
                    v.tensor_tensor(ost[:, hc % 2, tcc, :], ps[:, 6 + i % 2, 0:128], xpcs[:, hc % 2, tcc, :],
                                    ALU.add).then_inc(g("c_asm"), 1)

    return nc


_CACHE = {}


def kernel(**inputs):
    nc = _CACHE.get("nc99")
    if nc is None:
        nc = build_nc()
        _CACHE["nc99"] = nc
    in_maps = make_in_maps(inputs)
    res = run_bass_kernel_spmd(nc, in_maps, core_ids=list(range(8)))
    out = np.empty((B, L, H), dtype=np.float32)
    for c in range(8):
        b, r = divmod(c, 4)
        out[b, r * T:(r + 1) * T, :] = res.results[c]["out_tok"]
    return out


def make_in_maps(inputs):
    f32 = np.float32
    X = np.ascontiguousarray(np.asarray(inputs["L0_attn_hidden_states"], dtype=f32))
    asf = lambda k: np.ascontiguousarray(np.asarray(inputs[k], dtype=f32))
    def blockw(w, ogw):
        KK, CC = w.shape
        nk, nog = KK // 128, CC // ogw
        return np.ascontiguousarray(w.reshape(nk, 128, nog, ogw).transpose(2, 1, 0, 3).reshape(nog, 128, nk * ogw))
    shared = dict(
        w_in=blockw(asf("L0_attn_in_proj_w"), 512),
        outwT=np.ascontiguousarray(
            asf("L0_attn_out_proj_w").reshape(32, 128, 64, 128).transpose(0, 3, 2, 1)).reshape(32, 128, 8192),
        dtlnw=asf("L0_attn_dt_ln_w").reshape(1, R),
        blnw=asf("L0_attn_B_ln_w").reshape(1, NST),
        clnw=asf("L0_attn_C_ln_w").reshape(1, NST),
        gatew=np.ascontiguousarray(asf("L0_ffn_gate_w").reshape(NKH, 128, NE).transpose(1, 0, 2).reshape(128, NKH * NE)),
        upw=blockw(asf("L0_ffn_up_w"), 512),
        downw=blockw(asf("L0_ffn_down_w"), 128),
        fc1w=blockw(asf("L0_ffn_fc1_w"), 512),
        fc2w=blockw(asf("L0_ffn_fc2_w"), 128),
        expw=blockw(asf("L0_ffn_expert_w"), 512),
        lngamma=asf("L0_ffn_ln_gamma").reshape(1, H),
        lnbeta=asf("L0_ffn_ln_beta").reshape(1, H),
        identf=np.eye(128, dtype=f32),
    )
    cw = asf("L0_attn_conv_w"); cb = asf("L0_attn_conv_b")
    xpw = asf("L0_attn_x_proj_w"); dpw = asf("L0_attn_dt_proj_w")
    dtb = asf("L0_attn_dt_bias"); al = asf("L0_attn_A_log"); dp = asf("L0_attn_D_param")
    in_maps = []
    for c in range(8):
        b, r = divmod(c, 4)
        sl = slice(r * DSL, (r + 1) * DSL)
        m = dict(shared)
        m["x_tok"] = np.ascontiguousarray(X[b, r * T:(r + 1) * T, :])
        m["convw"] = np.ascontiguousarray(cw[sl])
        m["convb"] = np.ascontiguousarray(cb[sl].reshape(NKD, 128).T)
        m["xprojw"] = np.ascontiguousarray(xpw[:, sl])
        m["dtprojw"] = np.ascontiguousarray(dpw[sl])
        m["dtbias"] = np.ascontiguousarray(dtb[sl].reshape(NKD, 128).T)
        m["alog"] = np.ascontiguousarray(al[sl])
        m["dpar"] = np.ascontiguousarray(dp[sl].reshape(NKD, 128).T)
        in_maps.append(m)
    return in_maps


# version 05
# speedup vs baseline: 1.1120x; 1.0228x over previous
import numpy as np
import ml_dtypes
import concourse.bass as bass
import concourse.mybir as mybir
from concourse.bass_utils import run_bass_kernel_spmd

F32 = mybir.dt.float32
BF16 = mybir.dt.bfloat16
AF = mybir.ActivationFunctionType
ALU = mybir.AluOpType
SP = mybir.EngineType.SP
AX = mybir.AxisListType.X

H = 4096; DIN = 8192; NST = 16; KCV = 4; R = 256
B = 2; L = 4096; T = 1024; DSL = 2048; E2 = 2 * DIN
NE = 8; INT = 8192; LAT = 2048
EPS_LN = 1e-5; EPS_RMS = 1e-6
NTC = 8; LTC = 32; NKH = 32; NKD = 16
RG = [[0, 1, 2, 3], [4, 5, 6, 7]]
ts = bass.ts

EV_INP, EV_GATE, EV_FC1, EV_EXP, EV_UP, EV_END = 0, 256, 264, 296, 328, 456


def build_nc(debug_phase=99, race_detect=True):
    nc = bass.Bass("TRN2", target_bir_lowering=False, detect_race_conditions=race_detect)

    def I(name, shape):
        return nc.dram_tensor(name, shape, F32, kind="ExternalInput").ap()
    x_tok = I("x_tok", [T, H]); w_in = I("w_in", [32, 128, E2])
    convw = I("convw", [DSL, KCV]); convb = I("convb", [128, NKD])
    xprojw = I("xprojw", [R + 2 * NST, DSL]); dtprojw = I("dtprojw", [DSL, R])
    dtbias = I("dtbias", [128, NKD]); alog = I("alog", [DSL, NST]); dpar = I("dpar", [128, NKD])
    outwT = I("outwT", [32, 128, 8192])
    dtlnw = I("dtlnw", [1, R]); blnw = I("blnw", [1, NST]); clnw = I("clnw", [1, NST])
    gatew = I("gatew", [128, NKH * NE]); upw = I("upw", [16, 128, 2 * INT]); downw = I("downw", [32, 128, INT // 128 * 128])
    fc1w = I("fc1w", [4, 128, NKH * 512]); fc2w = I("fc2w", [32, 128, LAT]); expw = I("expw", [4, 128, 16 * 512])
    lngamma = I("lngamma", [1, H]); lnbeta = I("lnbeta", [1, H])
    identf = I("identf", [128, 128])

    out_tok = nc.dram_tensor("out_tok", [T, H], F32, kind="ExternalOutput").ap()
    dbg_out = None
    if debug_phase < 99:
        shp = {1: ([DIN, T], BF16), 2: ([DSL, L], BF16), 3: ([4 * DSL, T], BF16),
               4: ([T, H], F32), 10: ([128, NKH * T], BF16), 11: ([DIN, T], BF16),
               21: ([DSL, L], BF16), 22: ([L, R + 2 * NST], F32)}[debug_phase]
        dbg_out = nc.dram_tensor("dbg_out", shp[0], shp[1], kind="ExternalOutput").ap()

    D = nc.dram_tensor
    ccA_in = D("ccA_in", [DIN, T], BF16, kind="Internal").ap()
    ccA_out = D("ccA_out", [4 * DIN, T], BF16, kind="Internal").ap()
    ccB_in = D("ccB_in", [4 * DSL, T], BF16, kind="Internal").ap()
    ccB_out = D("ccB_out", [16 * DSL, T], BF16, kind="Internal").ap()
    ccC_in = D("ccC_in", [L, R + 2 * NST], F32, kind="Internal").ap()
    ccC_out = D("ccC_out", [L, R + 2 * NST], F32, kind="Internal").ap()
    zbufd = D("zbufd", [DIN, T], BF16, kind="Internal").ap()
    hactd = D("hactd", [INT, T], BF16, kind="Internal").ap()
    moed = D("moed", [LAT, T], BF16, kind="Internal").ap()
    routd = D("routd", [H, T], BF16, kind="Internal").ap()
    dtspd = D("dtspd", [DSL, L], BF16, kind="Internal").ap()
    xcspd = D("xcspd", [DSL, L], BF16, kind="Internal").ap()
    ffnTd = D("ffnTd", [T, H], BF16, kind="Internal").ap()
    xplusd = D("xplusd", [T, H], F32, kind="Internal").ap()
    browd = D("browd", [1, NST * L], BF16, kind="Internal").ap()
    crowd = D("crowd", [1, NST * L], BF16, kind="Internal").ap()
    srowd = D("srowd", [1, T], BF16, kind="Internal").ap()

    ps = nc.alloc_psum_tensor([128, 8, 512], F32)

    AL = nc.alloc_sbuf_tensor
    s_identf = AL([128, 128], F32)
    s_A = AL([128, NKD, NST], F32)
    s_convw = AL([128, NKD, KCV], F32)
    s_convb = AL([128, NKD], F32)
    s_dtbias = AL([128, NKD], F32)
    s_dpar = AL([128, NKD], F32)
    s_hstate = AL([128, NKD * NST], F32)
    s_gamr = AL([128, R + 2 * NST], F32)
    s_srow = AL([1, T], BF16)
    gsb = AL([128, NKH, NE], BF16)
    base0 = nc.sbuf_base

    xnf = AL([128, NKH, T], BF16)
    wslab = AL([128, 2, NKH, 512], BF16)
    ovl0 = nc.sbuf_base
    xt = AL([128, 2, H], F32)
    xnstage = AL([128, 2, H], F32)
    lncols = AL([128, 16], F32)
    p0_end = nc.sbuf_base
    nc.sbuf_base = ovl0
    evst = AL([128, 2, 4, 1024], BF16)
    xlat = AL([128, 16, T], BF16)
    rlst = AL([128, 2, 512], BF16)
    s_rep = AL([128, T], BF16)
    logit = AL([128, NE], F32)
    lgm = AL([128, NE], F32)
    l_m = AL([128, 4], F32)
    scols = AL([128, NTC], F32)
    p1_end = nc.sbuf_base

    nc.sbuf_base = base0
    xdb = AL([128, NKD, L], BF16)
    cacc = AL([128, L], F32)
    cvst = AL([128, L], BF16)
    wxa = AL([128, DSL], F32)
    wxb = AL([128, DSL], F32)
    wxc = AL([32, DSL], F32)
    wxT = AL([128, NKD, R + 2 * NST], BF16)
    ssmst = AL([128, 8, R + 2 * NST], F32)
    p2a_end = nc.sbuf_base

    nc.sbuf_base = base0
    ssm2 = AL([128, LTC, R + 2 * NST], F32)
    sqst = AL([128, R + 2 * NST], F32)
    rmcols = AL([128, 8], F32)
    dtr32 = AL([128, 4, R], F32)
    bnt = AL([128, 4, NST], F32)
    cnt = AL([128, 4, NST], F32)
    btst = AL([16, LTC, 128], BF16)
    ctst = AL([16, LTC, 128], BF16)
    dtrT = AL([128, 2, L], BF16)
    wdtnat = AL([128, NKD, R], F32)
    wdtT = AL([128, 2, DSL], BF16)
    moesb = AL([128, 16, T], BF16)
    fcsl = AL([128, 2, 16, 128], BF16)
    fc2st = AL([128, 2, 512], BF16)
    spb = AL([128, 2, DSL], F32)
    spa = AL([128, 2, DSL], F32)
    dtst = AL([128, 2, DSL], BF16)
    p2b_end = nc.sbuf_base

    nc.sbuf_base = base0
    brep = AL([128, NST, T], BF16)
    crep = AL([128, NST, T], BF16)
    dtq = AL([128, 2, T], BF16)
    xcq = AL([128, 2, T], BF16)
    dtxq = AL([128, T], BF16)
    abuf = AL([128, 3, T], BF16)
    bbuf = AL([128, T], BF16)
    hbuf = AL([128, T], BF16)
    hcb = AL([128, 2, T], BF16)
    yac = AL([128, T], F32)
    y2t = AL([128, T], F32)
    hacth = AL([128, 64, 512], BF16)
    dwsl = AL([128, 2, 64, 128], BF16)
    routt = AL([128, 3, 512], BF16)
    f4st = AL([128, 2, 512], F32)
    ftst = AL([128, 2, 4, 128], BF16)
    p34_end = nc.sbuf_base

    nc.sbuf_base = base0
    gbr = AL([128, 2, H], F32)
    fft = AL([128, 2, H], BF16)
    xrt = AL([128, 2, H], F32)
    xpt = AL([128, 2, H], F32)
    sqxt = AL([128, H], F32)
    xpcols = AL([128, 8], F32)
    p4b_end = nc.sbuf_base

    nc.sbuf_base = base0
    ybuf = AL([128, 64, T], BF16)
    wosl = AL([128, 2, 2048], F32)
    wline = AL([128, 64, 128], BF16)
    zkt = AL([128, 2, T], BF16)
    zst = AL([128, 2, T], BF16)
    xpcs = AL([128, 2, 8, 128], F32)
    ost = AL([128, 2, 8, 128], F32)
    a5st = AL([128, 1024], F32)
    p5_end = nc.sbuf_base

    LIMIT = 229344
    for nm, e in [("p0", p0_end), ("p1", p1_end), ("p2a", p2a_end), ("p2b", p2b_end),
                  ("p34", p34_end), ("p4b", p4b_end), ("p5", p5_end)]:
        assert e <= LIMIT, f"SBUF overflow {nm}: {e} > {LIMIT}"
    nc.sbuf_base = max(p0_end, p1_end, p2a_end, p2b_end, p34_end, p4b_end, p5_end)

    names = """q_misc q_x c_sq0 c_lnr c_ln c_tr0 c_xnf q_w0 q_w1 q_gw c_mm1 c_ev1 q_exp
    c_scol c_str c_srow q_srep c_relu q_cca q_xd c_tap c_silu c_cvb
    q_wx c_trwx c_wxc c_trwd c_wdtc c_mm2 c_ev2 q_ssm q_ccc q_ssml c_sq2 c_rms
    c_bct c_bcc c_dtt c_dtrc q_brow q_moe2 q_fcs c_mmf c_evf q_rout c_mm3 c_sp0 c_spa c_sp
    q_dtsp q_xcsp q_bc q_sc c_dtx c_a c_b c_h c_hc c_y c_fin q_yexp q_ccb
    q_hact q_dwn q_routt c_mm4 c_ev4 c_tr4 c_ev4b q_ffnT
    q_xp_in c_sqx c_xp q_xp_out q_y q_z c_zs c_gate q_wo0 q_wo1 c_tr5 c_ev5 c_mm5
    c_ev5b c_tr5b q_xpc c_asm q_out
    c_veps c_rstd c_red3 c_vep3 c_rst3 c_vepx c_rstx ch_v ch_s ch_p""".split()
    S = {n: nc.alloc_semaphore(n) for n in names}
    g = lambda n: S[n]

    LD = {}

    class DQ:
        def __init__(self, eng, semname):
            self.eng, self.sem, self.n = eng, S[semname], 0

        def dma(self, name, out, in_, **kw):
            if self.n >= 1:
                self.eng.wait_ge(self.sem, 16 * self.n)
            self.n += 1
            r = self.eng.dma_start(out=out, in_=in_, **kw).then_inc(self.sem, 16)
            if name is not None:
                LD[name] = (self.sem, 16 * self.n)
            return r

        def flush(self):
            if self.n >= 1:
                self.eng.wait_ge(self.sem, 16 * self.n)

    def wld(eng, name):
        sem, cnt = LD[name]
        eng.wait_ge(sem, cnt)

    class Chain:
        def __init__(self, semname):
            self.sem, self.n, self.last = S[semname], 0, None

        def bump(self, ins):
            ins.then_inc(self.sem, 1)
            self.n += 1
            self.last = (self.sem, self.n)
            return ins

        def mark(self, semname, val):
            self.last = (S[semname], val)

        def join(self, eng):
            if self.last is not None:
                eng.wait_ge(*self.last)

    chv, chs, chp = Chain("ch_v"), Chain("ch_s"), Chain("ch_p")

    dwn_sched = [(th, hc) for th in range(2) for hc in range(32)]

    with nc.Block() as blk:

        @blk.sync
        def _(sy):
            pid = nc.partition_id([SP])
            myg = pid % 4
            dq_misc = DQ(sy, "q_misc"); dq_x = DQ(sy, "q_x"); dq_exp = DQ(sy, "q_exp")
            dq_srep = DQ(sy, "q_srep"); dq_xd = DQ(sy, "q_xd"); dq_ssm = DQ(sy, "q_ssm")
            dq_xcsp = DQ(sy, "q_xcsp"); dq_ssml = DQ(sy, "q_ssml"); dq_moe2 = DQ(sy, "q_moe2")
            dq_brow = DQ(sy, "q_brow"); dq_rout = DQ(sy, "q_rout"); dq_dtsp = DQ(sy, "q_dtsp")
            dq_bc = DQ(sy, "q_bc"); dq_sc = DQ(sy, "q_sc"); dq_hact = DQ(sy, "q_hact")
            dq_routt = DQ(sy, "q_routt"); dq_ffnT = DQ(sy, "q_ffnT"); dq_xpin = DQ(sy, "q_xp_in")
            dq_xpout = DQ(sy, "q_xp_out"); dq_y = DQ(sy, "q_y"); dq_z = DQ(sy, "q_z")
            dq_wo = [DQ(sy, "q_wo0"), DQ(sy, "q_wo1")]
            dq_xpc = DQ(sy, "q_xpc"); dq_out = DQ(sy, "q_out")

            dq_misc.dma(None, out=s_identf[:], in_=identf)
            dq_misc.dma(None, out=s_convw[:, :, :], in_=convw.rearrange("(k p) j -> p k j", p=128))
            dq_misc.dma(None, out=s_convb[:, :], in_=convb)
            dq_misc.dma(None, out=s_dtbias[:, :], in_=dtbias)
            dq_misc.dma(None, out=s_dpar[:, :], in_=dpar)
            dq_misc.dma(None, out=s_A[:, :, :], in_=alog.rearrange("(k p) n -> p k n", p=128))
            dq_misc.dma(None, out=s_gamr[:, 0:R], in_=dtlnw.partition_broadcast(128))
            dq_misc.dma(None, out=s_gamr[:, R:R + NST], in_=blnw.partition_broadcast(128))
            dq_misc.dma("misc", out=s_gamr[:, R + NST:], in_=clnw.partition_broadcast(128))
            for tc in range(NTC):
                if tc >= 2:
                    sy.wait_ge(g("c_ln"), tc - 1)
                dq_x.dma(f"x{tc}", out=xt[:, tc % 2, :], in_=x_tok[ts(tc, 128), :])
            if debug_phase == 10:
                sy.wait_ge(g("c_xnf"), 256)
                dq_out.dma(None, out=dbg_out, in_=xnf[:, :, :].rearrange("p a b -> p (a b)"))
                dq_out.flush()
                return
            for og in range(32):
                sy.wait_ge(g("c_ev1"), EV_INP + 8 * (og + 1))
                dst, ro = (ccA_in, og * 512) if og < 16 else (zbufd, (og - 16) * 512)
                dq_exp.dma(f"exp{og}", out=dst[ro:ro + 512, :].rearrange("(j p) t -> p j t", p=128),
                           in_=evst[:, og % 2, :, :])
            if debug_phase == 11:
                dq_exp.flush()
                dq_out.dma(None, out=dbg_out, in_=zbufd)
                dq_out.flush()
                return
            sy.wait_ge(g("c_srow"), NTC)
            dq_srep.dma(None, out=srowd, in_=s_srow[:, :])
            dq_srep.dma("srep", out=s_rep[:, :], in_=srowd.partition_broadcast(128))
            for og in range(4):
                sy.wait_ge(g("c_ev1"), EV_EXP + 8 * (og + 1))
                dq_exp.dma(f"expm{og}", out=moed[og * 512:(og + 1) * 512, :].rearrange("(j p) t -> p j t", p=128),
                           in_=evst[:, og % 2, :, :])
            for og in range(16):
                sy.wait_ge(g("c_ev1"), EV_UP + 8 * (og + 1))
                dq_exp.dma(f"expu{og}", out=hactd[og * 512:(og + 1) * 512, :].rearrange("(j p) t -> p j t", p=128),
                           in_=evst[:, og % 2, :, :])
            if debug_phase == 1:
                dq_exp.flush()
                sy.wait_ge(g("q_cca"), 1)
                dq_out.dma(None, out=dbg_out, in_=zbufd)
                dq_out.flush()
                return

            sy.wait_ge(g("q_cca"), 1)
            sy.wait_ge(g("c_mm1"), EV_END)
            dq_exp.flush()
            for rr in range(4):
                srcx = ccA_out[rr * DIN:(rr + 1) * DIN, :].rearrange("(gg kd p) t -> gg p kd t", gg=4, p=128)[ts(myg, 1), :, :, :]
                dq_xd.dma("xd" if rr == 3 else None,
                          out=xdb[:, :, rr * 1024:(rr + 1) * 1024], in_=srcx)
            for gse in range(4):
                sy.wait_ge(g("c_ev2"), 8 * (gse + 1))
                dq_ssm.dma(f"ssm{gse}", out=ccC_in[gse * 1024:(gse + 1) * 1024, :].rearrange("(tc p) e -> p tc e", p=128),
                           in_=ssmst[:, :, :])
            sy.wait_ge(g("c_mm2"), LTC)
            dq_xcsp.dma("xcsp", out=xcspd.rearrange("(k p) t -> p k t", p=128), in_=xdb[:, :, :])
            if debug_phase == 21:
                sy.wait_ge(g("q_xcsp"), 16)
                dq_out.dma(None, out=dbg_out, in_=xcspd)
                dq_out.flush()
                return
            sy.wait_ge(g("q_xcsp"), 16)
            dq_moe2.dma("moe2", out=moesb[:, :, :], in_=moed.rearrange("(k p) t -> p k t", p=128))
            sy.wait_ge(g("q_ccc"), 1)
            if debug_phase == 22:
                dq_out.dma(None, out=dbg_out, in_=ccC_out)
                dq_out.flush()
                return
            dq_ssml.dma("ssml", out=ssm2[:, :, :], in_=ccC_out.rearrange("(tc p) e -> p tc e", p=128))
            for i in range(64):
                hc, th = divmod(i, 2)
                sy.wait_ge(g("c_evf"), i + 1)
                dq_rout.dma(f"rout{i}", out=routd[ts(hc, 128), ts(th, 512)], in_=fc2st[:, i % 2, :])
            sy.wait_ge(g("c_bcc"), 2 * LTC)
            dq_brow.dma(None, out=browd.rearrange("one (n tc j) -> one n tc j", n=NST, tc=LTC),
                        in_=btst[:, :, :])
            dq_brow.dma("brow", out=crowd.rearrange("one (n tc j) -> one n tc j", n=NST, tc=LTC),
                        in_=ctst[:, :, :])
            for i in range(2 * NKD):
                kd, hf = divmod(i, 2)
                sy.wait_ge(g("c_sp"), i + 1)
                dq_dtsp.dma(f"dtsp{i}", out=dtspd[ts(kd, 128), ts(hf, 2048)], in_=dtst[:, i % 2, :])
            if debug_phase == 2:
                dq_dtsp.flush()
                dq_out.dma(None, out=dbg_out, in_=dtspd)
                dq_out.flush()
                return

            dq_dtsp.flush()
            dq_brow.flush()
            dq_rout.flush()
            dq_hact.dma("hact0", out=hacth[:, :, :],
                        in_=hactd.rearrange("(k p) t -> p k t", p=128)[:, :, 0:512])

            def p4_interleave(i):
                if i < 64:
                    j = i
                    th4, hc4 = dwn_sched[j]
                    if j >= 3:
                        sy.wait_ge(g("c_ev4"), j - 2)
                    dq_routt.dma(f"routt{j}", out=routt[:, j % 3, :], in_=routd[ts(hc4, 128), ts(th4, 512)])
                je = i - 4
                if 0 <= je < 64:
                    th4, hc4 = dwn_sched[je]
                    sy.wait_ge(g("c_ev4b"), 4 * (je + 1))
                    dq_ffnT.dma(f"ffnT{je}",
                                out=ffnTd[th4 * 512:(th4 + 1) * 512, ts(hc4, 128)].rearrange("(sub p) c -> p sub c", p=128),
                                in_=ftst[:, je % 2, :, :])
                if i == 34:
                    sy.wait_ge(g("c_mm4"), 32)
                    dq_hact.dma("hact1", out=hacth[:, :, :],
                                in_=hactd.rearrange("(k p) t -> p k t", p=128)[:, :, 512:1024])

            for q in range(4):
                for n in range(NST):
                    if q > 0:
                        lt = 16 * q - 1
                        sy.wait_ge(g("c_b"), lt * 16 + n + 1)
                        if n == 0:
                            sy.wait_ge(g("c_h"), lt * 16 + 2)
                        else:
                            sy.wait_ge(g("c_hc"), lt * 15 + n)
                    dq_bc.dma(None, out=brep[:, n, :],
                              in_=browd[0:1, n * L + q * T:n * L + (q + 1) * T].partition_broadcast(128))
                    dq_bc.dma(f"bc{q}_{n}", out=crep[:, n, :],
                              in_=crowd[0:1, n * L + q * T:n * L + (q + 1) * T].partition_broadcast(128))
                for k in range(NKD):
                    i = q * NKD + k
                    if i >= 2:
                        sy.wait_ge(g("c_fin"), i - 1)
                    dq_sc.dma(None, out=dtq[:, i % 2, :], in_=dtspd[ts(k, 128), ts(q, T)])
                    dq_sc.dma(f"sc{i}", out=xcq[:, i % 2, :], in_=xcspd[ts(k, 128), ts(q, T)])
                    p4_interleave(i)
            for je in range(60, 64):
                th4, hc4 = dwn_sched[je]
                sy.wait_ge(g("c_ev4b"), 4 * (je + 1))
                dq_ffnT.dma(f"ffnT{je}",
                            out=ffnTd[th4 * 512:(th4 + 1) * 512, ts(hc4, 128)].rearrange("(sub p) c -> p sub c", p=128),
                            in_=ftst[:, je % 2, :, :])
            if debug_phase == 3:
                sy.wait_ge(g("q_yexp"), 16 * 64)
                dq_out.dma(None, out=dbg_out, in_=ccB_in)
                dq_out.flush()
                return

            dq_ffnT.flush()
            sy.wait_ge(g("c_fin"), 64)
            sy.wait_ge(g("q_yexp"), 16 * 64)
            dq_xpin.dma(None, out=gbr[:, 0, :], in_=lngamma.partition_broadcast(128))
            dq_xpin.dma("xpg", out=gbr[:, 1, :], in_=lnbeta.partition_broadcast(128))
            for tc in range(NTC):
                if tc >= 2:
                    sy.wait_ge(g("c_xp"), tc - 1)
                dq_xpin.dma(f"fft{tc}", out=fft[:, tc % 2, :], in_=ffnTd[ts(tc, 128), :])
                dq_xpin.dma(f"xrt{tc}", out=xrt[:, tc % 2, :], in_=x_tok[ts(tc, 128), :])
                if tc >= 2:
                    sy.wait_ge(g("c_xp"), tc - 1)
                    dq_xpout.dma(f"xpo{tc - 2}", out=xplusd[ts(tc - 2, 128), :], in_=xpt[:, tc % 2, :])
            for tc in range(NTC - 2, NTC):
                sy.wait_ge(g("c_xp"), tc + 1)
                dq_xpout.dma(f"xpo{tc}", out=xplusd[ts(tc, 128), :], in_=xpt[:, tc % 2, :])
            if debug_phase == 4:
                dq_xpout.flush()
                dq_out.dma(None, out=dbg_out, in_=xplusd)
                dq_out.flush()
                return

            sy.wait_ge(g("q_ccb"), 1)
            dq_xpout.flush()
            sy.wait_ge(g("c_mm4"), 64)
            for rr in range(4):
                srcy = ccB_out[rr * 4 * DSL:(rr + 1) * 4 * DSL, :].rearrange("(j kl p) t -> j p kl t", j=4, p=128)[ts(myg, 1), :, :, :]
                dq_y.dma("y" if rr == 3 else None,
                         out=ybuf[:, rr * 16:(rr + 1) * 16, :], in_=srcy)
            for i in range(64):
                if i >= 2:
                    sy.wait_ge(g("c_zs"), i - 1)
                dq_z.dma(f"z{i}", out=zkt[:, i % 2, :], in_=zbufd[ts(i, 128), :])
            def wo_load(i):
                hc4, qf = divmod(i, 4)
                if i >= 2:
                    sy.wait_ge(g("c_ev5"), i - 1)
                dq_wo[i % 2].dma(f"wo{i}", out=wosl[:, i % 2, :], in_=outwT[hc4, :, ts(qf, 2048)])
            wo_load(0)
            wo_load(1)
            for hc in range(32):
                if hc >= 2:
                    sy.wait_ge(g("c_asm"), (hc - 1) * 8)
                dq_xpc.dma(f"xpc{hc}", out=xpcs[:, hc % 2, :, :],
                           in_=xplusd.rearrange("(tc p) h -> p tc h", p=128)[:, :, ts(hc, 128)])
                for qq in range(4 * hc + 2, min(4 * hc + 6, 128)):
                    wo_load(qq)
                sy.wait_ge(g("c_asm"), (hc + 1) * 8)
                dq_out.dma(f"out{hc}", out=out_tok.rearrange("(tc p) h -> p tc h", p=128)[:, :, ts(hc, 128)],
                           in_=ost[:, hc % 2, :, :])
            dq_out.flush()

        @blk.gpsimd
        def _(gp):
            dq_gw = DQ(gp, "q_gw"); dq_w = [DQ(gp, "q_w0"), DQ(gp, "q_w1")]
            dq_wx = DQ(gp, "q_wx"); dq_fcs = DQ(gp, "q_fcs"); dq_dwn = DQ(gp, "q_dwn")
            dq_yexp = DQ(gp, "q_yexp")

            def slab_src(w, og, width=512):
                return w.rearrange("(k p) c -> p k c", p=128)[:, :, og * width:(og + 1) * width]
            if debug_phase == 10:
                return
            dq_gw.dma("gatew", out=gsb[:, :, :], in_=gatew.rearrange("(k p) e -> p k e", p=128))
            for og in range(32):
                if og >= 2:
                    gp.wait_ge(g("c_mm1"), EV_INP + 8 * (og - 1))
                dq_w[og % 2].dma(f"win{og}", out=wslab[:, og % 2, :, :], in_=w_in[og, :, :].rearrange("p (k c) -> p k c", c=512))
            if debug_phase == 11:
                return
            for og in range(4):
                gp.wait_ge(g("c_mm1"), (EV_INP + 256) if og < 2 else (EV_FC1 + 8 * (og - 1)))
                dq_w[og % 2].dma(f"fc1w{og}", out=wslab[:, og % 2, :, :], in_=fc1w[og, :, :].rearrange("p (k c) -> p k c", c=512))
            for og in range(4):
                gp.wait_ge(g("c_mm1"), (EV_FC1 + 32) if og < 2 else (EV_EXP + 8 * (og - 1)))
                dq_w[og % 2].dma(f"expw{og}", out=wslab[:, og % 2, 0:16, :], in_=expw[og, :, :].rearrange("p (k c) -> p k c", c=512))
            for og in range(16):
                gp.wait_ge(g("c_mm1"), (EV_EXP + 32) if og < 2 else (EV_UP + 8 * (og - 1)))
                dq_w[og % 2].dma(f"upw{og}", out=wslab[:, og % 2, :, :], in_=upw[og, :, :].rearrange("p (k c) -> p k c", c=512))
            wld(gp, "exp15")
            gp.collective_compute("AllGather", ALU.bypass, replica_groups=RG,
                                  ins=[ccA_in.opt()], outs=[ccA_out.opt()]).then_inc(g("q_cca"))
            if debug_phase == 1:
                return
            gp.wait_ge(g("c_mm1"), EV_END)
            wld(gp, "expu15")
            dq_wx.dma(None, out=wxa[:, :], in_=xprojw[0:128, :])
            dq_wx.dma(None, out=wxb[:, :], in_=xprojw[128:256, :])
            dq_wx.dma("wxc3", out=wxc[:, :], in_=xprojw[256:288, :])
            gp.wait_ge(g("q_xcsp"), 16)
            dq_wx.dma("wdtn", out=wdtnat[:, :, :], in_=dtprojw.rearrange("(k p) r -> p k r", p=128))
            if debug_phase == 21:
                return
            wld(gp, "ssm3")
            gp.collective_compute("AllReduce", ALU.add, replica_groups=RG,
                                  ins=[ccC_in.opt()], outs=[ccC_out.opt()]).then_inc(g("q_ccc"))
            if debug_phase == 22:
                return
            for hc in range(32):
                if hc >= 2:
                    gp.wait_ge(g("c_mmf"), 2 * (hc - 1))
                dq_fcs.dma(f"fcs{hc}", out=fcsl[:, hc % 2, :, :],
                           in_=fc2w[hc, :, :].rearrange("p (k c) -> p k c", c=128))
            if debug_phase in (2, 21, 22):
                return
            wld(gp, "dtsp31")
            wld(gp, "rout63")
            dwn_issued = 0
            for q in range(4):
                for k in range(NKD):
                    i = q * NKD + k
                    while dwn_issued < 64 and dwn_issued * 64 <= i * 64 + 32:
                        jj = dwn_issued
                        th4, hc4 = dwn_sched[jj]
                        if jj >= 2:
                            gp.wait_ge(g("c_mm4"), jj - 1)
                        dq_dwn.dma(f"dwn{jj}", out=dwsl[:, jj % 2, :, :],
                                   in_=downw[hc4, :, :].rearrange("p (k c) -> p k c", c=128))
                        dwn_issued += 1
                    for n in range(1, NST):
                        gp.wait_ge(g("c_hc"), i * 15 + n)
                        if n >= 2:
                            chp.join(gp)
                        gp.tensor_tensor(yac[:, :], yac[:, :], hcb[:, n % 2, :], ALU.add).then_inc(g("c_y"), 1)
                        chp.mark("c_y", i * 15 + n)
                    gp.wait_ge(g("c_fin"), i + 1)
                    dq_yexp.dma(f"yexp{i}", out=ccB_in[q * DSL + k * 128:q * DSL + (k + 1) * 128, :],
                                in_=y2t[:, :])
            while dwn_issued < 64:
                jj = dwn_issued
                th4, hc4 = dwn_sched[jj]
                if jj >= 2:
                    gp.wait_ge(g("c_mm4"), jj - 1)
                dq_dwn.dma(f"dwn{jj}", out=dwsl[:, jj % 2, :, :],
                           in_=downw[hc4, :, :].rearrange("p (k c) -> p k c", c=128))
                dwn_issued += 1
            dq_yexp.flush()
            gp.collective_compute("AllGather", ALU.bypass, replica_groups=RG,
                                  ins=[ccB_in.opt()], outs=[ccB_out.opt()]).then_inc(g("q_ccb"))

        @blk.tensor
        def _(t):
            wld(t, "misc")
            for tc in range(NTC):
                t.wait_ge(g("c_ln"), tc + 1)
                for hc in range(NKH):
                    i = tc * NKH + hc
                    if i >= 4:
                        t.wait_ge(g("c_xnf"), i - 3)
                    t.transpose(ps[:, i % 4, 0:128], xnstage[:, tc % 2, ts(hc, 128)], s_identf[:]).then_inc(g("c_tr0"), 1)
            def mm_slabs(n_og, nk, ev_base, wname, rhs_sb):
                for og in range(n_og):
                    wld(t, wname.format(og))
                    t.wait_ge(g("c_ev1"), ev_base + 8 * og)
                    for j in range(4):
                        for th in range(2):
                            last = None
                            for k in range(nk):
                                last = t.matmul(ps[:, j * 2 + th, :],
                                                wslab[:, og % 2, k, ts(j, 128)],
                                                rhs_sb[:, k, ts(th, 512)],
                                                start=(k == 0), stop=(k == nk - 1))
                            last.then_inc(g("c_mm1"), 1)
            if debug_phase == 10:
                return
            t.wait_ge(g("c_xnf"), 256)
            mm_slabs(32, NKH, EV_INP, "win{}", xnf)
            if debug_phase == 11:
                return
            wld(t, "gatew")
            for tc in range(NTC):
                t.wait_ge(g("c_ev1"), EV_GATE + tc)
                last = None
                for k in range(NKH):
                    last = t.matmul(ps[:, 6, 0:NE], xnf[:, k, ts(tc, 128)], gsb[:, k, :],
                                    start=(k == 0), stop=(k == NKH - 1))
                last.then_inc(g("c_mm1"), 1)
            for tc in range(NTC):
                t.wait_ge(g("c_scol"), tc + 1)
                if tc >= 1:
                    t.wait_ge(g("c_srow"), tc)
                t.transpose(ps[0:1, 7, 0:128], scols[:, tc:tc + 1], s_identf[:]).then_inc(g("c_str"), 1)
            t.wait_ge(g("c_srow"), NTC)
            mm_slabs(4, NKH, EV_FC1, "fc1w{}", xnf)
            t.wait_ge(g("c_ev1"), EV_FC1 + 32)
            mm_slabs(4, 16, EV_EXP, "expw{}", xlat)
            mm_slabs(16, NKH, EV_UP, "upw{}", xnf)
            if debug_phase == 1:
                return
            wld(t, "wxc3")
            for kd in range(NKD):
                if kd >= 1:
                    t.wait_ge(g("c_wxc"), 3 * kd)
                t.transpose(ps[:, 4, 0:128], wxa[:, ts(kd, 128)], s_identf[:])
                t.transpose(ps[:, 5, 0:128], wxb[:, ts(kd, 128)], s_identf[:])
                t.transpose(ps[:, 6, 0:32], wxc[0:32, ts(kd, 128)], s_identf[0:32, 0:32]).then_inc(g("c_trwx"), 3)
            wld(t, "xd")
            t.wait_ge(g("c_cvb"), NKD)
            t.wait_ge(g("c_wxc"), 3 * NKD)
            for tc in range(LTC):
                if tc >= 4:
                    t.wait_ge(g("c_ev2"), tc - 3)
                last = None
                for kd in range(NKD):
                    last = t.matmul(ps[:, tc % 4, 0:R + 2 * NST], xdb[:, kd, ts(tc, 128)], wxT[:, kd, :],
                                    start=(kd == 0), stop=(kd == NKD - 1))
                last.then_inc(g("c_mm2"), 1)
            if debug_phase in (21, 22):
                return
            t.wait_ge(g("c_ev2"), LTC)
            wld(t, "moe2")
            for i in range(64):
                hc, th = divmod(i, 2)
                wld(t, f"fcs{hc}")
                if i >= 4:
                    t.wait_ge(g("c_evf"), i - 3)
                last = None
                for k in range(16):
                    last = t.matmul(ps[:, i % 4, :], fcsl[:, hc % 2, k, :], moesb[:, k, ts(th, 512)],
                                    start=(k == 0), stop=(k == 15))
                last.then_inc(g("c_mmf"), 1)
            wld(t, "wdtn")
            for kd in range(NKD):
                for rc in range(2):
                    i = kd * 2 + rc
                    if i >= 1:
                        t.wait_ge(g("c_wdtc"), i)
                    t.transpose(ps[:, 7, 0:128], wdtnat[:, kd, ts(rc, 128)], s_identf[:]).then_inc(g("c_trwd"), 1)
            for tc in range(LTC):
                t.wait_ge(g("c_rms"), 3 * (tc + 1))
                if tc >= 1:
                    t.wait_ge(g("c_bcc"), 2 * tc)
                t.transpose(ps[0:16, 4, 0:128], bnt[:, tc % 4, :], s_identf[:])
                t.transpose(ps[0:16, 5, 0:128], cnt[:, tc % 4, :], s_identf[:]).then_inc(g("c_bct"), 2)
                for rc in range(2):
                    i = tc * 2 + rc
                    if i >= 1:
                        t.wait_ge(g("c_dtrc"), i)
                    t.transpose(ps[:, 6, 0:128], dtr32[:, tc % 4, ts(rc, 128)], s_identf[:]).then_inc(g("c_dtt"), 1)
            t.wait_ge(g("c_evf"), 64)
            t.wait_ge(g("c_dtrc"), 64)
            t.wait_ge(g("c_wdtc"), 32)
            for i in range(2 * NKD):
                kd, hf = divmod(i, 2)
                if i >= 1:
                    t.wait_ge(g("c_sp0"), i)
                last = None
                for f in range(4):
                    for rc in range(2):
                        last = t.matmul(ps[:, f, :], wdtT[:, rc, ts(kd, 128)],
                                        dtrT[:, rc, hf * 2048 + f * 512:hf * 2048 + (f + 1) * 512],
                                        start=(rc == 0), stop=(rc == 1))
                last.then_inc(g("c_mm3"), 1)
            if debug_phase in (2, 21, 22):
                return
            def p4trs(jt):
                t.wait_ge(g("c_ev4"), jt + 1)
                for sub in range(4):
                    si = jt * 4 + sub
                    if si >= 2:
                        t.wait_ge(g("c_ev4b"), si - 1)
                    t.transpose(ps[:, 6 + si % 2, 0:128], f4st[:, jt % 2, :][:, ts(sub, 128)],
                                s_identf[:]).then_inc(g("c_tr4"), 1)
            for j in range(64):
                if j % 32 == 0:
                    wld(t, f"hact{j // 32}")
                wld(t, f"dwn{j}")
                if j >= 2:
                    t.wait_ge(g("c_ev4"), j - 1)
                pb = (j % 2) * 2
                last = None
                for k in range(64):
                    last = t.matmul(ps[:, pb, :], dwsl[:, j % 2, k, :],
                                    hacth[:, k, :], start=(k == 0), stop=(k == 63))
                last.then_inc(g("c_mm4"), 1)
                if j >= 1:
                    p4trs(j - 1)
            p4trs(63)
            if debug_phase <= 4:
                return
            wld(t, "y")
            t.wait_ge(g("c_gate"), 64)
            for hc in range(32):
                if hc >= 2:
                    t.wait_ge(g("c_ev5b"), hc - 1)
                pb = (hc % 2) * 2
                last0 = last1 = None
                for k in range(64):
                    if k % 16 == 0:
                        t.wait_ge(g("c_ev5"), hc * 4 + k // 16 + 1)
                    last0 = t.matmul(ps[:, pb, :], wline[:, k, :], ybuf[:, k, 0:512],
                                     start=(k == 0), stop=(k == 63))
                    last1 = t.matmul(ps[:, pb + 1, :], wline[:, k, :], ybuf[:, k, 512:1024],
                                     start=(k == 0), stop=(k == 63))
                    if k in (15, 31, 47):
                        last1.then_inc(g("c_tr5"), 1)
                last0.then_inc(g("c_mm5"), 1)
                last1.then_inc(g("c_mm5"), 1)
                t.wait_ge(g("c_ev5b"), hc + 1)
                for tcc in range(8):
                    i = hc * 8 + tcc
                    if i >= 2:
                        t.wait_ge(g("c_asm"), i - 1)
                    t.transpose(ps[:, 6 + i % 2, 0:128], a5st[:, ts(tcc, 128)],
                                s_identf[:]).then_inc(g("c_tr5b"), 1)

        @blk.scalar
        def _(s):
            def xnf_copies(tcb):
                for hc in range(NKH):
                    i = tcb * NKH + hc
                    s.wait_ge(g("c_tr0"), i + 1)
                    s.copy(xnf[:, hc, ts(tcb, 128)], ps[:, i % 4, 0:128]).then_inc(g("c_xnf"), 1)
            for tc in range(NTC):
                if tc >= 2:
                    xnf_copies(tc - 2)
                wld(s, f"x{tc}")
                if tc >= 1:
                    s.wait_ge(g("c_lnr"), tc)
                if tc >= 2:
                    s.wait_ge(g("c_tr0"), 32 * (tc - 1))
                s.activation(xnstage[:, tc % 2, :], xt[:, tc % 2, :], AF.Square).then_inc(g("c_sq0"), 1)
                s.wait_ge(g("c_veps"), tc + 1)
                chs.bump(s.activation(lncols[:, 7:8], lncols[:, 6:7], AF.Ln))
                chs.join(s)
                s.activation(lncols[:, 7:8], lncols[:, 7:8], AF.Exp, scale=-0.5).then_inc(g("c_rstd"), 1)
            xnf_copies(NTC - 2)
            xnf_copies(NTC - 1)
            if debug_phase == 10 or debug_phase == 11:
                return
            for og in range(16):
                for jt in range(8):
                    idx = og * 8 + jt
                    j, th = divmod(jt, 2)
                    s.wait_ge(g("c_relu"), idx + 1)
                    s.activation(evst[:, og % 2, j, th * 512:(th + 1) * 512], rlst[:, idx % 2, :],
                                 AF.Square).then_inc(g("c_ev1"), 1)
            if debug_phase == 1:
                return
            for k in range(NKD):
                s.wait_ge(g("c_tap"), 4 * (k + 1))
                if k >= 1:
                    s.wait_ge(g("c_cvb"), k)
                s.activation(cvst[:, :], cacc[:, :], AF.Silu).then_inc(g("c_silu"), 1)
            if debug_phase in (21, 22):
                return
            wld(s, "ssml")
            for tc in range(LTC):
                if tc >= 1:
                    s.wait_ge(g("c_red3"), tc)
                s.activation(sqst[:, :], ssm2[:, tc, :], AF.Square).then_inc(g("c_sq2"), 1)
                s.wait_ge(g("c_vep3"), tc + 1)
                chs.bump(s.activation(rmcols[:, 3:4], rmcols[:, 3:4], AF.Ln))
                chs.bump(s.activation(rmcols[:, 4:5], rmcols[:, 4:5], AF.Ln))
                chs.bump(s.activation(rmcols[:, 5:6], rmcols[:, 5:6], AF.Ln))
                chs.join(s)
                s.activation(rmcols[:, 3:4], rmcols[:, 3:4], AF.Exp, scale=-0.5)
                s.activation(rmcols[:, 4:5], rmcols[:, 4:5], AF.Exp, scale=-0.5)
                s.activation(rmcols[:, 5:6], rmcols[:, 5:6], AF.Exp, scale=-0.5).then_inc(g("c_rst3"), 1)
            for i in range(2 * NKD):
                s.wait_ge(g("c_sp0"), i + 1)
                chs.bump(s.activation(spa[:, i % 2, :], spb[:, i % 2, :], AF.Abs))
                chs.join(s)
                chs.bump(s.activation(spa[:, i % 2, :], spa[:, i % 2, :], AF.Exp, scale=-1.0))
                chs.join(s)
                s.activation(spa[:, i % 2, :], spa[:, i % 2, :], AF.Ln, bias=1.0).then_inc(g("c_spa"), 1)
            if debug_phase in (2, 21, 22):
                return
            wld(s, "misc")
            chs.bump(s.activation(s_A[:, :, :], s_A[:, :, :], AF.Exp))
            chs.join(s)
            chs.bump(s.mul(s_A[:, :, :], s_A[:, :, :], -1.0))
            chs.join(s)
            for q in range(4):
                for k in range(NKD):
                    i = q * NKD + k
                    wld(s, f"sc{i}")
                    for n in range(NST):
                        idx = i * NST + n
                        if idx >= 3:
                            s.wait_ge(g("c_h"), idx - 2)
                        s.activation(abuf[:, idx % 3, :], dtq[:, i % 2, :], AF.Exp,
                                     scale=s_A[:, k, n:n + 1]).then_inc(g("c_a"), 1)
                    j = i - 3
                    if 0 <= j < 64:
                        if j >= 2:
                            wld(s, f"ffnT{j - 2}")
                        for sub in range(4):
                            si = j * 4 + sub
                            s.wait_ge(g("c_tr4"), si + 1)
                            s.copy(ftst[:, j % 2, sub, :], ps[:, 6 + si % 2, 0:128]).then_inc(g("c_ev4b"), 1)
            for j in (61, 62, 63):
                wld(s, f"ffnT{j - 2}")
                for sub in range(4):
                    si = j * 4 + sub
                    s.wait_ge(g("c_tr4"), si + 1)
                    s.copy(ftst[:, j % 2, sub, :], ps[:, 6 + si % 2, 0:128]).then_inc(g("c_ev4b"), 1)
            if debug_phase == 3:
                return
            for tc in range(NTC):
                wld(s, f"fft{tc}")
                if tc >= 1:
                    s.wait_ge(g("c_xp"), tc)
                s.activation(sqxt[:, :], fft[:, tc % 2, :], AF.Square).then_inc(g("c_sqx"), 1)
                s.wait_ge(g("c_vepx"), tc + 1)
                chs.bump(s.activation(xpcols[:, 7:8], xpcols[:, 6:7], AF.Ln))
                chs.join(s)
                s.activation(xpcols[:, 7:8], xpcols[:, 7:8], AF.Exp, scale=-0.5).then_inc(g("c_rstx"), 1)
            if debug_phase <= 4:
                return
            for i in range(64):
                wld(s, f"z{i}")
                if i >= 2:
                    s.wait_ge(g("c_gate"), i - 1)
                s.activation(zst[:, i % 2, :], zkt[:, i % 2, :], AF.Silu).then_inc(g("c_zs"), 1)
            for i in range(128):
                hc5, qf = divmod(i, 4)
                wld(s, f"wo{i}")
                if hc5 >= 1:
                    if qf < 3:
                        s.wait_ge(g("c_tr5"), (hc5 - 1) * 3 + qf + 1)
                    else:
                        s.wait_ge(g("c_mm5"), 2 * hc5)
                s.copy(wline[:, qf * 16:(qf + 1) * 16, :].rearrange("p a b -> p (a b)"),
                       wosl[:, i % 2, :]).then_inc(g("c_ev5"), 1)

        @blk.vector
        def _(v):
            for tc in range(NTC):
                wld(v, f"x{tc}")
                chv.bump(v.reduce_sum(lncols[:, 0:1], xt[:, tc % 2, :], axis=AX))
                chv.join(v)
                chv.bump(v.tensor_scalar(lncols[:, 1:2], lncols[:, 0:1], 1.0 / H, None, ALU.mult))
                v.wait_ge(g("c_sq0"), tc + 1)
                v.reduce_sum(lncols[:, 2:3], xnstage[:, tc % 2, :], axis=AX).then_inc(g("c_lnr"), 1)
                chv.mark("c_lnr", tc + 1)
                chv.join(v)
                chv.bump(v.tensor_scalar(lncols[:, 3:4], lncols[:, 2:3], 1.0 / H, None, ALU.mult))
                chv.bump(v.tensor_tensor(lncols[:, 4:5], lncols[:, 1:2], lncols[:, 1:2], ALU.mult))
                chv.join(v)
                chv.bump(v.tensor_tensor(lncols[:, 5:6], lncols[:, 3:4], lncols[:, 4:5], ALU.subtract))
                chv.join(v)
                v.tensor_scalar(lncols[:, 6:7], lncols[:, 5:6], 1.0, EPS_LN, ALU.mult, ALU.add).then_inc(g("c_veps"), 1)
                chv.mark("c_veps", tc + 1)
                if tc >= 2:
                    v.wait_ge(g("c_tr0"), 32 * (tc - 1))
                v.wait_ge(g("c_rstd"), tc + 1)
                v.tensor_scalar(xnstage[:, tc % 2, :], xt[:, tc % 2, :], lncols[:, 1:2], lncols[:, 7:8],
                                ALU.subtract, ALU.mult).then_inc(g("c_ln"), 1)
            if debug_phase == 10:
                return
            for og in range(32):
                if og >= 2:
                    wld(v, f"exp{og - 2}")
                for jt in range(8):
                    j, th = divmod(jt, 2)
                    v.wait_ge(g("c_mm1"), EV_INP + og * 8 + jt + 1)
                    v.tensor_copy(evst[:, og % 2, j, th * 512:(th + 1) * 512], ps[:, jt, :]).then_inc(g("c_ev1"), 1)
            if debug_phase == 11:
                return
            for tc in range(NTC):
                v.wait_ge(g("c_mm1"), EV_GATE + tc + 1)
                v.tensor_copy(logit[:, :], ps[:, 6, 0:NE]).then_inc(g("c_ev1"), 1)
                chv.mark("c_ev1", EV_GATE + tc + 1)
                chv.join(v)
                chv.bump(v.reduce_max(l_m[:, 0:1], logit[:, :], axis=AX))
                chv.join(v)
                chv.bump(v.tensor_scalar(lgm[:, :], logit[:, :], l_m[:, 0:1], -1e30, ALU.is_ge, ALU.mult))
                chv.join(v)
                chv.bump(v.tensor_tensor(lgm[:, :], lgm[:, :], logit[:, :], ALU.add))
                chv.join(v)
                chv.bump(v.reduce_max(l_m[:, 1:2], lgm[:, :], axis=AX))
                chv.join(v)
                v.tensor_tensor(scols[:, tc:tc + 1], l_m[:, 0:1], l_m[:, 1:2], ALU.add).then_inc(g("c_scol"), 1)
            for tc in range(NTC):
                v.wait_ge(g("c_str"), tc + 1)
                v.tensor_copy(s_srow[0:1, ts(tc, 128)], ps[0:1, 7, 0:128]).then_inc(g("c_srow"), 1)
            for og in range(4):
                for jt in range(8):
                    j, th = divmod(jt, 2)
                    v.wait_ge(g("c_mm1"), EV_FC1 + og * 8 + jt + 1)
                    v.tensor_copy(xlat[:, og * 4 + j, th * 512:(th + 1) * 512], ps[:, jt, :]).then_inc(g("c_ev1"), 1)
            wld(v, "srep")
            for og in range(4):
                wld(v, f"exp{30 + og}" if og < 2 else f"expm{og - 2}")
                for jt in range(8):
                    j, th = divmod(jt, 2)
                    v.wait_ge(g("c_mm1"), EV_EXP + og * 8 + jt + 1)
                    v.tensor_tensor(evst[:, og % 2, j, th * 512:(th + 1) * 512], ps[:, jt, :],
                                    s_rep[:, th * 512:(th + 1) * 512], ALU.mult).then_inc(g("c_ev1"), 1)
            for og in range(16):
                wld(v, f"expm{2 + og}" if og < 2 else f"expu{og - 2}")
                for jt in range(8):
                    idx = og * 8 + jt
                    v.wait_ge(g("c_mm1"), EV_UP + idx + 1)
                    if idx >= 2:
                        v.wait_ge(g("c_ev1"), EV_UP + idx - 1)
                    v.tensor_scalar(rlst[:, idx % 2, :], ps[:, jt, :], 0.0, None, ALU.max).then_inc(g("c_relu"), 1)
            if debug_phase == 1:
                return
            for kd in range(NKD):
                v.wait_ge(g("c_trwx"), 3 * (kd + 1))
                v.tensor_copy(wxT[:, kd, 0:128], ps[:, 4, 0:128])
                v.tensor_copy(wxT[:, kd, 128:256], ps[:, 5, 0:128])
                v.tensor_copy(wxT[:, kd, 256:288], ps[:, 6, 0:32]).then_inc(g("c_wxc"), 3)
            wld(v, "xd")
            wld(v, "misc")
            for k in range(NKD):
                v.tensor_scalar(cacc[:, :], xdb[:, k, :], s_convw[:, k, 3:4], s_convb[:, k:k + 1],
                                ALU.mult, ALU.add).then_inc(g("c_tap"), 1)
                chv.mark("c_tap", 4 * k + 1)
                for kk in range(3):
                    o = 3 - kk
                    chv.join(v)
                    v.scalar_tensor_tensor(cacc[:, o:L], xdb[:, k, 0:L - o], s_convw[:, k, kk:kk + 1],
                                           cacc[:, o:L], ALU.mult, ALU.add).then_inc(g("c_tap"), 1)
                    chv.mark("c_tap", 4 * k + 1 + kk + 1)
                v.wait_ge(g("c_silu"), k + 1)
                v.tensor_copy(xdb[:, k, :], cvst[:, :]).then_inc(g("c_cvb"), 1)
            for tc in range(LTC):
                v.wait_ge(g("c_mm2"), tc + 1)
                if tc >= 8:
                    wld(v, f"ssm{tc // 8 - 1}")
                v.tensor_copy(ssmst[:, tc % 8, :], ps[:, tc % 4, 0:R + 2 * NST]).then_inc(g("c_ev2"), 1)
            if debug_phase in (21, 22):
                return
            for i in range(64):
                hc, th = divmod(i, 2)
                v.wait_ge(g("c_mmf"), i + 1)
                if i >= 2:
                    wld(v, f"rout{i - 2}")
                v.tensor_copy(fc2st[:, i % 2, :], ps[:, i % 4, :]).then_inc(g("c_evf"), 1)
            for kd in range(NKD):
                for rc in range(2):
                    i = kd * 2 + rc
                    v.wait_ge(g("c_trwd"), i + 1)
                    v.tensor_copy(wdtT[:, rc, ts(kd, 128)], ps[:, 7, 0:128]).then_inc(g("c_wdtc"), 1)
            def bc_dtr_copies(tcb):
                v.wait_ge(g("c_bct"), 2 * (tcb + 1))
                v.tensor_copy(btst[:, tcb, :], ps[0:16, 4, 0:128])
                v.tensor_copy(ctst[:, tcb, :], ps[0:16, 5, 0:128]).then_inc(g("c_bcc"), 2)
                for rc in range(2):
                    i2 = tcb * 2 + rc
                    v.wait_ge(g("c_dtt"), i2 + 1)
                    v.tensor_copy(dtrT[:, rc, ts(tcb, 128)], ps[:, 6, 0:128]).then_inc(g("c_dtrc"), 1)
            wld(v, "ssml")
            for tc in range(LTC):
                v.wait_ge(g("c_sq2"), tc + 1)
                if tc >= 4:
                    v.wait_ge(g("c_dtt"), 2 * (tc - 3))
                    v.wait_ge(g("c_bct"), 2 * (tc - 3))
                v.reduce_sum(rmcols[:, 0:1], sqst[:, 0:R], axis=AX)
                v.reduce_sum(rmcols[:, 1:2], sqst[:, R:R + NST], axis=AX)
                v.reduce_sum(rmcols[:, 2:3], sqst[:, R + NST:], axis=AX).then_inc(g("c_red3"), 1)
                chv.mark("c_red3", tc + 1)
                chv.join(v)
                v.tensor_scalar(rmcols[:, 3:4], rmcols[:, 0:1], 1.0 / R, EPS_RMS, ALU.mult, ALU.add)
                v.tensor_scalar(rmcols[:, 4:5], rmcols[:, 1:2], 1.0 / NST, EPS_RMS, ALU.mult, ALU.add)
                v.tensor_scalar(rmcols[:, 5:6], rmcols[:, 2:3], 1.0 / NST, EPS_RMS, ALU.mult, ALU.add).then_inc(g("c_vep3"), 1)
                v.wait_ge(g("c_rst3"), tc + 1)
                v.scalar_tensor_tensor(dtr32[:, tc % 4, :], ssm2[:, tc, 0:R], rmcols[:, 3:4],
                                       s_gamr[:, 0:R], ALU.mult, ALU.mult).then_inc(g("c_rms"), 1)
                v.scalar_tensor_tensor(bnt[:, tc % 4, :], ssm2[:, tc, R:R + NST], rmcols[:, 4:5],
                                       s_gamr[:, R:R + NST], ALU.mult, ALU.mult).then_inc(g("c_rms"), 1)
                v.scalar_tensor_tensor(cnt[:, tc % 4, :], ssm2[:, tc, R + NST:], rmcols[:, 5:6],
                                       s_gamr[:, R + NST:], ALU.mult, ALU.mult).then_inc(g("c_rms"), 1)
                if tc >= 1:
                    bc_dtr_copies(tc - 1)
            bc_dtr_copies(LTC - 1)
            for i in range(2 * NKD):
                kd, hf = divmod(i, 2)
                v.wait_ge(g("c_mm3"), i + 1)
                if i >= 2:
                    wld(v, f"dtsp{i - 2}")
                v.tensor_scalar(spb[:, i % 2, :], ps[:, 0:4, :].rearrange("p b f -> p (b f)"),
                                s_dtbias[:, kd:kd + 1], None, ALU.add).then_inc(g("c_sp0"), 1)
                chv.mark("c_sp0", i + 1)
                v.wait_ge(g("c_spa"), i + 1)
                chv.join(v)
                chv.bump(v.tensor_scalar(spb[:, i % 2, :], spb[:, i % 2, :], 0.0, None, ALU.max))
                chv.join(v)
                v.tensor_tensor(dtst[:, i % 2, :], spb[:, i % 2, :], spa[:, i % 2, :], ALU.add).then_inc(g("c_sp"), 1)
            if debug_phase in (2, 21, 22):
                return
            v.memset(s_hstate[:, :], 0.0)
            for q in range(4):
                for k in range(NKD):
                    i = q * NKD + k
                    wld(v, f"sc{i}")
                    v.tensor_tensor(dtxq[:, :], dtq[:, i % 2, :], xcq[:, i % 2, :], ALU.mult).then_inc(g("c_dtx"), 1)
                    chv.mark("c_dtx", i + 1)
                    chv.join(v)
                    for n in range(NST):
                        idx = i * NST + n
                        kn = k * NST + n
                        v.wait_ge(g("c_a"), idx + 1)
                        wld(v, f"bc{q}_{n}")
                        v.tensor_tensor(bbuf[:, :], dtxq[:, :], brep[:, n, :], ALU.mult).then_inc(g("c_b"), 1)
                        chv.mark("c_b", i * NST + n + 1)
                        if n >= 1:
                            pn = n - 1
                            v.wait_ge(g("c_h"), idx)
                            v.tensor_copy(s_hstate[:, k * NST + pn:k * NST + pn + 1], hbuf[:, T - 1:T])
                            if pn == 0:
                                v.tensor_tensor(yac[:, :], hbuf[:, :], crep[:, 0, :], ALU.mult)
                            else:
                                if pn >= 3:
                                    v.wait_ge(g("c_y"), i * 15 + pn - 2)
                                v.tensor_tensor(hcb[:, pn % 2, :], hbuf[:, :], crep[:, pn, :], ALU.mult).then_inc(g("c_hc"), 1)
                        chv.join(v)
                        v.tensor_tensor_scan(hbuf[:, :], abuf[:, idx % 3, :], bbuf[:, :],
                                             s_hstate[:, kn:kn + 1], ALU.mult, ALU.add).then_inc(g("c_h"), 1)
                        chv.mark("c_h", idx + 1)
                    pn = NST - 1
                    v.wait_ge(g("c_h"), i * NST + NST)
                    v.tensor_copy(s_hstate[:, k * NST + pn:k * NST + pn + 1], hbuf[:, T - 1:T])
                    v.wait_ge(g("c_y"), i * 15 + pn - 2)
                    v.tensor_tensor(hcb[:, pn % 2, :], hbuf[:, :], crep[:, pn, :], ALU.mult).then_inc(g("c_hc"), 1)
                    v.wait_ge(g("c_y"), (i + 1) * 15)
                    if i >= 2:
                        wld(v, f"yexp{i - 2}")
                    v.scalar_tensor_tensor(y2t[:, :], xcq[:, i % 2, :], s_dpar[:, k:k + 1], yac[:, :],
                                           ALU.mult, ALU.add).then_inc(g("c_fin"), 1)
                    j = i - 2
                    if 0 <= j < 64:
                        v.wait_ge(g("c_mm4"), j + 1)
                        wld(v, f"routt{j}")
                        if j >= 2:
                            v.wait_ge(g("c_ev4b"), 4 * (j - 1))
                        pb4 = (j % 2) * 2
                        v.scalar_tensor_tensor(f4st[:, j % 2, :], ps[:, pb4, :], 1.0, routt[:, j % 3, :],
                                               ALU.mult, ALU.add).then_inc(g("c_ev4"), 1)
            for j in (62, 63):
                v.wait_ge(g("c_mm4"), j + 1)
                wld(v, f"routt{j}")
                v.wait_ge(g("c_ev4b"), 4 * (j - 1))
                pb4 = (j % 2) * 2
                v.scalar_tensor_tensor(f4st[:, j % 2, :], ps[:, pb4, :], 1.0, routt[:, j % 3, :],
                                       ALU.mult, ALU.add).then_inc(g("c_ev4"), 1)
            if debug_phase == 3:
                return
            wld(v, "xpg")
            for tc in range(NTC):
                wld(v, f"xrt{tc}")
                v.wait_ge(g("c_sqx"), tc + 1)
                if tc >= 2:
                    wld(v, f"xpo{tc - 2}")
                chv.bump(v.reduce_sum(xpcols[:, 0:1], fft[:, tc % 2, :], axis=AX))
                chv.join(v)
                chv.bump(v.tensor_scalar(xpcols[:, 1:2], xpcols[:, 0:1], 1.0 / H, None, ALU.mult))
                chv.bump(v.reduce_sum(xpcols[:, 2:3], sqxt[:, :], axis=AX))
                chv.join(v)
                chv.bump(v.tensor_scalar(xpcols[:, 3:4], xpcols[:, 2:3], 1.0 / H, None, ALU.mult))
                chv.bump(v.tensor_tensor(xpcols[:, 4:5], xpcols[:, 1:2], xpcols[:, 1:2], ALU.mult))
                chv.join(v)
                chv.bump(v.tensor_tensor(xpcols[:, 5:6], xpcols[:, 3:4], xpcols[:, 4:5], ALU.subtract))
                chv.join(v)
                v.tensor_scalar(xpcols[:, 6:7], xpcols[:, 5:6], 1.0, EPS_LN, ALU.mult, ALU.add).then_inc(g("c_vepx"), 1)
                chv.mark("c_vepx", tc + 1)
                v.wait_ge(g("c_rstx"), tc + 1)
                chv.bump(v.tensor_scalar(xpt[:, tc % 2, :], fft[:, tc % 2, :], xpcols[:, 1:2], xpcols[:, 7:8],
                                         ALU.subtract, ALU.mult))
                chv.join(v)
                chv.bump(v.tensor_tensor(xpt[:, tc % 2, :], xpt[:, tc % 2, :], gbr[:, 0, :], ALU.mult))
                chv.join(v)
                chv.bump(v.tensor_tensor(xpt[:, tc % 2, :], xpt[:, tc % 2, :], gbr[:, 1, :], ALU.add))
                chv.join(v)
                v.tensor_tensor(xpt[:, tc % 2, :], xpt[:, tc % 2, :], xrt[:, tc % 2, :], ALU.add).then_inc(g("c_xp"), 1)
            if debug_phase <= 4:
                return
            wld(v, "y")
            for i in range(64):
                v.wait_ge(g("c_zs"), i + 1)
                v.tensor_tensor(ybuf[:, i, :], ybuf[:, i, :], zst[:, i % 2, :], ALU.mult).then_inc(g("c_gate"), 1)
            for hc in range(32):
                v.wait_ge(g("c_mm5"), 2 * (hc + 1))
                if hc >= 1:
                    v.wait_ge(g("c_tr5b"), 8 * hc)
                pb = (hc % 2) * 2
                v.tensor_copy(a5st[:, :], ps[:, pb:pb + 2, :].rearrange("p b f -> p (b f)")).then_inc(g("c_ev5b"), 1)
                wld(v, f"xpc{hc}")
                if hc >= 2:
                    wld(v, f"out{hc - 2}")
                for tcc in range(8):
                    i = hc * 8 + tcc
                    v.wait_ge(g("c_tr5b"), i + 1)
                    v.tensor_tensor(ost[:, hc % 2, tcc, :], ps[:, 6 + i % 2, 0:128], xpcs[:, hc % 2, tcc, :],
                                    ALU.add).then_inc(g("c_asm"), 1)

    return nc


_CACHE = {}


def kernel(**inputs):
    nc = _CACHE.get("nc99")
    if nc is None:
        nc = build_nc()
        _CACHE["nc99"] = nc
    in_maps = make_in_maps(inputs)
    res = run_bass_kernel_spmd(nc, in_maps, core_ids=list(range(8)))
    out = np.empty((B, L, H), dtype=np.float32)
    for c in range(8):
        b, r = divmod(c, 4)
        out[b, r * T:(r + 1) * T, :] = res.results[c]["out_tok"]
    return out


def make_in_maps(inputs):
    f32 = np.float32
    X = np.ascontiguousarray(np.asarray(inputs["L0_attn_hidden_states"], dtype=f32))
    asf = lambda k: np.ascontiguousarray(np.asarray(inputs[k], dtype=f32))
    def blockw(w, ogw):
        KK, CC = w.shape
        nk, nog = KK // 128, CC // ogw
        return np.ascontiguousarray(w.reshape(nk, 128, nog, ogw).transpose(2, 1, 0, 3).reshape(nog, 128, nk * ogw))
    shared = dict(
        w_in=blockw(asf("L0_attn_in_proj_w"), 512),
        outwT=np.ascontiguousarray(
            asf("L0_attn_out_proj_w").reshape(32, 128, 64, 128).transpose(0, 3, 2, 1)).reshape(32, 128, 8192),
        dtlnw=asf("L0_attn_dt_ln_w").reshape(1, R),
        blnw=asf("L0_attn_B_ln_w").reshape(1, NST),
        clnw=asf("L0_attn_C_ln_w").reshape(1, NST),
        gatew=np.ascontiguousarray(asf("L0_ffn_gate_w").reshape(NKH, 128, NE).transpose(1, 0, 2).reshape(128, NKH * NE)),
        upw=blockw(asf("L0_ffn_up_w"), 512),
        downw=blockw(asf("L0_ffn_down_w"), 128),
        fc1w=blockw(asf("L0_ffn_fc1_w"), 512),
        fc2w=blockw(asf("L0_ffn_fc2_w"), 128),
        expw=blockw(asf("L0_ffn_expert_w"), 512),
        lngamma=asf("L0_ffn_ln_gamma").reshape(1, H),
        lnbeta=asf("L0_ffn_ln_beta").reshape(1, H),
        identf=np.eye(128, dtype=f32),
    )
    cw = asf("L0_attn_conv_w"); cb = asf("L0_attn_conv_b")
    xpw = asf("L0_attn_x_proj_w"); dpw = asf("L0_attn_dt_proj_w")
    dtb = asf("L0_attn_dt_bias"); al = asf("L0_attn_A_log"); dp = asf("L0_attn_D_param")
    in_maps = []
    for c in range(8):
        b, r = divmod(c, 4)
        sl = slice(r * DSL, (r + 1) * DSL)
        m = dict(shared)
        m["x_tok"] = np.ascontiguousarray(X[b, r * T:(r + 1) * T, :])
        m["convw"] = np.ascontiguousarray(cw[sl])
        m["convb"] = np.ascontiguousarray(cb[sl].reshape(NKD, 128).T)
        m["xprojw"] = np.ascontiguousarray(xpw[:, sl])
        m["dtprojw"] = np.ascontiguousarray(dpw[sl])
        m["dtbias"] = np.ascontiguousarray(dtb[sl].reshape(NKD, 128).T)
        m["alog"] = np.ascontiguousarray(al[sl])
        m["dpar"] = np.ascontiguousarray(dp[sl].reshape(NKD, 128).T)
        in_maps.append(m)
    return in_maps


# version 06 (final)
# speedup vs baseline: 1.1205x; 1.0076x over previous
import numpy as np
import ml_dtypes
import concourse.bass as bass
import concourse.mybir as mybir
from concourse.bass_utils import run_bass_kernel_spmd

F32 = mybir.dt.float32
BF16 = mybir.dt.bfloat16
AF = mybir.ActivationFunctionType
ALU = mybir.AluOpType
SP = mybir.EngineType.SP
AX = mybir.AxisListType.X

H = 4096; DIN = 8192; NST = 16; KCV = 4; R = 256
B = 2; L = 4096; T = 1024; DSL = 2048; E2 = 2 * DIN
NE = 8; INT = 8192; LAT = 2048
EPS_LN = 1e-5; EPS_RMS = 1e-6
NTC = 8; LTC = 32; NKH = 32; NKD = 16
RG = [[0, 1, 2, 3], [4, 5, 6, 7]]
ts = bass.ts

EV_INP, EV_GATE, EV_FC1, EV_EXP, EV_UP, EV_END = 0, 256, 264, 296, 328, 456


def build_nc(debug_phase=99, race_detect=True):
    nc = bass.Bass("TRN2", target_bir_lowering=False, detect_race_conditions=race_detect)

    def I(name, shape):
        return nc.dram_tensor(name, shape, F32, kind="ExternalInput").ap()
    x_tok = I("x_tok", [T, H]); w_in = I("w_in", [32, 128, E2])
    convw = I("convw", [DSL, KCV]); convb = I("convb", [128, NKD])
    xprojw = I("xprojw", [R + 2 * NST, DSL]); dtprojw = I("dtprojw", [DSL, R])
    dtbias = I("dtbias", [128, NKD]); alog = I("alog", [DSL, NST]); dpar = I("dpar", [128, NKD])
    outwT = I("outwT", [32, 128, 8192])
    dtlnw = I("dtlnw", [1, R]); blnw = I("blnw", [1, NST]); clnw = I("clnw", [1, NST])
    gatew = I("gatew", [128, NKH * NE]); upw = I("upw", [16, 128, 2 * INT]); downw = I("downw", [32, 128, INT // 128 * 128])
    fc1w = I("fc1w", [4, 128, NKH * 512]); fc2w = I("fc2w", [32, 128, LAT]); expw = I("expw", [4, 128, 16 * 512])
    lngamma = I("lngamma", [1, H]); lnbeta = I("lnbeta", [1, H])
    identf = I("identf", [128, 128])

    out_tok = nc.dram_tensor("out_tok", [T, H], F32, kind="ExternalOutput").ap()
    dbg_out = None
    if debug_phase < 99:
        shp = {1: ([DIN, T], BF16), 2: ([DSL, L], BF16), 3: ([4 * DSL, T], BF16),
               4: ([T, H], F32), 10: ([128, NKH * T], BF16), 11: ([DIN, T], BF16),
               21: ([DSL, L], BF16), 22: ([L, R + 2 * NST], F32)}[debug_phase]
        dbg_out = nc.dram_tensor("dbg_out", shp[0], shp[1], kind="ExternalOutput").ap()

    D = nc.dram_tensor
    ccA_in = D("ccA_in", [DIN, T], BF16, kind="Internal").ap()
    ccA_out = D("ccA_out", [4 * DIN, T], BF16, kind="Internal").ap()
    ccB_in = D("ccB_in", [4 * DSL, T], BF16, kind="Internal").ap()
    ccB_out = D("ccB_out", [16 * DSL, T], BF16, kind="Internal").ap()
    ccC_in = D("ccC_in", [L, R + 2 * NST], F32, kind="Internal").ap()
    ccC_out = D("ccC_out", [L, R + 2 * NST], F32, kind="Internal").ap()
    zbufd = D("zbufd", [DIN, T], BF16, kind="Internal").ap()
    hactd = D("hactd", [INT, T], BF16, kind="Internal").ap()
    moed = D("moed", [LAT, T], BF16, kind="Internal").ap()
    routd = D("routd", [H, T], BF16, kind="Internal").ap()
    dtspd = D("dtspd", [DSL, L], BF16, kind="Internal").ap()
    xcspd = D("xcspd", [DSL, L], BF16, kind="Internal").ap()
    ffnTd = D("ffnTd", [T, H], BF16, kind="Internal").ap()
    xplusd = D("xplusd", [T, H], F32, kind="Internal").ap()
    browd = D("browd", [1, NST * L], BF16, kind="Internal").ap()
    crowd = D("crowd", [1, NST * L], BF16, kind="Internal").ap()
    srowd = D("srowd", [1, T], BF16, kind="Internal").ap()

    ps = nc.alloc_psum_tensor([128, 8, 512], F32)

    AL = nc.alloc_sbuf_tensor
    s_identf = AL([128, 128], F32)
    s_A = AL([128, NKD, NST], F32)
    s_convw = AL([128, NKD, KCV], F32)
    s_convb = AL([128, NKD], F32)
    s_dtbias = AL([128, NKD], F32)
    s_dpar = AL([128, NKD], F32)
    s_hstate = AL([128, NKD * NST], F32)
    s_gamr = AL([128, R + 2 * NST], F32)
    s_srow = AL([1, T], BF16)
    gsb = AL([128, NKH, NE], BF16)
    base0 = nc.sbuf_base

    xnf = AL([128, NKH, T], BF16)
    wslab = AL([128, 2, NKH, 512], BF16)
    ovl0 = nc.sbuf_base
    xt = AL([128, 2, H], F32)
    xnstage = AL([128, 2, H], F32)
    lncols = AL([128, 16], F32)
    p0_end = nc.sbuf_base
    nc.sbuf_base = ovl0
    evst = AL([128, 2, 4, 1024], BF16)
    xlat = AL([128, 16, T], BF16)
    rlst = AL([128, 2, 512], BF16)
    s_rep = AL([128, T], BF16)
    logit = AL([128, NE], F32)
    lgm = AL([128, NE], F32)
    l_m = AL([128, 4], F32)
    scols = AL([128, NTC], F32)
    p1_end = nc.sbuf_base

    nc.sbuf_base = base0
    xdb = AL([128, NKD, L], BF16)
    cacc = AL([128, L], F32)
    cvst = AL([128, L], BF16)
    wxa = AL([128, DSL], F32)
    wxb = AL([128, DSL], F32)
    wxc = AL([32, DSL], F32)
    wxT = AL([128, NKD, R + 2 * NST], BF16)
    ssmst = AL([128, 8, R + 2 * NST], F32)
    p2a_end = nc.sbuf_base

    nc.sbuf_base = base0
    ssm2 = AL([128, LTC, R + 2 * NST], F32)
    sqst = AL([128, R + 2 * NST], F32)
    rmcols = AL([128, 8], F32)
    dtr32 = AL([128, 4, R], F32)
    bnt = AL([128, 4, NST], F32)
    cnt = AL([128, 4, NST], F32)
    btst = AL([16, LTC, 128], BF16)
    ctst = AL([16, LTC, 128], BF16)
    dtrT = AL([128, 2, L], BF16)
    wdtnat = AL([128, NKD, R], F32)
    wdtT = AL([128, 2, DSL], BF16)
    moesb = AL([128, 16, T], BF16)
    fcsl = AL([128, 2, 16, 128], BF16)
    fc2st = AL([128, 2, 512], BF16)
    spb = AL([128, 2, DSL], F32)
    spa = AL([128, 2, DSL], F32)
    dtst = AL([128, 2, DSL], BF16)
    p2b_end = nc.sbuf_base

    nc.sbuf_base = base0
    brep = AL([128, NST, T], BF16)
    crep = AL([128, NST, T], BF16)
    dtq = AL([128, 2, T], BF16)
    xcq = AL([128, 2, T], BF16)
    dtxq = AL([128, T], BF16)
    abuf = AL([128, 3, T], BF16)
    bbuf = AL([128, T], BF16)
    hbuf = AL([128, T], BF16)
    hcb = AL([128, 2, T], BF16)
    yac = AL([128, T], F32)
    y2t = AL([128, T], F32)
    hacth = AL([128, 64, 512], BF16)
    dwsl = AL([128, 2, 64, 128], BF16)
    routt = AL([128, 3, 512], BF16)
    f4st = AL([128, 2, 512], F32)
    ftst = AL([128, 2, 4, 128], BF16)
    p34_end = nc.sbuf_base

    nc.sbuf_base = base0
    gbr = AL([128, 2, H], F32)
    fft = AL([128, 2, H], BF16)
    xrt = AL([128, 2, H], F32)
    xpt = AL([128, 2, H], F32)
    sqxt = AL([128, H], F32)
    xpcols = AL([128, 8], F32)
    p4b_end = nc.sbuf_base

    nc.sbuf_base = base0
    ybuf = AL([128, 64, T], BF16)
    wosl = AL([128, 2, 2048], F32)
    wline = AL([128, 64, 128], BF16)
    zkt = AL([128, 2, T], BF16)
    zst = AL([128, 2, T], BF16)
    xpcs = AL([128, 2, 8, 128], F32)
    ost = AL([128, 2, 8, 128], F32)
    a5st = AL([128, 1024], F32)
    p5_end = nc.sbuf_base

    LIMIT = 229344
    for nm, e in [("p0", p0_end), ("p1", p1_end), ("p2a", p2a_end), ("p2b", p2b_end),
                  ("p34", p34_end), ("p4b", p4b_end), ("p5", p5_end)]:
        assert e <= LIMIT, f"SBUF overflow {nm}: {e} > {LIMIT}"
    nc.sbuf_base = max(p0_end, p1_end, p2a_end, p2b_end, p34_end, p4b_end, p5_end)

    names = """q_misc q_x c_sq0 c_lnr c_ln c_tr0 c_xnf q_w0 q_w1 q_gw c_mm1 c_ev1 q_exp
    c_scol c_str c_srow q_srep c_relu q_cca q_xd c_tap c_silu c_cvb
    q_wx c_trwx c_wxc c_trwd c_wdtc c_mm2 c_ev2 q_ssm q_ccc q_ssml c_sq2 c_rms
    c_bct c_bcc c_dtt c_dtrc q_brow q_moe2 q_fcs c_mmf c_evf q_rout c_mm3 c_sp0 c_spa c_sp
    q_dtsp q_xcsp q_bc q_sc c_dtx c_a c_b c_h c_hc c_y c_fin q_yexp q_ccb
    q_hact q_dwn q_routt c_mm4 c_ev4 c_tr4 c_ev4b q_ffnT
    q_xp_in c_sqx c_xp q_xp_out q_y q_z c_zs c_gate q_wo0 q_wo1 c_tr5 c_ev5 c_mm5
    c_ev5b c_tr5b q_xpc c_asm q_out
    c_veps c_rstd c_red3 c_vep3 c_rst3 c_vepx c_rstx ch_v ch_s ch_p""".split()
    S = {n: nc.alloc_semaphore(n) for n in names}
    g = lambda n: S[n]

    LD = {}

    class DQ:
        def __init__(self, eng, semname):
            self.eng, self.sem, self.n = eng, S[semname], 0

        def dma(self, name, out, in_, **kw):
            if self.n >= 1:
                self.eng.wait_ge(self.sem, 16 * self.n)
            self.n += 1
            r = self.eng.dma_start(out=out, in_=in_, **kw).then_inc(self.sem, 16)
            if name is not None:
                LD[name] = (self.sem, 16 * self.n)
            return r

        def flush(self):
            if self.n >= 1:
                self.eng.wait_ge(self.sem, 16 * self.n)

    def wld(eng, name):
        sem, cnt = LD[name]
        eng.wait_ge(sem, cnt)

    class Chain:
        def __init__(self, semname):
            self.sem, self.n, self.last = S[semname], 0, None

        def bump(self, ins):
            ins.then_inc(self.sem, 1)
            self.n += 1
            self.last = (self.sem, self.n)
            return ins

        def mark(self, semname, val):
            self.last = (S[semname], val)

        def join(self, eng):
            if self.last is not None:
                eng.wait_ge(*self.last)

    chv, chs, chp = Chain("ch_v"), Chain("ch_s"), Chain("ch_p")

    dwn_sched = [(th, hc) for th in range(2) for hc in range(32)]

    with nc.Block() as blk:

        @blk.sync
        def _(sy):
            pid = nc.partition_id([SP])
            myg = pid % 4
            dq_misc = DQ(sy, "q_misc"); dq_x = DQ(sy, "q_x"); dq_exp = DQ(sy, "q_exp")
            dq_srep = DQ(sy, "q_srep"); dq_ssm = DQ(sy, "q_ssm"); dq_xd = DQ(sy, "q_xd")
            dq_xcsp = DQ(sy, "q_xcsp"); dq_ssml = DQ(sy, "q_ssml"); dq_moe2 = DQ(sy, "q_moe2")
            dq_brow = DQ(sy, "q_brow"); dq_rout = DQ(sy, "q_rout"); dq_dtsp = DQ(sy, "q_dtsp")
            dq_bc = DQ(sy, "q_bc"); dq_sc = DQ(sy, "q_sc"); dq_hact = DQ(sy, "q_hact")
            dq_routt = DQ(sy, "q_routt"); dq_ffnT = DQ(sy, "q_ffnT"); dq_xpin = DQ(sy, "q_xp_in")
            dq_xpout = DQ(sy, "q_xp_out"); dq_y = DQ(sy, "q_y"); dq_z = DQ(sy, "q_z")
            dq_wo = [DQ(sy, "q_wo0"), DQ(sy, "q_wo1")]
            dq_xds = [dq_xd, dq_wo[0], dq_wo[1], dq_z]
            dq_ys = [dq_y, dq_ssm, dq_sc, dq_hact]
            dq_xpin2 = dq_bc
            dq_xpc = DQ(sy, "q_xpc"); dq_out = DQ(sy, "q_out")

            dq_misc.dma(None, out=s_identf[:], in_=identf)
            dq_misc.dma(None, out=s_convw[:, :, :], in_=convw.rearrange("(k p) j -> p k j", p=128))
            dq_misc.dma(None, out=s_convb[:, :], in_=convb)
            dq_misc.dma(None, out=s_dtbias[:, :], in_=dtbias)
            dq_misc.dma(None, out=s_dpar[:, :], in_=dpar)
            dq_misc.dma(None, out=s_A[:, :, :], in_=alog.rearrange("(k p) n -> p k n", p=128))
            dq_misc.dma(None, out=s_gamr[:, 0:R], in_=dtlnw.partition_broadcast(128))
            dq_misc.dma(None, out=s_gamr[:, R:R + NST], in_=blnw.partition_broadcast(128))
            dq_misc.dma("misc", out=s_gamr[:, R + NST:], in_=clnw.partition_broadcast(128))
            for tc in range(NTC):
                if tc >= 2:
                    sy.wait_ge(g("c_ln"), tc - 1)
                dq_x.dma(f"x{tc}", out=xt[:, tc % 2, :], in_=x_tok[ts(tc, 128), :])
            if debug_phase == 10:
                sy.wait_ge(g("c_xnf"), 256)
                dq_out.dma(None, out=dbg_out, in_=xnf[:, :, :].rearrange("p a b -> p (a b)"))
                dq_out.flush()
                return
            for og in range(32):
                sy.wait_ge(g("c_ev1"), EV_INP + 8 * (og + 1))
                dst, ro = (ccA_in, og * 512) if og < 16 else (zbufd, (og - 16) * 512)
                dq_exp.dma(f"exp{og}", out=dst[ro:ro + 512, :].rearrange("(j p) t -> p j t", p=128),
                           in_=evst[:, og % 2, :, :])
            if debug_phase == 11:
                dq_exp.flush()
                dq_out.dma(None, out=dbg_out, in_=zbufd)
                dq_out.flush()
                return
            sy.wait_ge(g("c_srow"), NTC)
            dq_srep.dma(None, out=srowd, in_=s_srow[:, :])
            dq_srep.dma("srep", out=s_rep[:, :], in_=srowd.partition_broadcast(128))
            for og in range(4):
                sy.wait_ge(g("c_ev1"), EV_EXP + 8 * (og + 1))
                dq_exp.dma(f"expm{og}", out=moed[og * 512:(og + 1) * 512, :].rearrange("(j p) t -> p j t", p=128),
                           in_=evst[:, og % 2, :, :])
            for og in range(16):
                sy.wait_ge(g("c_ev1"), EV_UP + 8 * (og + 1))
                dq_exp.dma(f"expu{og}", out=hactd[og * 512:(og + 1) * 512, :].rearrange("(j p) t -> p j t", p=128),
                           in_=evst[:, og % 2, :, :])
            if debug_phase == 1:
                dq_exp.flush()
                sy.wait_ge(g("q_cca"), 1)
                dq_out.dma(None, out=dbg_out, in_=zbufd)
                dq_out.flush()
                return

            sy.wait_ge(g("q_cca"), 1)
            sy.wait_ge(g("c_mm1"), EV_END)
            dq_exp.flush()
            for rr in range(4):
                srcx = ccA_out[rr * DIN:(rr + 1) * DIN, :].rearrange("(gg kd p) t -> gg p kd t", gg=4, p=128)[ts(myg, 1), :, :, :]
                dq_xds[rr].dma(f"xd{rr}",
                               out=xdb[:, :, rr * 1024:(rr + 1) * 1024], in_=srcx)
            for gse in range(4):
                sy.wait_ge(g("c_ev2"), 8 * (gse + 1))
                dq_ssm.dma(f"ssm{gse}", out=ccC_in[gse * 1024:(gse + 1) * 1024, :].rearrange("(tc p) e -> p tc e", p=128),
                           in_=ssmst[:, :, :])
            sy.wait_ge(g("c_mm2"), LTC)
            dq_xcsp.dma("xcsp", out=xcspd.rearrange("(k p) t -> p k t", p=128), in_=xdb[:, :, :])
            if debug_phase == 21:
                sy.wait_ge(g("q_xcsp"), 16)
                dq_out.dma(None, out=dbg_out, in_=xcspd)
                dq_out.flush()
                return
            sy.wait_ge(g("q_xcsp"), 16)
            dq_moe2.dma("moe2", out=moesb[:, :, :], in_=moed.rearrange("(k p) t -> p k t", p=128))
            sy.wait_ge(g("q_ccc"), 1)
            if debug_phase == 22:
                dq_out.dma(None, out=dbg_out, in_=ccC_out)
                dq_out.flush()
                return
            dq_ssml.dma("ssml", out=ssm2[:, :, :], in_=ccC_out.rearrange("(tc p) e -> p tc e", p=128))
            for i in range(64):
                hc, th = divmod(i, 2)
                sy.wait_ge(g("c_evf"), i + 1)
                dq_rout.dma(f"rout{i}", out=routd[ts(hc, 128), ts(th, 512)], in_=fc2st[:, i % 2, :])
            sy.wait_ge(g("c_bcc"), 2 * LTC)
            dq_brow.dma(None, out=browd.rearrange("one (n tc j) -> one n tc j", n=NST, tc=LTC),
                        in_=btst[:, :, :])
            dq_brow.dma("brow", out=crowd.rearrange("one (n tc j) -> one n tc j", n=NST, tc=LTC),
                        in_=ctst[:, :, :])
            for i in range(2 * NKD):
                kd, hf = divmod(i, 2)
                sy.wait_ge(g("c_sp"), i + 1)
                dq_dtsp.dma(f"dtsp{i}", out=dtspd[ts(kd, 128), ts(hf, 2048)], in_=dtst[:, i % 2, :])
            if debug_phase == 2:
                dq_dtsp.flush()
                dq_out.dma(None, out=dbg_out, in_=dtspd)
                dq_out.flush()
                return

            dq_dtsp.flush()
            dq_brow.flush()
            dq_rout.flush()
            dq_hact.dma("hact0", out=hacth[:, :, :],
                        in_=hactd.rearrange("(k p) t -> p k t", p=128)[:, :, 0:512])

            def p4_interleave(i):
                if i < 64:
                    j = i
                    th4, hc4 = dwn_sched[j]
                    if j >= 3:
                        sy.wait_ge(g("c_ev4"), j - 2)
                    dq_routt.dma(f"routt{j}", out=routt[:, j % 3, :], in_=routd[ts(hc4, 128), ts(th4, 512)])
                je = i - 4
                if 0 <= je < 64:
                    th4, hc4 = dwn_sched[je]
                    sy.wait_ge(g("c_ev4b"), 4 * (je + 1))
                    dq_ffnT.dma(f"ffnT{je}",
                                out=ffnTd[th4 * 512:(th4 + 1) * 512, ts(hc4, 128)].rearrange("(sub p) c -> p sub c", p=128),
                                in_=ftst[:, je % 2, :, :])
                if i == 34:
                    sy.wait_ge(g("c_mm4"), 32)
                    dq_hact.dma("hact1", out=hacth[:, :, :],
                                in_=hactd.rearrange("(k p) t -> p k t", p=128)[:, :, 512:1024])

            for q in range(4):
                for n in range(NST):
                    if q > 0:
                        lt = 16 * q - 1
                        sy.wait_ge(g("c_b"), lt * 16 + n + 1)
                        if n == 0:
                            sy.wait_ge(g("c_h"), lt * 16 + 2)
                        else:
                            sy.wait_ge(g("c_hc"), lt * 15 + n)
                    dq_bc.dma(None, out=brep[:, n, :],
                              in_=browd[0:1, n * L + q * T:n * L + (q + 1) * T].partition_broadcast(128))
                    dq_bc.dma(f"bc{q}_{n}", out=crep[:, n, :],
                              in_=crowd[0:1, n * L + q * T:n * L + (q + 1) * T].partition_broadcast(128))
                for k in range(NKD):
                    i = q * NKD + k
                    if i >= 2:
                        sy.wait_ge(g("c_fin"), i - 1)
                    dq_sc.dma(None, out=dtq[:, i % 2, :], in_=dtspd[ts(k, 128), ts(q, T)])
                    dq_sc.dma(f"sc{i}", out=xcq[:, i % 2, :], in_=xcspd[ts(k, 128), ts(q, T)])
                    p4_interleave(i)
            for je in range(60, 64):
                th4, hc4 = dwn_sched[je]
                sy.wait_ge(g("c_ev4b"), 4 * (je + 1))
                dq_ffnT.dma(f"ffnT{je}",
                            out=ffnTd[th4 * 512:(th4 + 1) * 512, ts(hc4, 128)].rearrange("(sub p) c -> p sub c", p=128),
                            in_=ftst[:, je % 2, :, :])
            if debug_phase == 3:
                sy.wait_ge(g("q_yexp"), 16 * 64)
                dq_out.dma(None, out=dbg_out, in_=ccB_in)
                dq_out.flush()
                return

            dq_ffnT.flush()
            sy.wait_ge(g("c_fin"), 64)
            sy.wait_ge(g("q_yexp"), 16 * 64)
            dq_xpin.dma(None, out=gbr[:, 0, :], in_=lngamma.partition_broadcast(128))
            dq_xpin.dma("xpg", out=gbr[:, 1, :], in_=lnbeta.partition_broadcast(128))
            for tc in range(NTC):
                if tc >= 2:
                    sy.wait_ge(g("c_xp"), tc - 1)
                dq_xpin.dma(f"fft{tc}", out=fft[:, tc % 2, :], in_=ffnTd[ts(tc, 128), :])
                dq_xpin2.dma(f"xrt{tc}", out=xrt[:, tc % 2, :], in_=x_tok[ts(tc, 128), :])
                if tc >= 2:
                    sy.wait_ge(g("c_xp"), tc - 1)
                    dq_xpout.dma(f"xpo{tc - 2}", out=xplusd[ts(tc - 2, 128), :], in_=xpt[:, tc % 2, :])
            for tc in range(NTC - 2, NTC):
                sy.wait_ge(g("c_xp"), tc + 1)
                dq_xpout.dma(f"xpo{tc}", out=xplusd[ts(tc, 128), :], in_=xpt[:, tc % 2, :])
            if debug_phase == 4:
                dq_xpout.flush()
                dq_out.dma(None, out=dbg_out, in_=xplusd)
                dq_out.flush()
                return

            sy.wait_ge(g("q_ccb"), 1)
            dq_xpout.flush()
            sy.wait_ge(g("c_mm4"), 64)
            for rr in range(4):
                srcy = ccB_out[rr * 4 * DSL:(rr + 1) * 4 * DSL, :].rearrange("(j kl p) t -> j p kl t", j=4, p=128)[ts(myg, 1), :, :, :]
                dq_ys[rr].dma(f"y{rr}",
                              out=ybuf[:, rr * 16:(rr + 1) * 16, :], in_=srcy)
            for i in range(64):
                if i >= 2:
                    sy.wait_ge(g("c_zs"), i - 1)
                dq_z.dma(f"z{i}", out=zkt[:, i % 2, :], in_=zbufd[ts(i, 128), :])
            def wo_load(i):
                hc4, qf = divmod(i, 4)
                if i >= 2:
                    sy.wait_ge(g("c_ev5"), i - 1)
                dq_wo[i % 2].dma(f"wo{i}", out=wosl[:, i % 2, :], in_=outwT[hc4, :, ts(qf, 2048)])
            wo_load(0)
            wo_load(1)
            for hc in range(32):
                if hc >= 2:
                    sy.wait_ge(g("c_asm"), (hc - 1) * 8)
                dq_xpc.dma(f"xpc{hc}", out=xpcs[:, hc % 2, :, :],
                           in_=xplusd.rearrange("(tc p) h -> p tc h", p=128)[:, :, ts(hc, 128)])
                for qq in range(4 * hc + 2, min(4 * hc + 6, 128)):
                    wo_load(qq)
                sy.wait_ge(g("c_asm"), (hc + 1) * 8)
                dq_out.dma(f"out{hc}", out=out_tok.rearrange("(tc p) h -> p tc h", p=128)[:, :, ts(hc, 128)],
                           in_=ost[:, hc % 2, :, :])
            dq_out.flush()

        @blk.gpsimd
        def _(gp):
            dq_gw = DQ(gp, "q_gw"); dq_w = [DQ(gp, "q_w0"), DQ(gp, "q_w1")]
            dq_wx = DQ(gp, "q_wx"); dq_fcs = DQ(gp, "q_fcs"); dq_dwn = DQ(gp, "q_dwn")
            dq_yexp = DQ(gp, "q_yexp")

            def slab_src(w, og, width=512):
                return w.rearrange("(k p) c -> p k c", p=128)[:, :, og * width:(og + 1) * width]
            if debug_phase == 10:
                return
            dq_gw.dma("gatew", out=gsb[:, :, :], in_=gatew.rearrange("(k p) e -> p k e", p=128))
            for og in range(32):
                if og >= 2:
                    gp.wait_ge(g("c_mm1"), EV_INP + 8 * (og - 1))
                dq_w[og % 2].dma(f"win{og}", out=wslab[:, og % 2, :, :], in_=w_in[og, :, :].rearrange("p (k c) -> p k c", c=512))
            if debug_phase == 11:
                return
            for og in range(4):
                gp.wait_ge(g("c_mm1"), (EV_INP + 256) if og < 2 else (EV_FC1 + 8 * (og - 1)))
                dq_w[og % 2].dma(f"fc1w{og}", out=wslab[:, og % 2, :, :], in_=fc1w[og, :, :].rearrange("p (k c) -> p k c", c=512))
            for og in range(4):
                gp.wait_ge(g("c_mm1"), (EV_FC1 + 32) if og < 2 else (EV_EXP + 8 * (og - 1)))
                dq_w[og % 2].dma(f"expw{og}", out=wslab[:, og % 2, 0:16, :], in_=expw[og, :, :].rearrange("p (k c) -> p k c", c=512))
            for og in range(16):
                gp.wait_ge(g("c_mm1"), (EV_EXP + 32) if og < 2 else (EV_UP + 8 * (og - 1)))
                dq_w[og % 2].dma(f"upw{og}", out=wslab[:, og % 2, :, :], in_=upw[og, :, :].rearrange("p (k c) -> p k c", c=512))
            wld(gp, "exp15")
            gp.collective_compute("AllGather", ALU.bypass, replica_groups=RG,
                                  ins=[ccA_in.opt()], outs=[ccA_out.opt()]).then_inc(g("q_cca"))
            if debug_phase == 1:
                return
            gp.wait_ge(g("c_mm1"), EV_END)
            wld(gp, "expu15")
            dq_wx.dma(None, out=wxa[:, :], in_=xprojw[0:128, :])
            dq_wx.dma(None, out=wxb[:, :], in_=xprojw[128:256, :])
            dq_wx.dma("wxc3", out=wxc[:, :], in_=xprojw[256:288, :])
            gp.wait_ge(g("q_xcsp"), 16)
            dq_wx.dma("wdtn", out=wdtnat[:, :, :], in_=dtprojw.rearrange("(k p) r -> p k r", p=128))
            if debug_phase == 21:
                return
            wld(gp, "ssm3")
            gp.collective_compute("AllReduce", ALU.add, replica_groups=RG,
                                  ins=[ccC_in.opt()], outs=[ccC_out.opt()]).then_inc(g("q_ccc"))
            if debug_phase == 22:
                return
            for hc in range(32):
                if hc >= 2:
                    gp.wait_ge(g("c_mmf"), 2 * (hc - 1))
                dq_fcs.dma(f"fcs{hc}", out=fcsl[:, hc % 2, :, :],
                           in_=fc2w[hc, :, :].rearrange("p (k c) -> p k c", c=128))
            if debug_phase in (2, 21, 22):
                return
            wld(gp, "dtsp31")
            wld(gp, "rout63")
            dwn_issued = 0
            for q in range(4):
                for k in range(NKD):
                    i = q * NKD + k
                    while dwn_issued < 64 and dwn_issued * 64 <= i * 64 + 32:
                        jj = dwn_issued
                        th4, hc4 = dwn_sched[jj]
                        if jj >= 2:
                            gp.wait_ge(g("c_mm4"), jj - 1)
                        dq_dwn.dma(f"dwn{jj}", out=dwsl[:, jj % 2, :, :],
                                   in_=downw[hc4, :, :].rearrange("p (k c) -> p k c", c=128))
                        dwn_issued += 1
                    for n in range(1, NST):
                        gp.wait_ge(g("c_hc"), i * 15 + n)
                        if n >= 2:
                            chp.join(gp)
                        gp.tensor_tensor(yac[:, :], yac[:, :], hcb[:, n % 2, :], ALU.add).then_inc(g("c_y"), 1)
                        chp.mark("c_y", i * 15 + n)
                    gp.wait_ge(g("c_fin"), i + 1)
                    dq_yexp.dma(f"yexp{i}", out=ccB_in[q * DSL + k * 128:q * DSL + (k + 1) * 128, :],
                                in_=y2t[:, :])
            while dwn_issued < 64:
                jj = dwn_issued
                th4, hc4 = dwn_sched[jj]
                if jj >= 2:
                    gp.wait_ge(g("c_mm4"), jj - 1)
                dq_dwn.dma(f"dwn{jj}", out=dwsl[:, jj % 2, :, :],
                           in_=downw[hc4, :, :].rearrange("p (k c) -> p k c", c=128))
                dwn_issued += 1
            dq_yexp.flush()
            gp.collective_compute("AllGather", ALU.bypass, replica_groups=RG,
                                  ins=[ccB_in.opt()], outs=[ccB_out.opt()]).then_inc(g("q_ccb"))

        @blk.tensor
        def _(t):
            wld(t, "misc")
            for tc in range(NTC):
                t.wait_ge(g("c_ln"), tc + 1)
                for hc in range(NKH):
                    i = tc * NKH + hc
                    if i >= 4:
                        t.wait_ge(g("c_xnf"), i - 3)
                    t.transpose(ps[:, i % 4, 0:128], xnstage[:, tc % 2, ts(hc, 128)], s_identf[:]).then_inc(g("c_tr0"), 1)
            def mm_slabs(n_og, nk, ev_base, wname, rhs_sb):
                for og in range(n_og):
                    wld(t, wname.format(og))
                    t.wait_ge(g("c_ev1"), ev_base + 8 * og)
                    for j in range(4):
                        for th in range(2):
                            last = None
                            for k in range(nk):
                                last = t.matmul(ps[:, j * 2 + th, :],
                                                wslab[:, og % 2, k, ts(j, 128)],
                                                rhs_sb[:, k, ts(th, 512)],
                                                start=(k == 0), stop=(k == nk - 1))
                            last.then_inc(g("c_mm1"), 1)
            if debug_phase == 10:
                return
            t.wait_ge(g("c_xnf"), 256)
            mm_slabs(32, NKH, EV_INP, "win{}", xnf)
            if debug_phase == 11:
                return
            wld(t, "gatew")
            for tc in range(NTC):
                t.wait_ge(g("c_ev1"), EV_GATE + tc)
                last = None
                for k in range(NKH):
                    last = t.matmul(ps[:, 6, 0:NE], xnf[:, k, ts(tc, 128)], gsb[:, k, :],
                                    start=(k == 0), stop=(k == NKH - 1))
                last.then_inc(g("c_mm1"), 1)
            for tc in range(NTC):
                t.wait_ge(g("c_scol"), tc + 1)
                if tc >= 1:
                    t.wait_ge(g("c_srow"), tc)
                t.transpose(ps[0:1, 7, 0:128], scols[:, tc:tc + 1], s_identf[:]).then_inc(g("c_str"), 1)
            t.wait_ge(g("c_srow"), NTC)
            mm_slabs(4, NKH, EV_FC1, "fc1w{}", xnf)
            t.wait_ge(g("c_ev1"), EV_FC1 + 32)
            mm_slabs(4, 16, EV_EXP, "expw{}", xlat)
            mm_slabs(16, NKH, EV_UP, "upw{}", xnf)
            if debug_phase == 1:
                return
            wld(t, "wxc3")
            for kd in range(NKD):
                if kd >= 1:
                    t.wait_ge(g("c_wxc"), 3 * kd)
                t.transpose(ps[:, 4, 0:128], wxa[:, ts(kd, 128)], s_identf[:])
                t.transpose(ps[:, 5, 0:128], wxb[:, ts(kd, 128)], s_identf[:])
                t.transpose(ps[:, 6, 0:32], wxc[0:32, ts(kd, 128)], s_identf[0:32, 0:32]).then_inc(g("c_trwx"), 3)
            for rr in range(4):
                wld(t, f"xd{rr}")
            t.wait_ge(g("c_cvb"), NKD)
            t.wait_ge(g("c_wxc"), 3 * NKD)
            for tc in range(LTC):
                if tc >= 4:
                    t.wait_ge(g("c_ev2"), tc - 3)
                last = None
                for kd in range(NKD):
                    last = t.matmul(ps[:, tc % 4, 0:R + 2 * NST], xdb[:, kd, ts(tc, 128)], wxT[:, kd, :],
                                    start=(kd == 0), stop=(kd == NKD - 1))
                last.then_inc(g("c_mm2"), 1)
            if debug_phase in (21, 22):
                return
            t.wait_ge(g("c_ev2"), LTC)
            wld(t, "moe2")
            for i in range(64):
                hc, th = divmod(i, 2)
                wld(t, f"fcs{hc}")
                if i >= 4:
                    t.wait_ge(g("c_evf"), i - 3)
                last = None
                for k in range(16):
                    last = t.matmul(ps[:, i % 4, :], fcsl[:, hc % 2, k, :], moesb[:, k, ts(th, 512)],
                                    start=(k == 0), stop=(k == 15))
                last.then_inc(g("c_mmf"), 1)
            wld(t, "wdtn")
            for kd in range(NKD):
                for rc in range(2):
                    i = kd * 2 + rc
                    if i >= 1:
                        t.wait_ge(g("c_wdtc"), i)
                    t.transpose(ps[:, 7, 0:128], wdtnat[:, kd, ts(rc, 128)], s_identf[:]).then_inc(g("c_trwd"), 1)
            for tc in range(LTC):
                t.wait_ge(g("c_rms"), 3 * (tc + 1))
                if tc >= 1:
                    t.wait_ge(g("c_bcc"), 2 * tc)
                t.transpose(ps[0:16, 4, 0:128], bnt[:, tc % 4, :], s_identf[:])
                t.transpose(ps[0:16, 5, 0:128], cnt[:, tc % 4, :], s_identf[:]).then_inc(g("c_bct"), 2)
                for rc in range(2):
                    i = tc * 2 + rc
                    if i >= 1:
                        t.wait_ge(g("c_dtrc"), i)
                    t.transpose(ps[:, 6, 0:128], dtr32[:, tc % 4, ts(rc, 128)], s_identf[:]).then_inc(g("c_dtt"), 1)
            t.wait_ge(g("c_evf"), 64)
            t.wait_ge(g("c_dtrc"), 64)
            t.wait_ge(g("c_wdtc"), 32)
            for i in range(2 * NKD):
                kd, hf = divmod(i, 2)
                if i >= 1:
                    t.wait_ge(g("c_sp0"), i)
                last = None
                for f in range(4):
                    for rc in range(2):
                        last = t.matmul(ps[:, f, :], wdtT[:, rc, ts(kd, 128)],
                                        dtrT[:, rc, hf * 2048 + f * 512:hf * 2048 + (f + 1) * 512],
                                        start=(rc == 0), stop=(rc == 1))
                last.then_inc(g("c_mm3"), 1)
            if debug_phase in (2, 21, 22):
                return
            def p4trs(jt):
                t.wait_ge(g("c_ev4"), jt + 1)
                for sub in range(4):
                    si = jt * 4 + sub
                    if si >= 2:
                        t.wait_ge(g("c_ev4b"), si - 1)
                    t.transpose(ps[:, 6 + si % 2, 0:128], f4st[:, jt % 2, :][:, ts(sub, 128)],
                                s_identf[:]).then_inc(g("c_tr4"), 1)
            for j in range(64):
                if j % 32 == 0:
                    wld(t, f"hact{j // 32}")
                wld(t, f"dwn{j}")
                if j >= 2:
                    t.wait_ge(g("c_ev4"), j - 1)
                pb = (j % 2) * 2
                last = None
                for k in range(64):
                    last = t.matmul(ps[:, pb, :], dwsl[:, j % 2, k, :],
                                    hacth[:, k, :], start=(k == 0), stop=(k == 63))
                last.then_inc(g("c_mm4"), 1)
                if j >= 1:
                    p4trs(j - 1)
            p4trs(63)
            if debug_phase <= 4:
                return
            for rr in range(4):
                wld(t, f"y{rr}")
            t.wait_ge(g("c_gate"), 64)
            for hc in range(32):
                if hc >= 2:
                    t.wait_ge(g("c_ev5b"), hc - 1)
                pb = (hc % 2) * 2
                last0 = last1 = None
                for k in range(64):
                    if k % 16 == 0:
                        t.wait_ge(g("c_ev5"), hc * 4 + k // 16 + 1)
                    last0 = t.matmul(ps[:, pb, :], wline[:, k, :], ybuf[:, k, 0:512],
                                     start=(k == 0), stop=(k == 63))
                    last1 = t.matmul(ps[:, pb + 1, :], wline[:, k, :], ybuf[:, k, 512:1024],
                                     start=(k == 0), stop=(k == 63))
                    if k in (15, 31, 47):
                        last1.then_inc(g("c_tr5"), 1)
                last0.then_inc(g("c_mm5"), 1)
                last1.then_inc(g("c_mm5"), 1)
                t.wait_ge(g("c_ev5b"), hc + 1)
                for tcc in range(8):
                    i = hc * 8 + tcc
                    if i >= 2:
                        t.wait_ge(g("c_asm"), i - 1)
                    t.transpose(ps[:, 6 + i % 2, 0:128], a5st[:, ts(tcc, 128)],
                                s_identf[:]).then_inc(g("c_tr5b"), 1)

        @blk.scalar
        def _(s):
            def xnf_copies(tcb):
                for hc in range(NKH):
                    i = tcb * NKH + hc
                    s.wait_ge(g("c_tr0"), i + 1)
                    s.copy(xnf[:, hc, ts(tcb, 128)], ps[:, i % 4, 0:128]).then_inc(g("c_xnf"), 1)
            for tc in range(NTC):
                if tc >= 2:
                    xnf_copies(tc - 2)
                wld(s, f"x{tc}")
                if tc >= 1:
                    s.wait_ge(g("c_lnr"), tc)
                if tc >= 2:
                    s.wait_ge(g("c_tr0"), 32 * (tc - 1))
                s.activation(xnstage[:, tc % 2, :], xt[:, tc % 2, :], AF.Square).then_inc(g("c_sq0"), 1)
                s.wait_ge(g("c_veps"), tc + 1)
                chs.bump(s.activation(lncols[:, 7:8], lncols[:, 6:7], AF.Ln))
                chs.join(s)
                s.activation(lncols[:, 7:8], lncols[:, 7:8], AF.Exp, scale=-0.5).then_inc(g("c_rstd"), 1)
            xnf_copies(NTC - 2)
            xnf_copies(NTC - 1)
            if debug_phase == 10 or debug_phase == 11:
                return
            for og in range(16):
                for jt in range(8):
                    idx = og * 8 + jt
                    j, th = divmod(jt, 2)
                    s.wait_ge(g("c_relu"), idx + 1)
                    s.activation(evst[:, og % 2, j, th * 512:(th + 1) * 512], rlst[:, idx % 2, :],
                                 AF.Square).then_inc(g("c_ev1"), 1)
            if debug_phase == 1:
                return
            for k in range(NKD):
                s.wait_ge(g("c_tap"), 4 * (k + 1))
                if k >= 1:
                    s.wait_ge(g("c_cvb"), k)
                s.activation(cvst[:, :], cacc[:, :], AF.Silu).then_inc(g("c_silu"), 1)
            if debug_phase in (21, 22):
                return
            wld(s, "ssml")
            for tc in range(LTC):
                if tc >= 1:
                    s.wait_ge(g("c_red3"), tc)
                s.activation(sqst[:, :], ssm2[:, tc, :], AF.Square).then_inc(g("c_sq2"), 1)
                s.wait_ge(g("c_vep3"), tc + 1)
                chs.bump(s.activation(rmcols[:, 3:4], rmcols[:, 3:4], AF.Ln))
                chs.bump(s.activation(rmcols[:, 4:5], rmcols[:, 4:5], AF.Ln))
                chs.bump(s.activation(rmcols[:, 5:6], rmcols[:, 5:6], AF.Ln))
                chs.join(s)
                s.activation(rmcols[:, 3:4], rmcols[:, 3:4], AF.Exp, scale=-0.5)
                s.activation(rmcols[:, 4:5], rmcols[:, 4:5], AF.Exp, scale=-0.5)
                s.activation(rmcols[:, 5:6], rmcols[:, 5:6], AF.Exp, scale=-0.5).then_inc(g("c_rst3"), 1)
            for i in range(2 * NKD):
                s.wait_ge(g("c_sp0"), i + 1)
                chs.bump(s.activation(spa[:, i % 2, :], spb[:, i % 2, :], AF.Abs))
                chs.join(s)
                chs.bump(s.activation(spa[:, i % 2, :], spa[:, i % 2, :], AF.Exp, scale=-1.0))
                chs.join(s)
                s.activation(spa[:, i % 2, :], spa[:, i % 2, :], AF.Ln, bias=1.0).then_inc(g("c_spa"), 1)
            if debug_phase in (2, 21, 22):
                return
            wld(s, "misc")
            chs.bump(s.activation(s_A[:, :, :], s_A[:, :, :], AF.Exp))
            chs.join(s)
            chs.bump(s.mul(s_A[:, :, :], s_A[:, :, :], -1.0))
            chs.join(s)
            for q in range(4):
                for k in range(NKD):
                    i = q * NKD + k
                    wld(s, f"sc{i}")
                    for n in range(NST):
                        idx = i * NST + n
                        if idx >= 3:
                            s.wait_ge(g("c_h"), idx - 2)
                        s.activation(abuf[:, idx % 3, :], dtq[:, i % 2, :], AF.Exp,
                                     scale=s_A[:, k, n:n + 1]).then_inc(g("c_a"), 1)
                    j = i - 3
                    if 0 <= j < 64:
                        if j >= 2:
                            wld(s, f"ffnT{j - 2}")
                        for sub in range(4):
                            si = j * 4 + sub
                            s.wait_ge(g("c_tr4"), si + 1)
                            s.copy(ftst[:, j % 2, sub, :], ps[:, 6 + si % 2, 0:128]).then_inc(g("c_ev4b"), 1)
            for j in (61, 62, 63):
                wld(s, f"ffnT{j - 2}")
                for sub in range(4):
                    si = j * 4 + sub
                    s.wait_ge(g("c_tr4"), si + 1)
                    s.copy(ftst[:, j % 2, sub, :], ps[:, 6 + si % 2, 0:128]).then_inc(g("c_ev4b"), 1)
            if debug_phase == 3:
                return
            for tc in range(NTC):
                wld(s, f"fft{tc}")
                if tc >= 1:
                    s.wait_ge(g("c_xp"), tc)
                s.activation(sqxt[:, :], fft[:, tc % 2, :], AF.Square).then_inc(g("c_sqx"), 1)
                s.wait_ge(g("c_vepx"), tc + 1)
                chs.bump(s.activation(xpcols[:, 7:8], xpcols[:, 6:7], AF.Ln))
                chs.join(s)
                s.activation(xpcols[:, 7:8], xpcols[:, 7:8], AF.Exp, scale=-0.5).then_inc(g("c_rstx"), 1)
            if debug_phase <= 4:
                return
            for i in range(64):
                wld(s, f"z{i}")
                if i >= 2:
                    s.wait_ge(g("c_gate"), i - 1)
                s.activation(zst[:, i % 2, :], zkt[:, i % 2, :], AF.Silu).then_inc(g("c_zs"), 1)
            for i in range(128):
                hc5, qf = divmod(i, 4)
                wld(s, f"wo{i}")
                if hc5 >= 1:
                    if qf < 3:
                        s.wait_ge(g("c_tr5"), (hc5 - 1) * 3 + qf + 1)
                    else:
                        s.wait_ge(g("c_mm5"), 2 * hc5)
                s.copy(wline[:, qf * 16:(qf + 1) * 16, :].rearrange("p a b -> p (a b)"),
                       wosl[:, i % 2, :]).then_inc(g("c_ev5"), 1)

        @blk.vector
        def _(v):
            for tc in range(NTC):
                wld(v, f"x{tc}")
                chv.bump(v.reduce_sum(lncols[:, 0:1], xt[:, tc % 2, :], axis=AX))
                chv.join(v)
                chv.bump(v.tensor_scalar(lncols[:, 1:2], lncols[:, 0:1], 1.0 / H, None, ALU.mult))
                v.wait_ge(g("c_sq0"), tc + 1)
                v.reduce_sum(lncols[:, 2:3], xnstage[:, tc % 2, :], axis=AX).then_inc(g("c_lnr"), 1)
                chv.mark("c_lnr", tc + 1)
                chv.join(v)
                chv.bump(v.tensor_scalar(lncols[:, 3:4], lncols[:, 2:3], 1.0 / H, None, ALU.mult))
                chv.bump(v.tensor_tensor(lncols[:, 4:5], lncols[:, 1:2], lncols[:, 1:2], ALU.mult))
                chv.join(v)
                chv.bump(v.tensor_tensor(lncols[:, 5:6], lncols[:, 3:4], lncols[:, 4:5], ALU.subtract))
                chv.join(v)
                v.tensor_scalar(lncols[:, 6:7], lncols[:, 5:6], 1.0, EPS_LN, ALU.mult, ALU.add).then_inc(g("c_veps"), 1)
                chv.mark("c_veps", tc + 1)
                if tc >= 2:
                    v.wait_ge(g("c_tr0"), 32 * (tc - 1))
                v.wait_ge(g("c_rstd"), tc + 1)
                v.tensor_scalar(xnstage[:, tc % 2, :], xt[:, tc % 2, :], lncols[:, 1:2], lncols[:, 7:8],
                                ALU.subtract, ALU.mult).then_inc(g("c_ln"), 1)
            if debug_phase == 10:
                return
            for og in range(32):
                if og >= 2:
                    wld(v, f"exp{og - 2}")
                for jt in range(8):
                    j, th = divmod(jt, 2)
                    v.wait_ge(g("c_mm1"), EV_INP + og * 8 + jt + 1)
                    v.tensor_copy(evst[:, og % 2, j, th * 512:(th + 1) * 512], ps[:, jt, :]).then_inc(g("c_ev1"), 1)
            if debug_phase == 11:
                return
            for tc in range(NTC):
                v.wait_ge(g("c_mm1"), EV_GATE + tc + 1)
                v.tensor_copy(logit[:, :], ps[:, 6, 0:NE]).then_inc(g("c_ev1"), 1)
                chv.mark("c_ev1", EV_GATE + tc + 1)
                chv.join(v)
                chv.bump(v.reduce_max(l_m[:, 0:1], logit[:, :], axis=AX))
                chv.join(v)
                chv.bump(v.tensor_scalar(lgm[:, :], logit[:, :], l_m[:, 0:1], -1e30, ALU.is_ge, ALU.mult))
                chv.join(v)
                chv.bump(v.tensor_tensor(lgm[:, :], lgm[:, :], logit[:, :], ALU.add))
                chv.join(v)
                chv.bump(v.reduce_max(l_m[:, 1:2], lgm[:, :], axis=AX))
                chv.join(v)
                v.tensor_tensor(scols[:, tc:tc + 1], l_m[:, 0:1], l_m[:, 1:2], ALU.add).then_inc(g("c_scol"), 1)
            for tc in range(NTC):
                v.wait_ge(g("c_str"), tc + 1)
                v.tensor_copy(s_srow[0:1, ts(tc, 128)], ps[0:1, 7, 0:128]).then_inc(g("c_srow"), 1)
            for og in range(4):
                for jt in range(8):
                    j, th = divmod(jt, 2)
                    v.wait_ge(g("c_mm1"), EV_FC1 + og * 8 + jt + 1)
                    v.tensor_copy(xlat[:, og * 4 + j, th * 512:(th + 1) * 512], ps[:, jt, :]).then_inc(g("c_ev1"), 1)
            wld(v, "srep")
            for og in range(4):
                wld(v, f"exp{30 + og}" if og < 2 else f"expm{og - 2}")
                for jt in range(8):
                    j, th = divmod(jt, 2)
                    v.wait_ge(g("c_mm1"), EV_EXP + og * 8 + jt + 1)
                    v.tensor_tensor(evst[:, og % 2, j, th * 512:(th + 1) * 512], ps[:, jt, :],
                                    s_rep[:, th * 512:(th + 1) * 512], ALU.mult).then_inc(g("c_ev1"), 1)
            for og in range(16):
                wld(v, f"expm{2 + og}" if og < 2 else f"expu{og - 2}")
                for jt in range(8):
                    idx = og * 8 + jt
                    v.wait_ge(g("c_mm1"), EV_UP + idx + 1)
                    if idx >= 2:
                        v.wait_ge(g("c_ev1"), EV_UP + idx - 1)
                    v.tensor_scalar(rlst[:, idx % 2, :], ps[:, jt, :], 0.0, None, ALU.max).then_inc(g("c_relu"), 1)
            if debug_phase == 1:
                return
            for kd in range(NKD):
                v.wait_ge(g("c_trwx"), 3 * (kd + 1))
                v.tensor_copy(wxT[:, kd, 0:128], ps[:, 4, 0:128])
                v.tensor_copy(wxT[:, kd, 128:256], ps[:, 5, 0:128])
                v.tensor_copy(wxT[:, kd, 256:288], ps[:, 6, 0:32]).then_inc(g("c_wxc"), 3)
            for rr in range(4):
                wld(v, f"xd{rr}")
            wld(v, "misc")
            for k in range(NKD):
                v.tensor_scalar(cacc[:, :], xdb[:, k, :], s_convw[:, k, 3:4], s_convb[:, k:k + 1],
                                ALU.mult, ALU.add).then_inc(g("c_tap"), 1)
                chv.mark("c_tap", 4 * k + 1)
                for kk in range(3):
                    o = 3 - kk
                    chv.join(v)
                    v.scalar_tensor_tensor(cacc[:, o:L], xdb[:, k, 0:L - o], s_convw[:, k, kk:kk + 1],
                                           cacc[:, o:L], ALU.mult, ALU.add).then_inc(g("c_tap"), 1)
                    chv.mark("c_tap", 4 * k + 1 + kk + 1)
                v.wait_ge(g("c_silu"), k + 1)
                v.tensor_copy(xdb[:, k, :], cvst[:, :]).then_inc(g("c_cvb"), 1)
            for tc in range(LTC):
                v.wait_ge(g("c_mm2"), tc + 1)
                if tc >= 8:
                    wld(v, f"ssm{tc // 8 - 1}")
                v.tensor_copy(ssmst[:, tc % 8, :], ps[:, tc % 4, 0:R + 2 * NST]).then_inc(g("c_ev2"), 1)
            if debug_phase in (21, 22):
                return
            for i in range(64):
                hc, th = divmod(i, 2)
                v.wait_ge(g("c_mmf"), i + 1)
                if i >= 2:
                    wld(v, f"rout{i - 2}")
                v.tensor_copy(fc2st[:, i % 2, :], ps[:, i % 4, :]).then_inc(g("c_evf"), 1)
            for kd in range(NKD):
                for rc in range(2):
                    i = kd * 2 + rc
                    v.wait_ge(g("c_trwd"), i + 1)
                    v.tensor_copy(wdtT[:, rc, ts(kd, 128)], ps[:, 7, 0:128]).then_inc(g("c_wdtc"), 1)
            def bc_dtr_copies(tcb):
                v.wait_ge(g("c_bct"), 2 * (tcb + 1))
                v.tensor_copy(btst[:, tcb, :], ps[0:16, 4, 0:128])
                v.tensor_copy(ctst[:, tcb, :], ps[0:16, 5, 0:128]).then_inc(g("c_bcc"), 2)
                for rc in range(2):
                    i2 = tcb * 2 + rc
                    v.wait_ge(g("c_dtt"), i2 + 1)
                    v.tensor_copy(dtrT[:, rc, ts(tcb, 128)], ps[:, 6, 0:128]).then_inc(g("c_dtrc"), 1)
            wld(v, "ssml")
            for tc in range(LTC):
                v.wait_ge(g("c_sq2"), tc + 1)
                if tc >= 4:
                    v.wait_ge(g("c_dtt"), 2 * (tc - 3))
                    v.wait_ge(g("c_bct"), 2 * (tc - 3))
                v.reduce_sum(rmcols[:, 0:1], sqst[:, 0:R], axis=AX)
                v.reduce_sum(rmcols[:, 1:2], sqst[:, R:R + NST], axis=AX)
                v.reduce_sum(rmcols[:, 2:3], sqst[:, R + NST:], axis=AX).then_inc(g("c_red3"), 1)
                chv.mark("c_red3", tc + 1)
                chv.join(v)
                v.tensor_scalar(rmcols[:, 3:4], rmcols[:, 0:1], 1.0 / R, EPS_RMS, ALU.mult, ALU.add)
                v.tensor_scalar(rmcols[:, 4:5], rmcols[:, 1:2], 1.0 / NST, EPS_RMS, ALU.mult, ALU.add)
                v.tensor_scalar(rmcols[:, 5:6], rmcols[:, 2:3], 1.0 / NST, EPS_RMS, ALU.mult, ALU.add).then_inc(g("c_vep3"), 1)
                v.wait_ge(g("c_rst3"), tc + 1)
                v.scalar_tensor_tensor(dtr32[:, tc % 4, :], ssm2[:, tc, 0:R], rmcols[:, 3:4],
                                       s_gamr[:, 0:R], ALU.mult, ALU.mult).then_inc(g("c_rms"), 1)
                v.scalar_tensor_tensor(bnt[:, tc % 4, :], ssm2[:, tc, R:R + NST], rmcols[:, 4:5],
                                       s_gamr[:, R:R + NST], ALU.mult, ALU.mult).then_inc(g("c_rms"), 1)
                v.scalar_tensor_tensor(cnt[:, tc % 4, :], ssm2[:, tc, R + NST:], rmcols[:, 5:6],
                                       s_gamr[:, R + NST:], ALU.mult, ALU.mult).then_inc(g("c_rms"), 1)
                if tc >= 1:
                    bc_dtr_copies(tc - 1)
            bc_dtr_copies(LTC - 1)
            for i in range(2 * NKD):
                kd, hf = divmod(i, 2)
                v.wait_ge(g("c_mm3"), i + 1)
                if i >= 2:
                    wld(v, f"dtsp{i - 2}")
                v.tensor_scalar(spb[:, i % 2, :], ps[:, 0:4, :].rearrange("p b f -> p (b f)"),
                                s_dtbias[:, kd:kd + 1], None, ALU.add).then_inc(g("c_sp0"), 1)
                chv.mark("c_sp0", i + 1)
                v.wait_ge(g("c_spa"), i + 1)
                chv.join(v)
                chv.bump(v.tensor_scalar(spb[:, i % 2, :], spb[:, i % 2, :], 0.0, None, ALU.max))
                chv.join(v)
                v.tensor_tensor(dtst[:, i % 2, :], spb[:, i % 2, :], spa[:, i % 2, :], ALU.add).then_inc(g("c_sp"), 1)
            if debug_phase in (2, 21, 22):
                return
            v.memset(s_hstate[:, :], 0.0)
            for q in range(4):
                for k in range(NKD):
                    i = q * NKD + k
                    wld(v, f"sc{i}")
                    v.tensor_tensor(dtxq[:, :], dtq[:, i % 2, :], xcq[:, i % 2, :], ALU.mult).then_inc(g("c_dtx"), 1)
                    chv.mark("c_dtx", i + 1)
                    chv.join(v)
                    for n in range(NST):
                        idx = i * NST + n
                        kn = k * NST + n
                        v.wait_ge(g("c_a"), idx + 1)
                        wld(v, f"bc{q}_{n}")
                        v.tensor_tensor(bbuf[:, :], dtxq[:, :], brep[:, n, :], ALU.mult).then_inc(g("c_b"), 1)
                        chv.mark("c_b", i * NST + n + 1)
                        if n >= 1:
                            pn = n - 1
                            v.wait_ge(g("c_h"), idx)
                            v.tensor_copy(s_hstate[:, k * NST + pn:k * NST + pn + 1], hbuf[:, T - 1:T])
                            if pn == 0:
                                v.tensor_tensor(yac[:, :], hbuf[:, :], crep[:, 0, :], ALU.mult)
                            else:
                                if pn >= 3:
                                    v.wait_ge(g("c_y"), i * 15 + pn - 2)
                                v.tensor_tensor(hcb[:, pn % 2, :], hbuf[:, :], crep[:, pn, :], ALU.mult).then_inc(g("c_hc"), 1)
                        chv.join(v)
                        v.tensor_tensor_scan(hbuf[:, :], abuf[:, idx % 3, :], bbuf[:, :],
                                             s_hstate[:, kn:kn + 1], ALU.mult, ALU.add).then_inc(g("c_h"), 1)
                        chv.mark("c_h", idx + 1)
                    pn = NST - 1
                    v.wait_ge(g("c_h"), i * NST + NST)
                    v.tensor_copy(s_hstate[:, k * NST + pn:k * NST + pn + 1], hbuf[:, T - 1:T])
                    v.wait_ge(g("c_y"), i * 15 + pn - 2)
                    v.tensor_tensor(hcb[:, pn % 2, :], hbuf[:, :], crep[:, pn, :], ALU.mult).then_inc(g("c_hc"), 1)
                    v.wait_ge(g("c_y"), (i + 1) * 15)
                    if i >= 2:
                        wld(v, f"yexp{i - 2}")
                    v.scalar_tensor_tensor(y2t[:, :], xcq[:, i % 2, :], s_dpar[:, k:k + 1], yac[:, :],
                                           ALU.mult, ALU.add).then_inc(g("c_fin"), 1)
                    j = i - 2
                    if 0 <= j < 64:
                        v.wait_ge(g("c_mm4"), j + 1)
                        wld(v, f"routt{j}")
                        if j >= 2:
                            v.wait_ge(g("c_ev4b"), 4 * (j - 1))
                        pb4 = (j % 2) * 2
                        v.scalar_tensor_tensor(f4st[:, j % 2, :], ps[:, pb4, :], 1.0, routt[:, j % 3, :],
                                               ALU.mult, ALU.add).then_inc(g("c_ev4"), 1)
            for j in (62, 63):
                v.wait_ge(g("c_mm4"), j + 1)
                wld(v, f"routt{j}")
                v.wait_ge(g("c_ev4b"), 4 * (j - 1))
                pb4 = (j % 2) * 2
                v.scalar_tensor_tensor(f4st[:, j % 2, :], ps[:, pb4, :], 1.0, routt[:, j % 3, :],
                                       ALU.mult, ALU.add).then_inc(g("c_ev4"), 1)
            if debug_phase == 3:
                return
            wld(v, "xpg")
            for tc in range(NTC):
                wld(v, f"xrt{tc}")
                v.wait_ge(g("c_sqx"), tc + 1)
                if tc >= 2:
                    wld(v, f"xpo{tc - 2}")
                chv.bump(v.reduce_sum(xpcols[:, 0:1], fft[:, tc % 2, :], axis=AX))
                chv.join(v)
                chv.bump(v.tensor_scalar(xpcols[:, 1:2], xpcols[:, 0:1], 1.0 / H, None, ALU.mult))
                chv.bump(v.reduce_sum(xpcols[:, 2:3], sqxt[:, :], axis=AX))
                chv.join(v)
                chv.bump(v.tensor_scalar(xpcols[:, 3:4], xpcols[:, 2:3], 1.0 / H, None, ALU.mult))
                chv.bump(v.tensor_tensor(xpcols[:, 4:5], xpcols[:, 1:2], xpcols[:, 1:2], ALU.mult))
                chv.join(v)
                chv.bump(v.tensor_tensor(xpcols[:, 5:6], xpcols[:, 3:4], xpcols[:, 4:5], ALU.subtract))
                chv.join(v)
                v.tensor_scalar(xpcols[:, 6:7], xpcols[:, 5:6], 1.0, EPS_LN, ALU.mult, ALU.add).then_inc(g("c_vepx"), 1)
                chv.mark("c_vepx", tc + 1)
                v.wait_ge(g("c_rstx"), tc + 1)
                chv.bump(v.tensor_scalar(xpt[:, tc % 2, :], fft[:, tc % 2, :], xpcols[:, 1:2], xpcols[:, 7:8],
                                         ALU.subtract, ALU.mult))
                chv.join(v)
                chv.bump(v.tensor_tensor(xpt[:, tc % 2, :], xpt[:, tc % 2, :], gbr[:, 0, :], ALU.mult))
                chv.join(v)
                chv.bump(v.tensor_tensor(xpt[:, tc % 2, :], xpt[:, tc % 2, :], gbr[:, 1, :], ALU.add))
                chv.join(v)
                v.tensor_tensor(xpt[:, tc % 2, :], xpt[:, tc % 2, :], xrt[:, tc % 2, :], ALU.add).then_inc(g("c_xp"), 1)
            if debug_phase <= 4:
                return
            for rr in range(4):
                wld(v, f"y{rr}")
            for i in range(64):
                v.wait_ge(g("c_zs"), i + 1)
                v.tensor_tensor(ybuf[:, i, :], ybuf[:, i, :], zst[:, i % 2, :], ALU.mult).then_inc(g("c_gate"), 1)
            for hc in range(32):
                v.wait_ge(g("c_mm5"), 2 * (hc + 1))
                if hc >= 1:
                    v.wait_ge(g("c_tr5b"), 8 * hc)
                pb = (hc % 2) * 2
                v.tensor_copy(a5st[:, :], ps[:, pb:pb + 2, :].rearrange("p b f -> p (b f)")).then_inc(g("c_ev5b"), 1)
                wld(v, f"xpc{hc}")
                if hc >= 2:
                    wld(v, f"out{hc - 2}")
                for tcc in range(8):
                    i = hc * 8 + tcc
                    v.wait_ge(g("c_tr5b"), i + 1)
                    v.tensor_tensor(ost[:, hc % 2, tcc, :], ps[:, 6 + i % 2, 0:128], xpcs[:, hc % 2, tcc, :],
                                    ALU.add).then_inc(g("c_asm"), 1)

    return nc


_CACHE = {}


def kernel(**inputs):
    nc = _CACHE.get("nc99")
    if nc is None:
        nc = build_nc()
        _CACHE["nc99"] = nc
    in_maps = make_in_maps(inputs)
    res = run_bass_kernel_spmd(nc, in_maps, core_ids=list(range(8)))
    out = np.empty((B, L, H), dtype=np.float32)
    for c in range(8):
        b, r = divmod(c, 4)
        out[b, r * T:(r + 1) * T, :] = res.results[c]["out_tok"]
    return out


def make_in_maps(inputs):
    f32 = np.float32
    X = np.ascontiguousarray(np.asarray(inputs["L0_attn_hidden_states"], dtype=f32))
    asf = lambda k: np.ascontiguousarray(np.asarray(inputs[k], dtype=f32))
    def blockw(w, ogw):
        KK, CC = w.shape
        nk, nog = KK // 128, CC // ogw
        return np.ascontiguousarray(w.reshape(nk, 128, nog, ogw).transpose(2, 1, 0, 3).reshape(nog, 128, nk * ogw))
    shared = dict(
        w_in=blockw(asf("L0_attn_in_proj_w"), 512),
        outwT=np.ascontiguousarray(
            asf("L0_attn_out_proj_w").reshape(32, 128, 64, 128).transpose(0, 3, 2, 1)).reshape(32, 128, 8192),
        dtlnw=asf("L0_attn_dt_ln_w").reshape(1, R),
        blnw=asf("L0_attn_B_ln_w").reshape(1, NST),
        clnw=asf("L0_attn_C_ln_w").reshape(1, NST),
        gatew=np.ascontiguousarray(asf("L0_ffn_gate_w").reshape(NKH, 128, NE).transpose(1, 0, 2).reshape(128, NKH * NE)),
        upw=blockw(asf("L0_ffn_up_w"), 512),
        downw=blockw(asf("L0_ffn_down_w"), 128),
        fc1w=blockw(asf("L0_ffn_fc1_w"), 512),
        fc2w=blockw(asf("L0_ffn_fc2_w"), 128),
        expw=blockw(asf("L0_ffn_expert_w"), 512),
        lngamma=asf("L0_ffn_ln_gamma").reshape(1, H),
        lnbeta=asf("L0_ffn_ln_beta").reshape(1, H),
        identf=np.eye(128, dtype=f32),
    )
    cw = asf("L0_attn_conv_w"); cb = asf("L0_attn_conv_b")
    xpw = asf("L0_attn_x_proj_w"); dpw = asf("L0_attn_dt_proj_w")
    dtb = asf("L0_attn_dt_bias"); al = asf("L0_attn_A_log"); dp = asf("L0_attn_D_param")
    in_maps = []
    for c in range(8):
        b, r = divmod(c, 4)
        sl = slice(r * DSL, (r + 1) * DSL)
        m = dict(shared)
        m["x_tok"] = np.ascontiguousarray(X[b, r * T:(r + 1) * T, :])
        m["convw"] = np.ascontiguousarray(cw[sl])
        m["convb"] = np.ascontiguousarray(cb[sl].reshape(NKD, 128).T)
        m["xprojw"] = np.ascontiguousarray(xpw[:, sl])
        m["dtprojw"] = np.ascontiguousarray(dpw[sl])
        m["dtbias"] = np.ascontiguousarray(dtb[sl].reshape(NKD, 128).T)
        m["alog"] = np.ascontiguousarray(al[sl])
        m["dpar"] = np.ascontiguousarray(dp[sl].reshape(NKD, 128).T)
        in_maps.append(m)
    return in_maps
